# version 12
# baseline (speedup 1.0000x reference)
import numpy as np
import ml_dtypes
from contextlib import ExitStack

import concourse.bass as bass
import concourse.mybir as mybir
from concourse.bass import IndirectOffsetOnAxis
from concourse.bass_utils import run_bass_kernel_spmd

F32 = mybir.dt.float32
BF16 = mybir.dt.bfloat16
I32 = mybir.dt.int32
AF = mybir.ActivationFunctionType
OP = mybir.AluOpType
AX = mybir.AxisListType.X
BF = ml_dtypes.bfloat16

NCORES = 8
BB, NN, D = 2, 8192, 1024
TOK = 2048
J = 2048
F4 = 4096
NT = TOK // 128
ND = D // 128
NJ = J // 128
EPS = 1e-5
INVN = 1.0 / NN
CAP = 4916
CAPP = 4992
TBL = 36864

ENGS = ("vector", "scalar", "tensor", "gpsimd", "sync")


class K:

    def __init__(self, nc, ctx):
        self.nc, self.ctx = nc, ctx
        self.esem, self.ecnt = {}, {}
        for e in ENGS:
            self.esem[e] = ctx.enter_context(nc.semaphore("es_" + e))
            self.ecnt[e] = 0
        self.hw = {e: {} for e in ENGS}
        self.pend = {e: [] for e in ENGS}
        self._slot_n = 0

    def slot(self, name=None):
        self._slot_n += 1
        h = self.ctx.enter_context(
            self.nc.semaphore(name or f"sl{self._slot_n}"))
        return [h, 0]

    def op(self, eng, fn, after=()):
        self.ecnt[eng] += 1
        v = self.ecnt[eng]
        self.pend[eng].append((fn, tuple(after), self.esem[eng], 1, v))
        return (self.esem[eng], v)

    def dma(self, eng, fn, slot, after=()):
        slot[1] += 16
        v = slot[1]
        self.pend[eng].append((fn, tuple(after), slot[0], 16, v))
        return (slot[0], v)

    def bar(self, extra=()):
        toks = [(self.esem[e], self.ecnt[e]) for e in ENGS if self.ecnt[e] > 0]
        toks += [t for t in extra if t[1] > 0]
        return toks

    def flush(self, block):
        for eng in ENGS:
            items = self.pend[eng]
            if not items:
                continue
            hw = self.hw[eng]

            def body(E, items=tuple(items), hw=hw):
                for fn, after, sem, incby, v in items:
                    for (sh, sv) in after:
                        key = id(sh)
                        if hw.get(key, -1) >= sv:
                            continue
                        E.wait_ge(sh, sv)
                        hw[key] = sv
                    fn(E).then_inc(sem, incby)
            getattr(block, eng)(body)
            self.pend[eng] = []

    def final_wait(self, block, eng, tokens):
        def body(E, tokens=tuple(tokens)):
            for (sh, sv) in tokens:
                E.wait_ge(sh, sv)
        getattr(block, eng)(body)


def _bass(scratch=16384):
    return bass.Bass("TRN2", target_bir_lowering=False,
                     dynamic_dma_scratch_size=scratch)


def build_l1():
    nc = _bass()
    x_i = nc.dram_tensor("x_sl", [TOK, D], F32, kind="ExternalInput").ap()
    lng_i = nc.dram_tensor("lng_r", [128, D], F32, kind="ExternalInput").ap()
    lnb_i = nc.dram_tensor("lnb_r", [128, D], F32, kind="ExternalInput").ap()
    wqk_i = nc.dram_tensor("Wqk", [D, J], F32, kind="ExternalInput").ap()
    bqk_i = nc.dram_tensor("bqk_r", [128, J], F32, kind="ExternalInput").ap()
    osg_i = nc.dram_tensor("osg_r", [256, J], F32, kind="ExternalInput").ap()
    osb_i = nc.dram_tensor("osb_r", [256, J], F32, kind="ExternalInput").ap()
    whid_i = nc.dram_tensor("Whid", [D, F4], F32, kind="ExternalInput").ap()
    bhid_i = nc.dram_tensor("bhid_c", [128, F4 // 128], F32,
                            kind="ExternalInput").ap()
    ident_i = nc.dram_tensor("ident", [128, 128], F32, kind="ExternalInput").ap()

    xacc_o = nc.dram_tensor("xacc_o", [TOK, D], F32, kind="ExternalOutput").ap()
    simp_o = nc.dram_tensor("simp_o", [J, J], BF16, kind="ExternalOutput").ap()
    vg_o = nc.dram_tensor("vg_o", [J, TOK], BF16, kind="ExternalOutput").ap()

    with ExitStack() as ctx:
        E = ctx.enter_context
        k = K(nc, ctx)
        block = E(nc.Block())
        n1fm = E(nc.sbuf_tensor("n1fm", [128, ND * TOK], BF16))
        identb = E(nc.sbuf_tensor("identb", [128, 128], BF16))
        cols = E(nc.sbuf_tensor("cols", [128, 32], F32))
        cols2 = E(nc.sbuf_tensor("cols2", [128, 16], F32))
        cols_pad = E(nc.sbuf_tensor("colspad", [128, 16], F32))
        psA = E(nc.psum_tensor("psA", [128, 512], F32))
        psB = E(nc.psum_tensor("psB", [128, 512], F32))
        pstA = E(nc.psum_tensor("pstA", [128, 128], BF16))
        pstB = E(nc.psum_tensor("pstB", [128, 128], BF16))
        store_toks = []

        with ExitStack() as sA:
            EA = sA.enter_context
            n1tm = EA(nc.sbuf_tensor("n1tm", [128, NT * D], BF16))
            lng = EA(nc.sbuf_tensor("lng", [128, D], F32))
            lnb = EA(nc.sbuf_tensor("lnb", [128, D], F32))
            xbuf = EA(nc.sbuf_tensor("xbuf", [128, 4 * D], F32))
            xcm = EA(nc.sbuf_tensor("xcm", [128, 4 * D], F32))
            ln1 = EA(nc.sbuf_tensor("ln1", [128, 4 * D], F32))
            sq = EA(nc.sbuf_tensor("sq", [128, D], F32))
            xas = EA(nc.sbuf_tensor("xas", [128, 4 * D], F32))

            slx = [k.slot(), k.slot()]
            slxa = [k.slot(), k.slot()]
            slc = k.slot()

            k.dma("gpsimd", lambda g: g.dma_start(out=lng[:, :], in_=lng_i), slc)
            k.dma("gpsimd", lambda g: g.dma_start(out=lnb[:, :], in_=lnb_i), slc)
            tconsts = k.dma("gpsimd",
                            lambda g: g.dma_start(out=identb[:, :], in_=ident_i), slc)

            t_x9 = [None] * NT
            t_n1 = [None] * NT
            xacc_sts = []
            slx = [k.slot() for _ in range(4)]
            slxa = [k.slot() for _ in range(4)]
            for i in range(NT):
                sl = slice((i % 4) * D, (i % 4 + 1) * D)
                c0 = (i % 4) * 8
                dep_slot = [t_x9[i - 4]] if i >= 4 else []
                t_ld = k.dma("sync", lambda s, i=i, sl=sl: s.dma_start(
                    out=xbuf[:, sl], in_=x_i[i * 128:(i + 1) * 128, :]),
                    slx[i % 4], after=dep_slot)

                a = [t_ld] + ([tconsts] if i == 0 else [])
                t1 = k.op("vector", lambda v, sl=sl, c0=c0: v.reduce_sum(
                    cols[:, c0:c0 + 1], xbuf[:, sl], axis=AX), a)
                t2 = k.op("vector", lambda v, c0=c0: v.tensor_scalar(
                    cols[:, c0 + 1:c0 + 2], cols[:, c0:c0 + 1], 1.0 / D, None,
                    OP.mult), [t1])
                t3 = k.op("vector", lambda v, sl=sl, c0=c0: v.tensor_scalar(
                    xcm[:, sl], xbuf[:, sl], cols[:, c0 + 1:c0 + 2], None,
                    OP.subtract), [t2])
                t4 = k.op("scalar", lambda s, sl=sl, c0=c0: s.activation(
                    sq[:, :], xcm[:, sl], AF.Square,
                    accum_out=cols[:, c0 + 2:c0 + 3]), [t3])
                t5 = k.op("vector", lambda v, c0=c0: v.tensor_scalar(
                    cols[:, c0 + 3:c0 + 4], cols[:, c0 + 2:c0 + 3], 1.0 / D, EPS,
                    OP.mult, OP.add), [t4])
                t6 = k.op("scalar", lambda s, c0=c0: s.sqrt(
                    cols[:, c0 + 4:c0 + 5], cols[:, c0 + 3:c0 + 4]), [t5])
                t7 = k.op("vector", lambda v, c0=c0: v.reciprocal(
                    cols[:, c0 + 5:c0 + 6], cols[:, c0 + 4:c0 + 5]), [t6])
                t8 = k.op("vector", lambda v, c0=c0: v.tensor_scalar(
                    cols[:, c0 + 6:c0 + 7], cols[:, c0 + 3:c0 + 4], 1.0, -EPS,
                    OP.mult, OP.add), [t7])
                t9 = k.op("vector", lambda v, c0=c0: v.tensor_mul(
                    cols[:, c0 + 6:c0 + 7], cols[:, c0 + 6:c0 + 7],
                    cols[:, c0 + 5:c0 + 6]), [t8])
                t11 = k.op("vector", lambda v, c0=c0: v.scalar_tensor_tensor(
                    cols[:, c0 + 6:c0 + 7], cols[:, c0 + 6:c0 + 7], 1.0,
                    cols[:, c0 + 5:c0 + 6], OP.mult, OP.mult), [t9])
                t12 = k.op("vector", lambda v, c0=c0: v.tensor_scalar(
                    cols[:, c0 + 6:c0 + 7], cols[:, c0 + 6:c0 + 7], 1.0, EPS,
                    OP.mult, OP.add), [t11])
                t13 = k.op("scalar", lambda s, c0=c0: s.sqrt(
                    cols[:, c0 + 7:c0 + 8], cols[:, c0 + 6:c0 + 7]), [t12])
                c2 = (i % 4) * 4
                t14 = k.op("vector", lambda v, c2=c2, c0=c0: v.reciprocal(
                    cols2[:, c2:c2 + 1], cols[:, c0 + 7:c0 + 8]), [t13])
                t15 = k.op("vector", lambda v, c2=c2, c0=c0: v.tensor_mul(
                    cols2[:, c2 + 1:c2 + 2], cols2[:, c2:c2 + 1],
                    cols[:, c0 + 5:c0 + 6]), [t14])
                dep_store = [xacc_sts[i - 4]] if i >= 4 else []
                tA = k.op("vector", lambda v, sl=sl, c0=c0: v.scalar_tensor_tensor(
                    xas[:, sl], xcm[:, sl], cols[:, c0 + 5:c0 + 6], xbuf[:, sl],
                    OP.mult, OP.add), [t7] + dep_store)
                t_x9[i] = tA
                t_st = k.dma("scalar", lambda s, i=i, sl=sl: s.dma_start(
                    out=xacc_o[i * 128:(i + 1) * 128, :], in_=xas[:, sl]),
                    slxa[i % 4], after=[tA])
                xacc_sts.append(t_st)
                store_toks.append(t_st)
                tB = k.op("vector", lambda v, sl=sl, c2=c2: v.tensor_scalar(
                    xcm[:, sl], xcm[:, sl], cols2[:, c2 + 1:c2 + 2], None,
                    OP.mult), [t15, tA, t4])
                tC = k.op("vector", lambda v, sl=sl: v.tensor_mul(
                    ln1[:, sl], xcm[:, sl], lng[:, :]), [tB])
                tD = k.op("vector", lambda v, i=i, sl=sl: v.tensor_add(
                    n1tm[:, i * D:(i + 1) * D], ln1[:, sl], lnb[:, :]), [tC])
                t_n1[i] = tD
            k.flush(block)

            t_trcopy = [None] * (NT * ND)
            for i in range(NT):
                for j in range(ND):
                    b = i * ND + j
                    pst = pstA if b % 2 == 0 else pstB
                    dep = [t_n1[i]]
                    if b >= 2:
                        dep.append(t_trcopy[b - 2])
                    if b == 0:
                        dep.append(tconsts)
                    t_tr = k.op("tensor", lambda t, i=i, j=j, pst=pst: t.transpose(
                        pst[:, :], n1tm[:, i * D + j * 128:i * D + (j + 1) * 128],
                        identb[:, :]), dep)
                    t_trcopy[b] = k.op("vector", lambda v, i=i, j=j, pst=pst:
                                       v.tensor_copy(
                                           n1fm[:, j * TOK + i * 128:j * TOK + (i + 1) * 128],
                                           pst[:, :]), [t_tr])
            k.flush(block)
            bar_AB = k.bar([tuple(s) for s in (slx[0], slx[1], slxa[0],
                                               slxa[1], slc)])

        with ExitStack() as sC:
            EC = sC.enter_context
            q_tm = EC(nc.sbuf_tensor("q_tm", [128, NT * J], BF16))
            k_tm = EC(nc.sbuf_tensor("k_tm", [128, NT * J], BF16))
            with ExitStack() as sCc:
                ECc = sCc.enter_context
                wqkb = ECc(nc.sbuf_tensor("wqkb", [128, 2 * 8 * 512], BF16))
                bqk = ECc(nc.sbuf_tensor("bqk", [1, J], BF16))
                ones1 = ECc(nc.sbuf_tensor("ones1", [1, 128], BF16))
                g0b = ECc(nc.sbuf_tensor("g0b", [128, J], BF16))
                g1b = ECc(nc.sbuf_tensor("g1b", [128, J], BF16))
                b0b = ECc(nc.sbuf_tensor("b0b", [128, J], BF16))
                b1b = ECc(nc.sbuf_tensor("b1b", [128, J], BF16))
                qsf = ECc(nc.sbuf_tensor("qsf", [128, 512], F32))
                qsb = ECc(nc.sbuf_tensor("qsb", [128, 512], BF16))
                qx = ECc(nc.sbuf_tensor("qx", [128, 512], BF16))

                slcc = k.slot()
                k.dma("gpsimd", lambda g: g.dma_start(out=bqk[:, :],
                                                      in_=bqk_i[0:1, :]), slcc,
                      after=bar_AB)
                t_o1 = k.op("vector", lambda v: v.memset(ones1[:, :], 1.0))
                k.dma("gpsimd", lambda g: g.dma_start(out=g0b[:, :],
                                                      in_=osg_i[0:128, :]), slcc)
                k.dma("gpsimd", lambda g: g.dma_start(out=g1b[:, :],
                                                      in_=osg_i[128:256, :]), slcc)
                k.dma("gpsimd", lambda g: g.dma_start(out=b0b[:, :],
                                                      in_=osb_i[0:128, :]), slcc)
                tcc = k.dma("gpsimd", lambda g: g.dma_start(out=b1b[:, :],
                                                            in_=osb_i[128:256, :]), slcc)

                slw = [k.slot(), k.slot()]
                t_wl = [None] * 4
                t_mm = {}
                t_ev1 = {}
                t_ev2 = {}
                for p in range(4):
                    js = p * 512
                    wbase = (p % 2) * 4096
                    dep_w = list(bar_AB) + [t_o1] if p < 2 else [t_mm[(p - 2, NT - 1)]]
                    t_wl[p] = k.dma("gpsimd", lambda g, js=js, wbase=wbase:
                                    g.dma_start(
                                        out=wqkb[:, wbase:wbase + 4096].rearrange(
                                            "p (k j) -> p k j", k=8),
                                        in_=wqk_i[:, js:js + 512].rearrange(
                                            "(k p) j -> p k j", p=128)),
                                    slw[p % 2], after=dep_w)
                    for i in range(NT):
                        b = p * NT + i
                        ps = psA if b % 2 == 0 else psB
                        dep = [t_wl[p], t_trcopy[i * ND + ND - 1]]
                        if b >= 2:
                            dep.append(t_ev1[(b - 2) // NT, (b - 2) % NT])

                        def mm(t, i=i, js=js, wbase=wbase, ps=ps):
                            for kk in range(8):
                                t.matmul(
                                    ps[:, :],
                                    n1fm[:, kk * TOK + i * 128:kk * TOK + (i + 1) * 128],
                                    wqkb[:, wbase + kk * 512:wbase + (kk + 1) * 512],
                                    start=(kk == 0), stop=False)
                            return t.matmul(ps[:, :], ones1[:, :],
                                            bqk[:, js:js + 512],
                                            start=False, stop=True)
                        t_mm[(p, i)] = k.op("tensor", mm, dep)
                        dep1 = [t_mm[(p, i)], tcc]
                        if b == 0:
                            dep1 += bar_AB
                        if b >= 1:
                            dep1.append(t_ev2[(b - 1) // NT, (b - 1) % NT])
                        t2 = k.op("scalar", lambda s, ps=ps: s.activation(
                            qsb[:, :], ps[:, :], AF.Silu), dep1)
                        t_ev1[(p, i)] = t2
                        t_ev2[(p, i)] = t2
                        t3 = k.op("vector", lambda v, js=js: v.tensor_mul(
                            qx[:, :], qsb[:, :], g0b[:, js:js + 512]), [t2])
                        t4 = k.op("vector", lambda v, i=i, js=js: v.tensor_add(
                            q_tm[:, i * J + js:i * J + js + 512], qx[:, :],
                            b0b[:, js:js + 512]), [t3])
                        t5 = k.op("vector", lambda v, js=js: v.tensor_mul(
                            qx[:, :], qsb[:, :], g1b[:, js:js + 512]), [t4])
                        k.op("vector", lambda v, i=i, js=js: v.tensor_add(
                            k_tm[:, i * J + js:i * J + js + 512], qx[:, :],
                            b1b[:, js:js + 512]), [t5])
                k.flush(block)
                bar_C = k.bar()

            simstage = EC(nc.sbuf_tensor("simstage", [128, 2 * J], BF16))
            slst = [k.slot(), k.slot()]
            t_ev = {}
            for m in range(NJ):
                sbase = (m % 2) * J
                for p in range(4):
                    b = m * 4 + p
                    ps = psA if b % 2 == 0 else psB
                    dep = list(bar_C) if b < 2 else [t_ev[b - 2]]
                    if p == 0 and m >= 2:
                        dep.append((slst[m % 2][0], slst[m % 2][1]))

                    def mm(t, m=m, p=p, ps=ps):
                        for i in range(NT):
                            ins = t.matmul(
                                ps[:, :],
                                q_tm[:, i * J + m * 128:i * J + (m + 1) * 128],
                                k_tm[:, i * J + p * 512:i * J + (p + 1) * 512],
                                start=(i == 0), stop=(i == NT - 1))
                        return ins
                    t_m = k.op("tensor", mm, dep)
                    t_ev[b] = k.op("vector", lambda v, sbase=sbase, p=p, ps=ps:
                                   v.tensor_scalar(
                                       simstage[:, sbase + p * 512:sbase + (p + 1) * 512],
                                       ps[:, :], INVN, None, OP.mult), [t_m])
                t_st = k.dma("sync", lambda s, m=m, sbase=sbase: s.dma_start(
                    out=simp_o[m * 128:(m + 1) * 128, :],
                    in_=simstage[:, sbase:sbase + J]), slst[m % 2],
                    after=[t_ev[m * 4 + 3]])
                store_toks.append(t_st)
            k.flush(block)
            bar_D = k.bar()

        with ExitStack() as sE:
            EE = sE.enter_context
            whb = EE(nc.sbuf_tensor("whb", [128, 4 * 8 * 128], BF16))
            bhid = EE(nc.sbuf_tensor("bhid", [128, F4 // 128], F32))
            hv = EE(nc.sbuf_tensor("hv", [128, J], BF16))
            hg = EE(nc.sbuf_tensor("hg", [128, J], BF16))
            vgs = EE(nc.sbuf_tensor("vgs", [128, 2 * J], BF16))

            slb = k.slot()
            t_bh = k.dma("gpsimd", lambda g: g.dma_start(out=bhid[:, :], in_=bhid_i),
                         slb, after=bar_D)
            slwh = [k.slot() for _ in range(4)]
            slvg = [k.slot(), k.slot()]
            t_hev_flat = []
            t_mm_flat = []
            t_vg = [None] * NJ
            order = 0
            for mi in range(NJ):
                for half, m in enumerate((mi, mi + 16)):
                    tgt = hv if half == 0 else hg
                    wbase = (order % 4) * 1024
                    dep_w = list(bar_D) if order < 4 else [t_mm_flat[(order - 4) * 4 + 3]]
                    t_w = k.dma("gpsimd", lambda g, m=m, wbase=wbase: g.dma_start(
                        out=whb[:, wbase:wbase + 1024].rearrange(
                            "p (k f) -> p k f", k=8),
                        in_=whid_i[:, m * 128:(m + 1) * 128].rearrange(
                            "(k p) f -> p k f", p=128)),
                        slwh[order % 4], after=dep_w)
                    for p in range(4):
                        b = order * 4 + p
                        ps = psA if b % 2 == 0 else psB
                        dep = [t_w, t_bh] if b < 2 else [t_w, t_hev_flat[b - 2]]
                        if b == 0:
                            dep += bar_D

                        def mmh(t, p=p, ps=ps, wbase=wbase):
                            for kk in range(8):
                                ins = t.matmul(
                                    ps[:, :],
                                    whb[:, wbase + kk * 128:wbase + (kk + 1) * 128],
                                    n1fm[:, kk * TOK + p * 512:kk * TOK + (p + 1) * 512],
                                    start=(kk == 0), stop=(kk == 7))
                            return ins
                        t_m = k.op("tensor", mmh, dep)
                        t_mm_flat.append(t_m)
                        dep_e = [t_m]
                        if p == 0 and half == 0 and mi >= 1:
                            dep_e.append(t_vg[mi - 1])
                        t_e = k.op("scalar", lambda s, m=m, p=p, ps=ps, tgt=tgt:
                                   s.activation(tgt[:, p * 512:(p + 1) * 512],
                                                ps[:, :], AF.Silu,
                                                bias=bhid[:, m:m + 1]), dep_e)
                        t_hev_flat.append(t_e)
                    order += 1
                vbase = (mi % 2) * J
                dep_v = [t_hev_flat[-1], t_hev_flat[-5]]
                if mi >= 2:
                    dep_v.append((slvg[mi % 2][0], slvg[mi % 2][1]))
                t_vg[mi] = k.op("vector", lambda v, vbase=vbase: v.tensor_mul(
                    vgs[:, vbase:vbase + J], hv[:, :], hg[:, :]), dep_v)
                t_st = k.dma("sync", lambda s, mi=mi, vbase=vbase: s.dma_start(
                    out=vg_o[mi * 128:(mi + 1) * 128, :],
                    in_=vgs[:, vbase:vbase + J]), slvg[mi % 2], after=[t_vg[mi]])
                store_toks.append(t_st)
            k.flush(block)
            k.final_wait(block, "sync", store_toks)

    return nc


def l1_inputs(x_slice, inp):
    rep = lambda a: np.ascontiguousarray(
        np.broadcast_to(np.asarray(a, np.float32)[None, :], (128, a.shape[-1])))
    osg = np.asarray(inp["L0_attn_os_gamma"], np.float32)
    osb = np.asarray(inp["L0_attn_os_beta"], np.float32)
    osg_r = np.concatenate([np.broadcast_to(osg[0][None, :], (128, J)),
                            np.broadcast_to(osg[1][None, :], (128, J))], 0)
    osb_r = np.concatenate([np.broadcast_to(osb[0][None, :], (128, J)),
                            np.broadcast_to(osb[1][None, :], (128, J))], 0)
    return {
        "x_sl": np.ascontiguousarray(x_slice, dtype=np.float32),
        "lng_r": rep(inp["L0_attn_ln_g"]),
        "lnb_r": rep(inp["L0_attn_ln_b"]),
        "Wqk": np.ascontiguousarray(inp["L0_attn_W_qk"], np.float32),
        "bqk_r": rep(inp["L0_attn_b_qk"]),
        "osg_r": np.ascontiguousarray(osg_r, np.float32),
        "osb_r": np.ascontiguousarray(osb_r, np.float32),
        "Whid": np.ascontiguousarray(inp["L0_attn_W_hidden"], np.float32),
        "bhid_c": np.ascontiguousarray(
            np.asarray(inp["L0_attn_b_hidden"], np.float32).reshape(F4 // 128, 128).T),
        "ident": np.eye(128, dtype=np.float32),
    }


def build_l2():
    nc = _bass()
    simp4_i = nc.dram_tensor("simp4", [4 * J, J], BF16, kind="ExternalInput").ap()
    vg_i = nc.dram_tensor("vg_i", [J, TOK], BF16, kind="ExternalInput").ap()
    xacc_i = nc.dram_tensor("xacc_i", [TOK, D], F32, kind="ExternalInput").ap()
    wout_i = nc.dram_tensor("Wout", [J, D], F32, kind="ExternalInput").ap()
    bout_i = nc.dram_tensor("bout_r", [1, D], F32, kind="ExternalInput").ap()
    wg_i = nc.dram_tensor("Wg", [D, 8], F32, kind="ExternalInput").ap()
    bg_i = nc.dram_tensor("bg_r", [128, 8], F32, kind="ExternalInput").ap()
    ident_i = nc.dram_tensor("ident", [128, 128], F32, kind="ExternalInput").ap()
    tokb_i = nc.dram_tensor("tokbase_r", [128, 1], F32, kind="ExternalInput").ap()
    w1e_i = nc.dram_tensor("W1e", [D, F4], F32, kind="ExternalInput").ap()
    w2e_i = nc.dram_tensor("W2e", [F4, D], F32, kind="ExternalInput").ap()

    xmid_o = nc.dram_tensor("xmid_o", [TOK, D], F32, kind="ExternalOutput").ap()
    n2_o = nc.dram_tensor("n2_o", [TOK, D], BF16, kind="ExternalOutput").ap()
    route_o = nc.dram_tensor("route_o", [128, 96], F32, kind="ExternalOutput").ap()
    counts_o = nc.dram_tensor("counts_o", [8, 2], F32, kind="ExternalOutput").ap()
    w1c_o = nc.dram_tensor("w1c_o", [D, F4], BF16, kind="ExternalOutput").ap()
    w2c_o = nc.dram_tensor("w2c_o", [F4, D], BF16, kind="ExternalOutput").ap()
    T_o = nc.dram_tensor("T_o", [16, TOK], F32, kind="ExternalOutput").ap()
    T_flat = T_o.rearrange("a b -> (a b)")[:, None]

    with ExitStack() as ctx:
        E = ctx.enter_context
        k = K(nc, ctx)
        block = E(nc.Block())
        identb = E(nc.sbuf_tensor("identb", [128, 128], BF16))
        identf = E(nc.sbuf_tensor("identf", [128, 128], F32))
        scols = E(nc.sbuf_tensor("scols", [128, 16], F32))
        cols = E(nc.sbuf_tensor("cols", [128, 16], F32))
        n2tm = E(nc.sbuf_tensor("n2tm", [128, NT * D], BF16))
        psA = E(nc.psum_tensor("psA", [128, 512], F32))
        psB = E(nc.psum_tensor("psB", [128, 512], F32))
        store_toks = []

        tid = k.dma("gpsimd", lambda g: g.dma_start(out=identb[:, :], in_=ident_i),
                    k.slot())
        tidf = k.dma("sync", lambda s: s.dma_start(out=identf[:, :], in_=ident_i),
                     k.slot())

        with ExitStack() as s1:
            E1 = s1.enter_context
            sbuf4 = [E1(nc.sbuf_tensor(f"pp{g}", [128, 8 * J], BF16)) for g in range(4)]
            acc = E1(nc.sbuf_tensor("acc", [128, 2 * J], BF16))
            exps = E1(nc.sbuf_tensor("exps", [128, 2 * J], F32))
            sll = [k.slot() for _ in range(8)]
            t_lds = {}
            t_sdone = [None] * NJ
            t_exp = [None] * NJ
            for m in range(NJ):
                q4 = (m % 8) * J
                h2 = (m % 2) * J
                for g in range(4):
                    dep = [t_sdone[m - 8]] if m >= 8 else []
                    eng = "sync" if g % 2 == 0 else "gpsimd"
                    t_lds[(m, g)] = k.dma(eng, lambda s, m=m, g=g, q4=q4:
                        s.dma_start(out=sbuf4[g][:, q4:q4 + J],
                                    in_=simp4_i[g * J + m * 128:g * J + (m + 1) * 128, :]),
                        sll[g * 2 + m % 2], after=dep)
                dep_a = [t_lds[(m, 0)], t_lds[(m, 1)]]
                if m >= 2:
                    dep_a.append(t_exp[m - 2])
                ta = k.op("vector", lambda v, q4=q4, h2=h2: v.tensor_add(
                    acc[:, h2:h2 + J], sbuf4[0][:, q4:q4 + J],
                    sbuf4[1][:, q4:q4 + J]), dep_a)
                tb = k.op("vector", lambda v, q4=q4, h2=h2: v.tensor_add(
                    acc[:, h2:h2 + J], acc[:, h2:h2 + J], sbuf4[2][:, q4:q4 + J]),
                    [ta, t_lds[(m, 2)]])
                tc = k.op("vector", lambda v, q4=q4, h2=h2: v.tensor_add(
                    acc[:, h2:h2 + J], acc[:, h2:h2 + J], sbuf4[3][:, q4:q4 + J]),
                    [tb, t_lds[(m, 3)]])
                c0 = (m % 2) * 4
                dep_e = [tc]
                if m >= 2:
                    dep_e.append(t_sdone[m - 2])
                tex = k.op("scalar", lambda s, h2=h2, c0=c0: s.activation(
                    exps[:, h2:h2 + J], acc[:, h2:h2 + J], AF.Exp,
                    accum_out=cols[:, c0 + 1:c0 + 2]), dep_e)
                t_exp[m] = tex
                trc = k.op("vector", lambda v, c0=c0: v.reciprocal(
                    cols[:, c0 + 2:c0 + 3], cols[:, c0 + 1:c0 + 2]), [tex])
                t_sdone[m] = k.op("scalar", lambda s, m=m, h2=h2, c0=c0:
                    s.activation(
                        exps[:, h2:h2 + J], exps[:, h2:h2 + J], AF.Copy,
                        scale=cols[:, c0 + 2:c0 + 3],
                        accum_out=scols[:, m:m + 1]), [trc])
            k.flush(block)
            bar1 = k.bar([tuple(s) for s in sll])

        t_n2 = [None] * NT
        with ExitStack() as s2:
            E2 = s2.enter_context
            vgs = E2(nc.sbuf_tensor("vgs", [128, NJ * TOK], BF16))
            woutb = [E2(nc.sbuf_tensor(f"woutb{h}", [128, NJ * 512], BF16))
                     for h in range(2)]
            xaccb = E2(nc.sbuf_tensor("xaccb", [128, 2 * D], F32))
            xmid = E2(nc.sbuf_tensor("xmid", [128, 2 * D], F32))
            sqq = E2(nc.sbuf_tensor("sqq", [128, 2 * D], F32))
            sq2 = E2(nc.sbuf_tensor("sq2", [128, D], F32))

            slvg = [k.slot(), k.slot()]
            slwo = [k.slot(), k.slot()]
            slxa = [k.slot(), k.slot()]
            slxm = [k.slot(), k.slot()]
            sln2 = [k.slot(), k.slot()]
            t_wo = [None, None]
            for h in range(2):
                t_wo[h] = k.dma("gpsimd", lambda g, h=h: g.dma_start(
                    out=woutb[h][:, :].rearrange("p (k d) -> p k d", k=NJ),
                    in_=wout_i[:, h * 512:(h + 1) * 512].rearrange(
                        "(k p) d -> p k d", p=128)), slwo[h],
                    after=(bar1 if h == 0 else []))
            boutb = E2(nc.sbuf_tensor("boutb", [1, D], BF16))
            ones1b = E2(nc.sbuf_tensor("ones1b", [1, 128], BF16))
            t_bo = k.dma("gpsimd", lambda g: g.dma_start(out=boutb[:, :],
                                                         in_=bout_i), k.slot(),
                         after=bar1)
            t_o1b = k.op("vector", lambda v: v.memset(ones1b[:, :], 1.0))
            t_vgs = [None] * NJ
            for m in range(NJ):
                dep = list(bar1) if m < 2 else [t_vgs[m - 2]]
                t_l = k.dma("sync", lambda s, m=m: s.dma_start(
                    out=vgs[:, m * TOK:(m + 1) * TOK],
                    in_=vg_i[m * 128:(m + 1) * 128, :]), slvg[m % 2], after=dep)
                t_vgs[m] = k.op("vector", lambda v, m=m: v.tensor_scalar(
                    vgs[:, m * TOK:(m + 1) * TOK], vgs[:, m * TOK:(m + 1) * TOK],
                    scols[:, m:m + 1], None, OP.mult), [t_l])
            t_evx = [None] * (NT * 2)
            for t in range(NT):
                half = (t % 2) * D
                t_xl = k.dma("gpsimd", lambda s, t=t, half=half: s.dma_start(
                    out=xaccb[:, half:half + D],
                    in_=xacc_i[t * 128:(t + 1) * 128, :]), slxa[t % 2],
                    after=([t_n2[t - 2]] if t >= 2 else []))
                for dp in range(2):
                    b = t * 2 + dp
                    ps = psA if b % 2 == 0 else psB
                    dep = [t_wo[dp], t_vgs[7], t_bo, t_o1b]
                    if b >= 2:
                        dep.append(t_evx[b - 2])
                    if t >= 2 and dp == 0:
                        dep.append((slxm[t % 2][0], slxm[t % 2][1]))

                    def mmA(tt, t=t, ps=ps, dp=dp):
                        for kk in range(8):
                            ins = tt.matmul(
                                ps[:, :],
                                vgs[:, kk * TOK + t * 128:kk * TOK + (t + 1) * 128],
                                woutb[dp][:, kk * 512:(kk + 1) * 512],
                                start=(kk == 0), stop=False)
                        return ins

                    def mmB(tt, t=t, ps=ps, dp=dp):
                        for kk in range(8, NJ):
                            tt.matmul(
                                ps[:, :],
                                vgs[:, kk * TOK + t * 128:kk * TOK + (t + 1) * 128],
                                woutb[dp][:, kk * 512:(kk + 1) * 512],
                                start=False, stop=False)
                        return tt.matmul(
                            ps[:, :], ones1b[:, :],
                            boutb[0:1, dp * 512:(dp + 1) * 512],
                            start=False, stop=True)
                    k.op("tensor", mmA, dep)
                    t_m = k.op("tensor", mmB, [t_vgs[NJ - 1]])
                    t_evx[b] = k.op("vector", lambda v, half=half, dp=dp, ps=ps:
                        v.tensor_add(xmid[:, half + dp * 512:half + (dp + 1) * 512],
                                     ps[:, :],
                                     xaccb[:, half + dp * 512:half + (dp + 1) * 512]),
                        [t_m, t_xl])
                t_stx = k.dma("scalar", lambda s, t=t, half=half: s.dma_start(
                    out=xmid_o[t * 128:(t + 1) * 128, :], in_=xmid[:, half:half + D]),
                    slxm[t % 2], after=[t_evx[t * 2], t_evx[t * 2 + 1]])
                store_toks.append(t_stx)
                c4 = 4 + (t % 2) * 6
                t1 = k.op("vector", lambda v, half=half, c4=c4: v.reduce_sum(
                    cols[:, c4:c4 + 1], xmid[:, half:half + D], axis=AX),
                    [t_evx[t * 2], t_evx[t * 2 + 1]])
                t2 = k.op("vector", lambda v, c4=c4: v.tensor_scalar(
                    cols[:, c4 + 1:c4 + 2], cols[:, c4:c4 + 1], 1.0 / D, None,
                    OP.mult), [t1])
                t3 = k.op("vector", lambda v, half=half, c4=c4: v.tensor_scalar(
                    sqq[:, half:half + D], xmid[:, half:half + D],
                    cols[:, c4 + 1:c4 + 2], None, OP.subtract), [t2])
                t4 = k.op("scalar", lambda s, half=half, c4=c4: s.activation(
                    sq2[:, :], sqq[:, half:half + D], AF.Square,
                    accum_out=cols[:, c4 + 2:c4 + 3]), [t3])
                t5 = k.op("vector", lambda v, c4=c4: v.tensor_scalar(
                    cols[:, c4 + 3:c4 + 4], cols[:, c4 + 2:c4 + 3], 1.0 / D, EPS,
                    OP.mult, OP.add), [t4])
                t6 = k.op("scalar", lambda s, c4=c4: s.sqrt(
                    cols[:, c4 + 4:c4 + 5], cols[:, c4 + 3:c4 + 4]), [t5])
                t7 = k.op("vector", lambda v, c4=c4: v.reciprocal(
                    cols[:, c4 + 5:c4 + 6], cols[:, c4 + 4:c4 + 5]), [t6])
                t9 = k.op("vector", lambda v, t=t, half=half, c4=c4: v.tensor_scalar(
                    n2tm[:, t * D:(t + 1) * D], sqq[:, half:half + D],
                    cols[:, c4 + 5:c4 + 6], None, OP.mult), [t7])
                t_n2[t] = t9
                t_st2 = k.dma("scalar", lambda s, t=t: s.dma_start(
                    out=n2_o[t * 128:(t + 1) * 128, :],
                    in_=n2tm[:, t * D:(t + 1) * D]), sln2[t % 2], after=[t9])
                store_toks.append(t_st2)
            k.flush(block)
            bar2 = k.bar()

        with ExitStack() as s3:
            E3 = s3.enter_context
            n2fm = E3(nc.sbuf_tensor("n2fm", [128, ND * TOK], BF16))
            pstA = E3(nc.psum_tensor("pstA", [128, 128], BF16))
            pstB = E3(nc.psum_tensor("pstB", [128, 128], BF16))
            psTA = E3(nc.psum_tensor("psTA", [128, 128], F32))
            psTB = E3(nc.psum_tensor("psTB", [128, 128], F32))
            psC0 = E3(nc.psum_tensor("psC0", [8, 1], F32))
            psC1 = E3(nc.psum_tensor("psC1", [8, 1], F32))

            t_trc = [None] * (NT * ND)
            for i in range(NT):
                for jj in range(ND):
                    b = i * ND + jj
                    pst = pstA if b % 2 == 0 else pstB
                    dep = [t_n2[i]]
                    if b == 0:
                        dep.append(tid)
                    if b >= 2:
                        dep.append(t_trc[b - 2])
                    t_tr = k.op("tensor", lambda t, i=i, jj=jj, pst=pst: t.transpose(
                        pst[:, :], n2tm[:, i * D + jj * 128:i * D + (jj + 1) * 128],
                        identb[:, :]), dep)
                    t_trc[b] = k.op("vector", lambda v, i=i, jj=jj, pst=pst:
                        v.tensor_copy(
                            n2fm[:, jj * TOK + i * 128:jj * TOK + (i + 1) * 128],
                            pst[:, :]), [t_tr])
            k.flush(block)

            wgb = E3(nc.sbuf_tensor("wgb", [128, ND * 8], BF16))
            bgr = E3(nc.sbuf_tensor("bgr", [128, 8], F32))
            tokb = E3(nc.sbuf_tensor("tokb", [128, 1], F32))
            lg = E3(nc.sbuf_tensor("lg", [128, NT * 8], F32))
            iota8 = E3(nc.sbuf_tensor("iota8", [128, 8], F32))
            iotac = E3(nc.sbuf_tensor("iotac", [128, 1], F32))
            fld = E3(nc.sbuf_tensor("fld", [128, 6 * NT], F32))
            s8a = E3(nc.sbuf_tensor("s8a", [128, 8], F32))
            s8b = E3(nc.sbuf_tensor("s8b", [128, 8], F32))
            s8c = E3(nc.sbuf_tensor("s8c", [128, 8], F32))
            s8d = E3(nc.sbuf_tensor("s8d", [128, 8], F32))
            m0 = E3(nc.sbuf_tensor("m0", [128, NT * 8], F32))
            m1t = E3(nc.sbuf_tensor("m1t", [128, NT * 8], F32))
            m0f = E3(nc.sbuf_tensor("m0f", [8, J], F32))
            m1f = E3(nc.sbuf_tensor("m1f", [8, J], F32))
            ex0 = E3(nc.sbuf_tensor("ex0", [8, J], F32))
            ex1 = E3(nc.sbuf_tensor("ex1", [8, J], F32))
            ones8 = E3(nc.sbuf_tensor("ones8", [8, J], F32))
            onescol = E3(nc.sbuf_tensor("onescol", [128, 1], F32))
            ctile = E3(nc.sbuf_tensor("ctile", [8, 2], F32))
            zrow = E3(nc.sbuf_tensor("zrow", [1, J], F32))
            ivals = E3(nc.sbuf_tensor("ivals", [128, 2], F32))
            itgtf = E3(nc.sbuf_tensor("itgtf", [128, 4], F32))
            itgt = E3(nc.sbuf_tensor("itgt", [128, 4], I32))

            t_wg = k.dma("gpsimd", lambda g: g.dma_start(
                out=wgb[:, :].rearrange("p (k e) -> p k e", k=ND),
                in_=wg_i.rearrange("(k p) e -> p k e", p=128)), k.slot(), after=bar2)
            t_bg = k.dma("gpsimd", lambda g: g.dma_start(out=bgr[:, :], in_=bg_i),
                         k.slot(), after=bar2)
            t_tb = k.dma("gpsimd", lambda g: g.dma_start(out=tokb[:, :], in_=tokb_i),
                         k.slot(), after=bar2)
            slwc = k.slot()
            for kk in range(8):
                k.dma("gpsimd", lambda g, kk=kk: g.dma_start(
                    out=w1c_o[kk * 128:(kk + 1) * 128, :],
                    in_=w1e_i[kk * 128:(kk + 1) * 128, :]), slwc)
                k.dma("gpsimd", lambda g, kk=kk: g.dma_start(
                    out=w2c_o[kk * 512:(kk + 1) * 512, :],
                    in_=w2e_i[kk * 512:(kk + 1) * 512, :]), slwc)
            store_toks.append((slwc[0], 16 * 16))
            t_io = k.op("gpsimd", lambda g: g.iota(
                iota8[:, :], pattern=[[1, 8]], base=0, channel_multiplier=0,
                allow_small_or_imprecise_dtypes=True))
            t_ic = k.op("gpsimd", lambda g: g.iota(
                iotac[:, :], pattern=[[0, 1]], base=1, channel_multiplier=1,
                allow_small_or_imprecise_dtypes=True))
            t_on = k.op("vector", lambda v: v.memset(onescol[:, :], 1.0))
            t_o8 = k.op("vector", lambda v: v.memset(ones8[:, :], 1.0))
            t_zr = k.op("vector", lambda v: v.memset(zrow[:, :], 0.0))
            slz = k.slot()
            for r in range(16):
                k.dma("sync", lambda s, r=r: s.dma_start(
                    out=T_o[r:r + 1, :], in_=zrow[:, :]), slz,
                    after=[t_zr] if r == 0 else [])
            t_zall = (slz[0], slz[1])

            t_lg = [None] * NT
            for t in range(NT):
                ps = psA if t % 2 == 0 else psB
                dep = [t_wg, t_trc[NT * ND - 1]]
                if t >= 2:
                    dep.append(t_lg[t - 2])

                def mmg(tt, t=t, ps=ps):
                    for kk in range(ND):
                        ins = tt.matmul(
                            ps[:, 0:8],
                            n2fm[:, kk * TOK + t * 128:kk * TOK + (t + 1) * 128],
                            wgb[:, kk * 8:(kk + 1) * 8],
                            start=(kk == 0), stop=(kk == ND - 1))
                    return ins
                t_m = k.op("tensor", mmg, dep)
                t_lg[t] = k.op("vector", lambda v, t=t, ps=ps: v.tensor_add(
                    lg[:, t * 8:(t + 1) * 8], ps[:, 0:8], bgr[:, :]),
                    [t_m, t_bg])

            t_m0w = [None] * NT
            t_m1w = [None] * NT
            t_gts = [None] * NT
            for t in range(NT):
                lgt = lg[:, t * 8:(t + 1) * 8]
                a1 = k.op("vector", lambda v, lgt=lgt: v.reduce_max(
                    cols[:, 0:1], lgt, axis=AX), [t_lg[t], t_io, t_on])
                a2 = k.op("vector", lambda v, lgt=lgt: v.tensor_scalar(
                    s8a[:, :], lgt, cols[:, 0:1], None, OP.is_equal), [a1])
                a3 = k.op("vector", lambda v: v.tensor_mul(s8b[:, :], s8a[:, :],
                                                           iota8[:, :]), [a2])
                a4 = k.op("vector", lambda v: v.tensor_scalar(
                    s8c[:, :], s8a[:, :], 9.0, None, OP.mult), [a3])
                a5 = k.op("vector", lambda v: v.tensor_sub(s8b[:, :], s8b[:, :],
                                                           s8c[:, :]), [a4])
                a6 = k.op("vector", lambda v: v.tensor_scalar(
                    s8b[:, :], s8b[:, :], 9.0, None, OP.add), [a5])
                a7 = k.op("vector", lambda v, t=t: v.tensor_reduce(
                    fld[:, 0 * NT + t:0 * NT + t + 1], s8b[:, :], AX, OP.min), [a6])
                a8 = k.op("vector", lambda v, t=t: v.tensor_scalar(
                    s8c[:, :], iota8[:, :], fld[:, 0 * NT + t:0 * NT + t + 1],
                    None, OP.is_equal), [a7])
                a9 = k.op("vector", lambda v: v.tensor_scalar(
                    s8c[:, :], s8c[:, :], 1e9, None, OP.mult), [a8])
                a10 = k.op("vector", lambda v, lgt=lgt: v.tensor_sub(
                    s8d[:, :], lgt, s8c[:, :]), [a9])
                a11 = k.op("vector", lambda v: v.reduce_max(
                    cols[:, 1:2], s8d[:, :], axis=AX), [a10])
                a12 = k.op("vector", lambda v: v.tensor_scalar(
                    s8c[:, :], s8d[:, :], cols[:, 1:2], None, OP.is_equal), [a11])
                a13 = k.op("vector", lambda v: v.tensor_mul(s8d[:, :], s8c[:, :],
                                                            iota8[:, :]), [a12])
                a14 = k.op("vector", lambda v: v.tensor_scalar(
                    s8c[:, :], s8c[:, :], 9.0, None, OP.mult), [a13])
                a15 = k.op("vector", lambda v: v.tensor_sub(s8d[:, :], s8d[:, :],
                                                            s8c[:, :]), [a14])
                a16 = k.op("vector", lambda v: v.tensor_scalar(
                    s8d[:, :], s8d[:, :], 9.0, None, OP.add), [a15])
                a17 = k.op("vector", lambda v, t=t: v.tensor_reduce(
                    fld[:, 1 * NT + t:1 * NT + t + 1], s8d[:, :], AX, OP.min), [a16])
                a18 = k.op("vector", lambda v: v.tensor_sub(
                    cols[:, 2:3], cols[:, 0:1], cols[:, 1:2]), [a17, a11])
                a19 = k.op("scalar", lambda s, t=t: s.activation(
                    fld[:, 4 * NT + t:4 * NT + t + 1], cols[:, 2:3], AF.Sigmoid),
                    [a18])
                a20 = k.op("vector", lambda v, t=t: v.tensor_scalar(
                    fld[:, 5 * NT + t:5 * NT + t + 1],
                    fld[:, 4 * NT + t:4 * NT + t + 1], -1.0, 1.0,
                    OP.mult, OP.add), [a19])
                t_gts[t] = a20
                t_m0w[t] = k.op("vector", lambda v, t=t: v.tensor_scalar(
                    m0[:, t * 8:(t + 1) * 8], iota8[:, :],
                    fld[:, 0 * NT + t:0 * NT + t + 1], None, OP.is_equal), [a7])
                t_m1w[t] = k.op("vector", lambda v, t=t: v.tensor_scalar(
                    m1t[:, t * 8:(t + 1) * 8], iota8[:, :],
                    fld[:, 1 * NT + t:1 * NT + t + 1], None, OP.is_equal), [a17])
            k.flush(block)

            t_cnt0 = None
            t_cnt1 = None

            def mmc0(tt):
                for t in range(NT):
                    ins = tt.matmul(psC0[:, :], m0[:, t * 8:(t + 1) * 8],
                                    onescol[:, :], start=(t == 0),
                                    stop=(t == NT - 1))
                return ins

            def mmc1(tt):
                for t in range(NT):
                    ins = tt.matmul(psC1[:, :], m1t[:, t * 8:(t + 1) * 8],
                                    onescol[:, :], start=(t == 0),
                                    stop=(t == NT - 1))
                return ins
            t_cnt0 = k.op("tensor", mmc0, t_m0w + [t_on])
            t_cnt1 = k.op("tensor", mmc1, t_m1w)
            tc0 = k.op("vector", lambda v: v.tensor_copy(ctile[:, 0:1], psC0[:, :]),
                       [t_cnt0])
            tc1 = k.op("vector", lambda v: v.tensor_copy(ctile[:, 1:2], psC1[:, :]),
                       [t_cnt1])
            t_stc = k.dma("sync", lambda s: s.dma_start(out=counts_o, in_=ctile[:, :]),
                          k.slot(), after=[tc0, tc1])
            store_toks.append(t_stc)

            t_mf = [None] * (2 * NT)
            for t in range(NT):
                for kk, (src, dst) in enumerate(((m0, m0f), (m1t, m1f))):
                    b = 2 * t + kk
                    pst = psTA if b % 2 == 0 else psTB
                    dep = [t_m0w[t] if kk == 0 else t_m1w[t], tidf]
                    if b >= 2:
                        dep.append(t_mf[b - 2])
                    t_tr = k.op("tensor", lambda tt, t=t, src=src, pst=pst:
                        tt.transpose(pst[0:8, 0:128], src[:, t * 8:(t + 1) * 8],
                                     identf[:, :]), dep)
                    t_mf[b] = k.op("vector", lambda v, t=t, dst=dst, pst=pst:
                        v.tensor_copy(dst[:, t * 128:(t + 1) * 128],
                                      pst[0:8, 0:128]), [t_tr])
            t_sc0 = k.op("vector", lambda v: v.tensor_tensor_scan(
                ex0[:, :], ones8[:, :], m0f[:, :], 0.0, OP.mult, OP.add),
                [t_mf[2 * NT - 2], t_mf[2 * NT - 1], t_o8])
            t_sc0b = k.op("vector", lambda v: v.tensor_sub(ex0[:, :], ex0[:, :],
                                                           m0f[:, :]), [t_sc0])
            t_sc1 = k.op("vector", lambda v: v.tensor_tensor_scan(
                ex1[:, :], ones8[:, :], m1f[:, :], 0.0, OP.mult, OP.add), [t_sc0b])
            t_sc1b = k.op("vector", lambda v: v.tensor_sub(ex1[:, :], ex1[:, :],
                                                           m1f[:, :]), [t_sc1])
            k.flush(block)

            slsc = [k.slot(), k.slot()]
            t_lpd = [None] * (2 * NT)
            for t in range(NT):
                for kk in range(2):
                    b = 2 * t + kk
                    exf = ex0 if kk == 0 else ex1
                    msk = m0 if kk == 0 else m1t
                    pst = psTA if b % 2 == 0 else psTB
                    dep = [t_sc0b if kk == 0 else t_sc1b]
                    if b >= 2:
                        dep.append(t_lpd[b - 2])
                    t_tr = k.op("tensor", lambda tt, t=t, exf=exf, pst=pst:
                        tt.transpose(pst[0:128, 0:8], exf[:, t * 128:(t + 1) * 128],
                                     identf[0:8, 0:8]), dep)
                    t_cp = k.op("vector", lambda v, pst=pst: v.tensor_copy(
                        s8a[:, :], pst[0:128, 0:8]), [t_tr])
                    t_mu = k.op("vector", lambda v, t=t, msk=msk: v.tensor_mul(
                        s8b[:, :], msk[:, t * 8:(t + 1) * 8], s8a[:, :]), [t_cp])
                    t_lp = k.op("vector", lambda v, t=t, kk=kk: v.reduce_sum(
                        fld[:, (2 + kk) * NT + t:(2 + kk) * NT + t + 1],
                        s8b[:, :], axis=AX), [t_mu])
                    t_lpd[b] = t_lp
                    c0 = 2 * (t % 2) + kk if False else (t % 2)
                    tv = k.op("vector", lambda v, t=t, c0=c0: v.scalar_tensor_tensor(
                        ivals[:, c0:c0 + 1], iotac[:, :], float(t * 128),
                        tokb[:, :], OP.add, OP.add),
                        [t_lp, t_ic, t_tb,
                         (slsc[t % 2][0], slsc[t % 2][1])])
                    cc = 2 * (t % 2) + kk
                    tg1 = k.op("vector", lambda v, t=t, kk=kk, cc=cc:
                        v.scalar_tensor_tensor(
                            itgtf[:, cc:cc + 1], fld[:, kk * NT + t:kk * NT + t + 1],
                            2048.0, fld[:, (2 + kk) * NT + t:(2 + kk) * NT + t + 1],
                            OP.mult, OP.add), [t_lp])
                    tg2 = k.op("vector", lambda v, kk=kk, cc=cc: v.tensor_scalar(
                        itgtf[:, cc:cc + 1], itgtf[:, cc:cc + 1],
                        float(kk * 16384), None, OP.add), [tg1])
                    tg3 = k.op("vector", lambda v, cc=cc: v.tensor_copy(
                        itgt[:, cc:cc + 1], itgtf[:, cc:cc + 1]), [tg2])
                    t_sct = k.dma("gpsimd", lambda g, c0=c0, cc=cc:
                        g.indirect_dma_start(
                            T_flat, IndirectOffsetOnAxis(itgt[:, cc:cc + 1], 0),
                            ivals[:, c0:c0 + 1], None),
                        slsc[t % 2], after=[tv, tg3, t_zall])
                    store_toks.append(t_sct)
            slr = k.slot()
            for f in range(6):
                t_r = k.dma("sync", lambda s, f=f: s.dma_start(
                    out=route_o[:, f * NT:(f + 1) * NT],
                    in_=fld[:, f * NT:(f + 1) * NT]), slr,
                    after=[t_lpd[2 * NT - 1], t_gts[NT - 1]])
                store_toks.append(t_r)
            k.flush(block)
        k.final_wait(block, "sync", store_toks)
    return nc


def l2_inputs(simp4, vg, xacc, inp, core):
    rep = lambda a: np.ascontiguousarray(
        np.broadcast_to(np.asarray(a, np.float32)[None, :], (128, a.shape[-1])))
    return {
        "simp4": np.ascontiguousarray(simp4.reshape(4 * J, J), BF),
        "vg_i": np.ascontiguousarray(vg, BF),
        "xacc_i": np.ascontiguousarray(xacc, np.float32),
        "Wout": np.ascontiguousarray(inp["L0_attn_W_out"], np.float32),
        "bout_r": np.ascontiguousarray(
            np.asarray(inp["L0_attn_b_out"], np.float32)[None, :]),
        "Wg": np.ascontiguousarray(inp["L0_ffn_Wg"], np.float32),
        "bg_r": rep(inp["L0_ffn_bg"]),
        "ident": np.eye(128, dtype=np.float32),
        "tokbase_r": np.full((128, 1), core * TOK, np.float32),
        "W1e": np.ascontiguousarray(np.asarray(inp["L0_ffn_W1"], np.float32)[core]),
        "W2e": np.ascontiguousarray(np.asarray(inp["L0_ffn_W2"], np.float32)[core]),
    }


NPAN = 10


def build_l3(debug=False):
    nc = _bass()
    n2g_i = nc.dram_tensor("n2glob", [NCORES * TOK + 128, D], BF16,
                           kind="ExternalInput").ap()
    tmy_i = nc.dram_tensor("Tmy", [18, TOK], F32, kind="ExternalInput").ap()
    tmy_flat = tmy_i.rearrange("a b -> (a b)")[:, None]
    cnt_i = nc.dram_tensor("cntrow", [1, 16], F32, kind="ExternalInput").ap()
    w1_i = nc.dram_tensor("W1", [D, F4], BF16, kind="ExternalInput").ap()
    b1_i = nc.dram_tensor("b1_c", [128, F4 // 128], F32, kind="ExternalInput").ap()
    w2_i = nc.dram_tensor("W2", [F4, D], BF16, kind="ExternalInput").ap()
    b2_i = nc.dram_tensor("b2_r", [128, D], F32, kind="ExternalInput").ap()

    y_o = nc.dram_tensor("y_o", [CAPP, D], BF16, kind="ExternalOutput").ap()
    ident_i3 = nc.dram_tensor("ident", [128, 128], F32, kind="ExternalInput").ap()

    with ExitStack() as ctx:
        E = ctx.enter_context
        k = K(nc, ctx)
        block = E(nc.Block())
        w1b = E(nc.sbuf_tensor("w1b", [128, ND * F4], BF16))
        w2b = E(nc.sbuf_tensor("w2b", [128, 32 * D], BF16))
        hpan = E(nc.sbuf_tensor("hpan", [128, 32 * 512], BF16))
        buffm = E(nc.sbuf_tensor("buffm", [128, 2 * ND * 512], BF16))
        b1c = E(nc.sbuf_tensor("b1c", [128, F4 // 128], F32))
        b2r = E(nc.sbuf_tensor("b2r", [128, D], F32))
        ystage = E(nc.sbuf_tensor("ystage", [128, 2 * D], BF16))
        crow = E(nc.sbuf_tensor("crow", [1, 16], F32))
        ones16 = E(nc.sbuf_tensor("ones16", [1, 16], F32))
        ones1r = E(nc.sbuf_tensor("ones1r", [1, 128], F32))
        bases = E(nc.sbuf_tensor("bases", [1, 32], F32))
        basesrep = E(nc.sbuf_tensor("basesrep", [128, 32], F32))
        iotaS = E(nc.sbuf_tensor("iotaS", [128, 39], F32))
        acc39 = E(nc.sbuf_tensor("acc39", [128, 39], F32))
        m39 = E(nc.sbuf_tensor("m39", [128, 39], F32))
        buftm = E(nc.sbuf_tensor("buftm", [128, 4 * D], BF16))
        identb3 = E(nc.sbuf_tensor("identb3", [128, 128], BF16))
        tokf = E(nc.sbuf_tensor("tokf", [128, 39], F32))
        idx2i = E(nc.sbuf_tensor("idx2i", [128, 39], I32))
        idxi = E(nc.sbuf_tensor("idxi", [128, 39], I32))
        psmm = [E(nc.psum_tensor(f"ps{i}", [128, 512], F32)) for i in range(4)]
        psR = E(nc.psum_tensor("psR", [128, 32], F32))
        store_toks = []

        t_cl = k.dma("sync", lambda s: s.dma_start(out=crow[:, :], in_=cnt_i),
                     k.slot())
        t_b1 = k.dma("sync", lambda s: s.dma_start(out=b1c[:, :], in_=b1_i), k.slot())
        t_b2 = k.dma("sync", lambda s: s.dma_start(out=b2r[:, :], in_=b2_i), k.slot())
        t_id3 = k.dma("gpsimd", lambda g: g.dma_start(out=identb3[:, :],
                                                      in_=ident_i3), k.slot())
        slw1 = k.slot()
        slw2 = k.slot()
        for kk in range(ND):
            k.dma("sync", lambda s, kk=kk: s.dma_start(
                out=w1b[:, kk * F4:(kk + 1) * F4],
                in_=w1_i[kk * 128:(kk + 1) * 128, :]), slw1)
        for k2 in range(8):
            k.dma("scalar", lambda s, k2=k2: s.dma_start(
                out=w2b[:, :].rearrange("p (k dd) -> p k dd", k=32)[:, k2 * 4:(k2 + 1) * 4, :],
                in_=w2_i[k2 * 512:(k2 + 1) * 512, :].rearrange(
                    "(k p) dd -> p k dd", p=128)), slw2)
        t_w1_all = (slw1[0], 8 * 16)
        t_w2_all = (slw2[0], 8 * 16)

        t_o16 = k.op("vector", lambda v: v.memset(ones16[:, :], 1.0))
        t_o1r = k.op("vector", lambda v: v.memset(ones1r[:, :], 1.0))
        t_io = k.op("gpsimd", lambda g: g.iota(
            iotaS[:, :], pattern=[[128, 39]], base=0, channel_multiplier=1,
            allow_small_or_imprecise_dtypes=True))
        t_sc = k.op("vector", lambda v: v.tensor_tensor_scan(
            bases[:, 0:16], ones16[:, :], crow[:, :], 0.0, OP.mult, OP.add),
            [t_cl, t_o16])
        t_ex = k.op("vector", lambda v: v.tensor_sub(bases[:, 0:16], bases[:, 0:16],
                                                     crow[:, :]), [t_sc])
        t_jm0 = k.op("vector", lambda v: v.memset(bases[:, 16:17], 0.0))
        t_jm = k.op("vector", lambda v: v.tensor_scalar(
            bases[:, 17:32], crow[:, 0:15], -1.0, 2048.0, OP.mult, OP.add),
            [t_ex, t_jm0])
        t_rep = k.op("tensor", lambda t: t.matmul(
            psR[:, :], ones1r[:, :], bases[:, :], start=True, stop=True),
            [t_jm, t_o1r])
        t_repc = k.op("vector", lambda v: v.tensor_copy(basesrep[:, :], psR[:, :]),
                      [t_rep])
        t_acc = k.op("vector", lambda v: v.tensor_copy(acc39[:, :], iotaS[:, :]),
                     [t_io])
        prev = t_acc
        for r in range(1, 16):
            tm_ = k.op("vector", lambda v, r=r: v.tensor_scalar(
                m39[:, :], iotaS[:, :], basesrep[:, r:r + 1], None, OP.is_ge),
                [prev, t_repc])
            prev = k.op("vector", lambda v, r=r: v.scalar_tensor_tensor(
                acc39[:, :], m39[:, :], basesrep[:, 16 + r:17 + r], acc39[:, :],
                OP.mult, OP.add), [tm_])
        t_i2 = k.op("vector", lambda v: v.tensor_copy(idx2i[:, :], acc39[:, :]),
                    [prev])
        k.flush(block)

        sltk = [k.slot() for _ in range(4)]
        t_tk = [None] * 39
        t_fix = [None] * 39

        def emit_tokgrp(gidx):
            g0 = 4 * gidx
            g1 = min(g0 + 4, 39)
            for a in range(g0, g1):
                dep = [t_i2] if a < 4 else [t_tk[a - 4]]
                t_tk[a] = k.dma("gpsimd", lambda g, a=a: g.indirect_dma_start(
                    tokf[:, a:a + 1], None, tmy_flat,
                    IndirectOffsetOnAxis(idx2i[:, a:a + 1], 0)), sltk[a % 4],
                    after=dep)
            depf = [t_tk[a] for a in range(g0, g1)]
            f1 = k.op("vector", lambda v, g0=g0, g1=g1: v.tensor_scalar(
                m39[:, g0:g1], tokf[:, g0:g1], 0.0, None, OP.is_equal), depf)
            f2 = k.op("vector", lambda v, g0=g0, g1=g1: v.scalar_tensor_tensor(
                m39[:, g0:g1], m39[:, g0:g1], float(NCORES * TOK + 1),
                tokf[:, g0:g1], OP.mult, OP.add), [f1])
            f3 = k.op("vector", lambda v, g0=g0, g1=g1: v.tensor_scalar(
                m39[:, g0:g1], m39[:, g0:g1], -1.0, None, OP.add), [f2])
            f4 = k.op("vector", lambda v, g0=g0, g1=g1: v.tensor_copy(
                idxi[:, g0:g1], m39[:, g0:g1]), [f3])
            for a in range(g0, g1):
                t_fix[a] = f4
        emit_tokgrp(0)
        slga = [k.slot() for _ in range(4)]
        t_gablk = [None] * 39
        t_trcp = [None] * 39

        def emit_gather(a):
            par = a % 4
            dep_g = [t_fix[a]]
            if a >= 4:
                dep_g.append(t_trcp[a - 4])
            t_gablk[a] = k.dma("gpsimd", lambda g, a=a, par=par:
                g.indirect_dma_start(
                    buftm[:, par * D:(par + 1) * D], None,
                    n2g_i, IndirectOffsetOnAxis(idxi[:, a:a + 1], 0)),
                slga[par], after=dep_g)
        for a in range(4):
            emit_gather(a)
        k.flush(block)

        pstT = [E(nc.psum_tensor("pstT0", [128, 128], BF16)),
                E(nc.psum_tensor("pstT1", [128, 128], BF16))]
        trn_cnt = [0]
        t_trcv = [None, None]
        slys = [k.slot(), k.slot()]
        t_h = {}
        t_y = {}
        ev1_flat = []
        ev2_flat = []
        ev2_key = {}
        t_mm2_last = None
        b1cnt = 0
        b2cnt = 0
        for p in range(NPAN):
            nsl = 512 if p < NPAN - 1 else CAPP - 512 * (NPAN - 1)
            nsm = nsl // 128
            pb = (p % 2) * ND * 512
            blks = range(4 * p, min(4 * p + 4, 39))
            tr_toks = []
            for a in blks:
                par = a % 4
                for kk in range(ND):
                    bb = trn_cnt[0]
                    trn_cnt[0] += 1
                    pst = pstT[bb % 2]
                    dep_t = [t_gablk[a], t_id3]
                    if bb >= 2:
                        dep_t.append(t_trcv[bb % 2])
                    t_tr = k.op("tensor", lambda t, par=par, kk=kk, pst=pst:
                        t.transpose(pst[:, :],
                                    buftm[:, par * D + kk * 128:par * D + (kk + 1) * 128],
                                    identb3[:, :]), dep_t)
                    t_cv = k.op("vector", lambda v, p=p, a=a, kk=kk, pb=pb, pst=pst:
                        v.tensor_copy(
                            buffm[:, pb + kk * 512 + (a - 4 * p) * 128:
                                  pb + kk * 512 + (a - 4 * p) * 128 + 128],
                            pst[:, :]), [t_tr])
                    t_trcv[bb % 2] = t_cv
                    tr_toks.append(t_cv)
                t_trcp[a] = tr_toks[-1]
            t_trall = tr_toks[-1]
            if p + 1 < NPAN:
                emit_tokgrp(p + 1)
            for a2 in range(4 * (p + 1), min(4 * (p + 2), 39)):
                emit_gather(a2)
            for m in range(32):
                b = b1cnt
                b1cnt += 1
                ps = psmm[b % 2]
                dep = [t_trall, t_w1_all]
                if b >= 2:
                    dep.append(ev1_flat[b - 2])

                def mm1(t, m=m, ps=ps, pb=pb, nsl=nsl):
                    for kk in range(ND):
                        ins = t.matmul(
                            ps[:, 0:nsl],
                            w1b[:, kk * F4 + m * 128:kk * F4 + (m + 1) * 128],
                            buffm[:, pb + kk * 512:pb + kk * 512 + nsl],
                            start=(kk == 0), stop=(kk == ND - 1))
                    return ins
                t_m = k.op("tensor", mm1, dep)
                dep_gelu = [t_m, t_b1]
                if p > 0 and m == 0:
                    dep_gelu.append(t_mm2_last)
                t_e = k.op("scalar", lambda s, m=m, ps=ps, nsl=nsl: s.activation(
                    hpan[:, m * 512:m * 512 + nsl], ps[:, 0:nsl],
                    AF.Gelu_apprx_tanh, bias=b1c[:, m:m + 1]),
                    dep_gelu)
                t_h[(p, m)] = t_e
                ev1_flat.append(t_e)
            for sm in range(nsm):
                for dp in range(2):
                    b = b2cnt
                    b2cnt += 1
                    ps = psmm[2 + b % 2]
                    dep = [t_h[(p, 15)], t_w2_all]
                    if b >= 2:
                        dep.append(ev2_flat[b - 2])

                    def mm2a(t, sm=sm, dp=dp, ps=ps):
                        for kk in range(16):
                            ins = t.matmul(
                                ps[:, :],
                                hpan[:, kk * 512 + sm * 128:kk * 512 + (sm + 1) * 128],
                                w2b[:, kk * D + dp * 512:kk * D + (dp + 1) * 512],
                                start=(kk == 0), stop=False)
                        return ins

                    def mm2b(t, sm=sm, dp=dp, ps=ps):
                        for kk in range(16, 32):
                            ins = t.matmul(
                                ps[:, :],
                                hpan[:, kk * 512 + sm * 128:kk * 512 + (sm + 1) * 128],
                                w2b[:, kk * D + dp * 512:kk * D + (dp + 1) * 512],
                                start=False, stop=(kk == 31))
                        return ins
                    k.op("tensor", mm2a, dep)
                    t_m2 = k.op("tensor", mm2b, [t_h[(p, 31)]])
                    t_mm2_last = t_m2
                    yh = (sm % 2) * D
                    dep_e = [t_m2, t_b2]
                    if b >= 4:
                        dep_e.append(t_y[ev2_key[b - 4]])
                    t_e2 = k.op("vector", lambda v, yh=yh, dp=dp, ps=ps: v.tensor_add(
                        ystage[:, yh + dp * 512:yh + (dp + 1) * 512], ps[:, :],
                        b2r[:, dp * 512:(dp + 1) * 512]), dep_e)
                    ev2_flat.append(t_e2)
                    ev2_key[b] = (p, sm, dp)
                t_st = k.dma("scalar", lambda s, p=p, sm=sm, yh=yh:
                    s.dma_start(out=y_o[p * 512 + sm * 128:p * 512 + (sm + 1) * 128, :],
                                in_=ystage[:, yh:yh + D]), slys[sm % 2],
                    after=[ev2_flat[-1], ev2_flat[-2]])
                t_y[ev2_key[b2cnt - 1]] = t_st
                t_y[ev2_key[b2cnt - 2]] = t_st
                store_toks.append(t_st)
        k.flush(block)
        k.final_wait(block, "sync", store_toks)
    return nc


def l3_inputs(n2glob, T_all, counts_all, inp, core):
    e = core
    tmy = np.zeros((18, TOK), np.float32)
    crow = np.zeros((1, 16), np.float32)
    for kk in range(2):
        for cp in range(8):
            tmy[kk * 8 + cp] = T_all[cp][kk * 8 + e]
            crow[0, kk * 8 + cp] = counts_all[cp][e, kk]
    b1 = np.asarray(inp["L0_ffn_b1"], np.float32)[e]
    b2 = np.asarray(inp["L0_ffn_b2"], np.float32)[e]
    W1 = np.asarray(inp["w1b_pre"])
    W2 = np.asarray(inp["w2b_pre"])
    return {
        "n2glob": np.ascontiguousarray(
            np.concatenate([np.asarray(n2glob, BF),
                            np.zeros((128, D), BF)], 0)),
        "Tmy": tmy,
        "cntrow": crow,
        "W1": np.ascontiguousarray(W1),
        "b1_c": np.ascontiguousarray(b1.reshape(F4 // 128, 128).T),
        "W2": np.ascontiguousarray(W2),
        "b2_r": np.ascontiguousarray(np.broadcast_to(b2[None, :], (128, D))),
        "ident": np.eye(128, dtype=np.float32),
    }


def build_l4():
    nc = _bass()
    yg_i = nc.dram_tensor("yglob", [NCORES * CAPP + 128, D], BF16,
                          kind="ExternalInput").ap()
    xmid_i = nc.dram_tensor("xmid_i", [TOK, D], F32, kind="ExternalInput").ap()
    route_i = nc.dram_tensor("route_i", [128, 96], F32, kind="ExternalInput").ap()
    cnt0_i = nc.dram_tensor("cnt0", [8, 8], F32, kind="ExternalInput").ap()
    cnt1_i = nc.dram_tensor("cnt1", [8, 8], F32, kind="ExternalInput").ap()
    mlt_i = nc.dram_tensor("masklt", [8, 2], F32, kind="ExternalInput").ap()
    ident_i = nc.dram_tensor("ident", [128, 128], F32, kind="ExternalInput").ap()

    out_o = nc.dram_tensor("out_o", [TOK, D], F32, kind="ExternalOutput").ap()

    with ExitStack() as ctx:
        E = ctx.enter_context
        k = K(nc, ctx)
        block = E(nc.Block())
        identf = E(nc.sbuf_tensor("identf", [128, 128], F32))
        cnt0b = E(nc.sbuf_tensor("cnt0b", [8, 8], F32))
        cnt1b = E(nc.sbuf_tensor("cnt1b", [8, 8], F32))
        mltb = E(nc.sbuf_tensor("mltb", [8, 2], F32))
        fld = E(nc.sbuf_tensor("fld", [128, 96], F32))
        iota8 = E(nc.sbuf_tensor("iota8", [128, 8], F32))
        onesr = E(nc.sbuf_tensor("onesr", [1, 128], F32))
        pre8 = E(nc.sbuf_tensor("pre8", [8, 2], F32))
        pre8b = E(nc.sbuf_tensor("pre8b", [8, 2], F32))
        rows28 = E(nc.sbuf_tensor("rows28", [1, 16], F32))
        rows28b = E(nc.sbuf_tensor("rows28b", [1, 16], F32))
        b1row = E(nc.sbuf_tensor("b1row", [1, 8], F32))
        brep = E(nc.sbuf_tensor("brep", [128, 16], F32))
        m8 = E(nc.sbuf_tensor("m8", [128, 8], F32))
        s8 = E(nc.sbuf_tensor("s8", [128, 8], F32))
        colsq = E(nc.sbuf_tensor("colsq", [128, 16], F32))
        idxg = E(nc.sbuf_tensor("idxg", [128, 8], I32))
        idxgf = E(nc.sbuf_tensor("idxgf", [128, 8], F32))
        At = E(nc.sbuf_tensor("At", [128, 4 * D], BF16))
        Bt = E(nc.sbuf_tensor("Bt", [128, 4 * D], BF16))
        xmb = E(nc.sbuf_tensor("xmb", [128, 4 * D], F32))
        outs = E(nc.sbuf_tensor("outs", [128, 4 * D], F32))
        psA = E(nc.psum_tensor("psA", [128, 512], F32))
        psB = E(nc.psum_tensor("psB", [128, 512], F32))
        store_toks = []

        slc = k.slot()
        k.dma("sync", lambda s: s.dma_start(out=cnt0b[:, :], in_=cnt0_i), slc)
        k.dma("sync", lambda s: s.dma_start(out=cnt1b[:, :], in_=cnt1_i), slc)
        k.dma("sync", lambda s: s.dma_start(out=mltb[:, :], in_=mlt_i), slc)
        k.dma("sync", lambda s: s.dma_start(out=identf[:, :], in_=ident_i), slc)
        t_call = k.dma("sync", lambda s: s.dma_start(out=fld[:, :], in_=route_i), slc)
        t_c0 = t_c1 = t_ml = t_idf = t_fl = t_call
        t_io = k.op("gpsimd", lambda g: g.iota(
            iota8[:, :], pattern=[[1, 8]], base=0, channel_multiplier=0,
            allow_small_or_imprecise_dtypes=True))
        t_on = k.op("vector", lambda v: v.memset(onesr[:, :], 1.0))
        t_mA = k.op("vector", lambda v: v.memset(At[:, :], 0))
        t_mB = k.op("vector", lambda v: v.memset(Bt[:, :], 0))

        def mmb0(t):
            return t.matmul(psA[0:8, 0:2], cnt0b[:, :], mltb[:, :],
                            start=True, stop=True)

        def mmb1(t):
            return t.matmul(psB[0:8, 0:2], cnt1b[:, :], mltb[:, :],
                            start=True, stop=True)
        t_b0 = k.op("tensor", mmb0, [t_c0, t_ml])
        t_b1m = k.op("tensor", mmb1, [t_c1, t_ml])
        t_p0 = k.op("vector", lambda v: v.tensor_copy(pre8[:, :], psA[0:8, 0:2]),
                    [t_b0])
        t_p1 = k.op("vector", lambda v: v.tensor_copy(pre8b[:, :], psB[0:8, 0:2]),
                    [t_b1m])
        t_t0 = k.op("tensor", lambda t: t.transpose(
            psA[0:1, 0:8], pre8[:, 0:1], identf[0:8, 0:8]), [t_p0, t_idf])
        t_t0c = k.op("vector", lambda v: v.tensor_copy(rows28[0:1, 0:8],
                                                       psA[0:1, 0:8]), [t_t0])
        t_t0b = k.op("tensor", lambda t: t.transpose(
            psB[0:1, 0:8], pre8[:, 1:2], identf[0:8, 0:8]), [t_p0, t_idf, t_p1])
        t_t0bc = k.op("vector", lambda v: v.tensor_copy(rows28[0:1, 8:16],
                                                        psB[0:1, 0:8]), [t_t0b])
        t_t1 = k.op("tensor", lambda t: t.transpose(
            psA[0:1, 8:16], pre8b[:, 0:1], identf[0:8, 0:8]),
            [t_p1, t_t0c, t_t0])
        t_t1c = k.op("vector", lambda v: v.tensor_copy(rows28b[0:1, 0:8],
                                                       psA[0:1, 8:16]), [t_t1])
        t_br = k.op("vector", lambda v: v.tensor_add(
            b1row[:, :], rows28[0:1, 8:16], rows28b[0:1, 0:8]),
            [t_t1c, t_t0bc])

        def mmr0(t):
            return t.matmul(psA[0:128, 0:8], onesr[:, :], rows28[0:1, 0:8],
                            start=True, stop=True)

        def mmr1(t):
            return t.matmul(psB[0:128, 0:8], onesr[:, :], b1row[:, :],
                            start=True, stop=True)
        t_r0 = k.op("tensor", mmr0, [t_br, t_on])
        t_r1 = k.op("tensor", mmr1, [t_r0])
        t_r0c = k.op("vector", lambda v: v.tensor_copy(brep[:, 0:8], psA[0:128, 0:8]),
                     [t_r0])
        t_r1c = k.op("vector", lambda v: v.tensor_copy(brep[:, 8:16], psB[0:128, 0:8]),
                     [t_r1])
        k.flush(block)

        slg = [k.slot() for _ in range(8)]
        slxm = [k.slot() for _ in range(4)]
        slout = [k.slot() for _ in range(4)]
        t_done = [None] * NT
        t_gath = {}
        for grp in range(0, NT, 4):
            for t in range(grp, grp + 4):
                half = (t % 4) * D
                t_xl = k.dma("sync", lambda s, t=t, half=half: s.dma_start(
                    out=xmb[:, half:half + D], in_=xmid_i[t * 128:(t + 1) * 128, :]),
                    slxm[t % 4], after=[t_done[t - 4]] if t >= 4 else [t_fl])
                t_gath[(t, "x")] = t_xl
                for kk in range(2):
                    cc = (t % 4) * 2 + kk
                    e_col = fld[:, kk * NT + t:kk * NT + t + 1]
                    lp_col = fld[:, (2 + kk) * NT + t:(2 + kk) * NT + t + 1]
                    g_col = fld[:, (4 + kk) * NT + t:(4 + kk) * NT + t + 1]
                    dep0 = [t_r0c, t_r1c] if t < 4 else [t_done[t - 4]]
                    a1 = k.op("vector", lambda v, e_col=e_col: v.tensor_scalar(
                        m8[:, :], iota8[:, :], e_col, None, OP.is_equal),
                        dep0 + [t_io])
                    a2 = k.op("vector", lambda v, kk=kk: v.tensor_mul(
                        s8[:, :], m8[:, :], brep[:, kk * 8:(kk + 1) * 8]), [a1])
                    a3 = k.op("vector", lambda v, cc=cc: v.reduce_sum(
                        colsq[:, cc:cc + 1], s8[:, :], axis=AX), [a2])
                    a4 = k.op("vector", lambda v, cc=cc, lp_col=lp_col: v.tensor_add(
                        colsq[:, cc:cc + 1], colsq[:, cc:cc + 1], lp_col), [a3])
                    a5 = k.op("vector", lambda v, cc=cc: v.tensor_scalar(
                        colsq[:, 8 + cc:9 + cc], colsq[:, cc:cc + 1], float(CAP),
                        None, OP.is_lt), [a4])
                    a6 = k.op("vector", lambda v, cc=cc, g_col=g_col: v.tensor_mul(
                        colsq[:, 8 + cc:9 + cc], colsq[:, 8 + cc:9 + cc], g_col), [a5])
                    a5b = k.op("vector", lambda v, cc=cc: v.tensor_scalar(
                        s8[:, 0:1], colsq[:, cc:cc + 1], float(CAP), None,
                        OP.is_lt), [a6])
                    a7 = k.op("vector", lambda v, cc=cc, e_col=e_col:
                        v.scalar_tensor_tensor(
                            idxgf[:, cc:cc + 1], e_col, float(CAPP),
                            colsq[:, cc:cc + 1], OP.mult, OP.add), [a5b])
                    SENT = float(NCORES * CAPP)
                    a8 = k.op("vector", lambda v: v.tensor_scalar(
                        s8[:, 1:2], s8[:, 0:1], -SENT, SENT, OP.mult, OP.add), [a7])
                    a8b = k.op("vector", lambda v, cc=cc: v.tensor_mul(
                        idxgf[:, cc:cc + 1], idxgf[:, cc:cc + 1], s8[:, 0:1]), [a8])
                    a9 = k.op("vector", lambda v, cc=cc: v.tensor_add(
                        idxgf[:, cc:cc + 1], idxgf[:, cc:cc + 1], s8[:, 1:2]), [a8b])
                    a10 = k.op("vector", lambda v, cc=cc: v.tensor_copy(
                        idxg[:, cc:cc + 1], idxgf[:, cc:cc + 1]), [a9])
                    tgt = At if kk == 0 else Bt
                    t_g = k.dma("gpsimd", lambda g, cc=cc, tgt=tgt, half=half:
                        g.indirect_dma_start(
                            tgt[:, half:half + D], None,
                            yg_i, IndirectOffsetOnAxis(idxg[:, cc:cc + 1], 0)),
                        slg[cc], after=[a10, t_mA, t_mB])
                    t_gath[(t, kk)] = t_g
            for t in range(grp, grp + 4):
                half = (t % 4) * D
                dep_c1 = [t_gath[(t, "x")], t_gath[(t, 0)]]
                if t >= 4:
                    dep_c1.append((slout[t % 4][0], slout[t % 4][1]))
                c1 = k.op("vector", lambda v, half=half, t=t: v.scalar_tensor_tensor(
                    outs[:, half:half + D], At[:, half:half + D],
                    colsq[:, 8 + (t % 4) * 2:9 + (t % 4) * 2],
                    xmb[:, half:half + D], OP.mult, OP.add),
                    dep_c1)
                c2 = k.op("vector", lambda v, half=half, t=t: v.scalar_tensor_tensor(
                    outs[:, half:half + D], Bt[:, half:half + D],
                    colsq[:, 8 + (t % 4) * 2 + 1:9 + (t % 4) * 2 + 1],
                    outs[:, half:half + D], OP.mult, OP.add),
                    [c1, t_gath[(t, 1)]])
                t_st = k.dma("scalar", lambda s, t=t, half=half: s.dma_start(
                    out=out_o[t * 128:(t + 1) * 128, :], in_=outs[:, half:half + D]),
                    slout[t % 4], after=[c2])
                t_done[t] = c2
                store_toks.append(t_st)
        k.flush(block)
        k.final_wait(block, "sync", store_toks)
    return nc


def l4_inputs(yglob, xmid, route, counts_all, core):
    cnt0 = np.stack([counts_all[c][:, 0] for c in range(8)])
    cnt1 = np.stack([counts_all[c][:, 1] for c in range(8)])
    mlt = np.zeros((8, 2), np.float32)
    mlt[:core, 0] = 1.0
    mlt[:, 1] = 1.0
    return {
        "yglob": np.ascontiguousarray(np.concatenate(
            [np.asarray(yglob, BF).reshape(NCORES * CAPP, D),
             np.zeros((128, D), BF)], 0)),
        "xmid_i": np.ascontiguousarray(xmid, np.float32),
        "route_i": np.ascontiguousarray(route, np.float32),
        "cnt0": np.ascontiguousarray(cnt0, np.float32),
        "cnt1": np.ascontiguousarray(cnt1, np.float32),
        "masklt": mlt,
        "ident": np.eye(128, dtype=np.float32),
    }


_NC_CACHE = {}


def _get_nc(name):
    if name not in _NC_CACHE:
        _NC_CACHE[name] = {"l1": build_l1, "l2": build_l2,
                           "l3": build_l3, "l4": build_l4}[name]()
    return _NC_CACHE[name]


def kernel(**inputs):
    inp = inputs
    x = np.asarray(inp["L0_attn_x"], np.float32).reshape(NCORES * TOK, D)
    cores = list(range(NCORES))

    nc1 = _get_nc("l1")
    maps1 = [l1_inputs(x[c * TOK:(c + 1) * TOK], inp) for c in cores]
    r1 = run_bass_kernel_spmd(nc1, maps1, core_ids=cores).results

    nc2 = _get_nc("l2")
    maps2 = []
    for c in cores:
        g = (c // 4) * 4
        simp4 = np.stack([r1[i]["simp_o"] for i in range(g, g + 4)])
        maps2.append(l2_inputs(simp4, r1[c]["vg_o"], r1[c]["xacc_o"], inp, c))
    r2 = run_bass_kernel_spmd(nc2, maps2, core_ids=cores).results

    nc3 = _get_nc("l3")
    n2glob = np.concatenate([r2[c]["n2_o"] for c in cores], 0)
    T_all = [r2[c]["T_o"] for c in cores]
    counts_all = [r2[c]["counts_o"] for c in cores]
    maps3 = []
    for c in cores:
        inp3 = dict(inp)
        inp3["w1b_pre"] = r2[c]["w1c_o"]
        inp3["w2b_pre"] = r2[c]["w2c_o"]
        maps3.append(l3_inputs(n2glob, T_all, counts_all, inp3, c))
    r3 = run_bass_kernel_spmd(nc3, maps3, core_ids=cores).results

    nc4 = _get_nc("l4")
    yglob = np.concatenate([r3[c]["y_o"] for c in cores], 0)
    maps4 = [l4_inputs(yglob, r2[c]["xmid_o"], r2[c]["route_o"],
                       counts_all, c) for c in cores]
    r4 = run_bass_kernel_spmd(nc4, maps4, core_ids=cores).results

    out = np.concatenate([r4[c]["out_o"] for c in cores], 0)
    return out.reshape(BB, NN, D)


# version 13
# speedup vs baseline: 1.1827x; 1.1827x over previous
import numpy as np
import ml_dtypes
from contextlib import ExitStack

import concourse.bass as bass
import concourse.mybir as mybir
from concourse.bass import IndirectOffsetOnAxis
from concourse.bass_utils import run_bass_kernel_spmd

F32 = mybir.dt.float32
BF16 = mybir.dt.bfloat16
I32 = mybir.dt.int32
AF = mybir.ActivationFunctionType
OP = mybir.AluOpType
AX = mybir.AxisListType.X
BF = ml_dtypes.bfloat16

NCORES = 8
BB, NN, D = 2, 8192, 1024
TOK = 2048
J = 2048
F4 = 4096
NT = TOK // 128
ND = D // 128
NJ = J // 128
EPS = 1e-5
INVN = 1.0 / NN
CAP = 4916
CAPP = 4992
TBL = 36864

ENGS = ("vector", "scalar", "tensor", "gpsimd", "sync")


class K:

    def __init__(self, nc, ctx):
        self.nc, self.ctx = nc, ctx
        self.esem, self.ecnt = {}, {}
        for e in ENGS:
            self.esem[e] = ctx.enter_context(nc.semaphore("es_" + e))
            self.ecnt[e] = 0
        self.hw = {e: {} for e in ENGS}
        self.pend = {e: [] for e in ENGS}
        self._slot_n = 0

    def slot(self, name=None):
        self._slot_n += 1
        h = self.ctx.enter_context(
            self.nc.semaphore(name or f"sl{self._slot_n}"))
        return [h, 0]

    def op(self, eng, fn, after=()):
        self.ecnt[eng] += 1
        v = self.ecnt[eng]
        self.pend[eng].append((fn, tuple(after), self.esem[eng], 1, v))
        return (self.esem[eng], v)

    def dma(self, eng, fn, slot, after=()):
        slot[1] += 16
        v = slot[1]
        self.pend[eng].append((fn, tuple(after), slot[0], 16, v))
        return (slot[0], v)

    def bar(self, extra=()):
        toks = [(self.esem[e], self.ecnt[e]) for e in ENGS if self.ecnt[e] > 0]
        toks += [t for t in extra if t[1] > 0]
        return toks

    def flush(self, block):
        for eng in ENGS:
            items = self.pend[eng]
            if not items:
                continue
            hw = self.hw[eng]

            def body(E, items=tuple(items), hw=hw):
                for fn, after, sem, incby, v in items:
                    for (sh, sv) in after:
                        key = id(sh)
                        if hw.get(key, -1) >= sv:
                            continue
                        E.wait_ge(sh, sv)
                        hw[key] = sv
                    fn(E).then_inc(sem, incby)
            getattr(block, eng)(body)
            self.pend[eng] = []

    def final_wait(self, block, eng, tokens):
        def body(E, tokens=tuple(tokens)):
            for (sh, sv) in tokens:
                E.wait_ge(sh, sv)
        getattr(block, eng)(body)


def _bass(scratch=16384):
    return bass.Bass("TRN2", target_bir_lowering=False,
                     dynamic_dma_scratch_size=scratch)


def build_l1():
    nc = _bass()
    x_i = nc.dram_tensor("x_sl", [TOK, D], F32, kind="ExternalInput").ap()
    lng_i = nc.dram_tensor("lng_r", [128, D], F32, kind="ExternalInput").ap()
    lnb_i = nc.dram_tensor("lnb_r", [128, D], F32, kind="ExternalInput").ap()
    wqk_i = nc.dram_tensor("Wqk", [D, J], F32, kind="ExternalInput").ap()
    bqk_i = nc.dram_tensor("bqk_r", [128, J], F32, kind="ExternalInput").ap()
    osg_i = nc.dram_tensor("osg_r", [256, J], F32, kind="ExternalInput").ap()
    osb_i = nc.dram_tensor("osb_r", [256, J], F32, kind="ExternalInput").ap()
    whid_i = nc.dram_tensor("Whid", [D, F4], F32, kind="ExternalInput").ap()
    bhid_i = nc.dram_tensor("bhid_c", [128, F4 // 128], F32,
                            kind="ExternalInput").ap()
    ident_i = nc.dram_tensor("ident", [128, 128], F32, kind="ExternalInput").ap()

    xacc_o = nc.dram_tensor("xacc_o", [TOK, D], F32, kind="ExternalOutput").ap()
    simp_o = nc.dram_tensor("simp_o", [J, J], BF16, kind="ExternalOutput").ap()
    vg_o = nc.dram_tensor("vg_o", [J, TOK], BF16, kind="ExternalOutput").ap()

    with ExitStack() as ctx:
        E = ctx.enter_context
        k = K(nc, ctx)
        block = E(nc.Block())
        n1fm = E(nc.sbuf_tensor("n1fm", [128, ND * TOK], BF16))
        identb = E(nc.sbuf_tensor("identb", [128, 128], BF16))
        cols = E(nc.sbuf_tensor("cols", [128, 32], F32))
        cols2 = E(nc.sbuf_tensor("cols2", [128, 16], F32))
        cols_pad = E(nc.sbuf_tensor("colspad", [128, 16], F32))
        psA = E(nc.psum_tensor("psA", [128, 512], F32))
        psB = E(nc.psum_tensor("psB", [128, 512], F32))
        pstA = E(nc.psum_tensor("pstA", [128, 128], BF16))
        pstB = E(nc.psum_tensor("pstB", [128, 128], BF16))
        store_toks = []

        with ExitStack() as sA:
            EA = sA.enter_context
            n1tm = EA(nc.sbuf_tensor("n1tm", [128, NT * D], BF16))
            lng = EA(nc.sbuf_tensor("lng", [128, D], F32))
            lnb = EA(nc.sbuf_tensor("lnb", [128, D], F32))
            xbuf = EA(nc.sbuf_tensor("xbuf", [128, 4 * D], F32))
            xcm = EA(nc.sbuf_tensor("xcm", [128, 4 * D], F32))
            ln1 = EA(nc.sbuf_tensor("ln1", [128, 4 * D], F32))
            sq = EA(nc.sbuf_tensor("sq", [128, D], F32))
            xas = EA(nc.sbuf_tensor("xas", [128, 4 * D], F32))

            slx = [k.slot(), k.slot()]
            slxa = [k.slot(), k.slot()]
            slc = k.slot()

            k.dma("gpsimd", lambda g: g.dma_start(out=lng[:, :], in_=lng_i), slc)
            k.dma("gpsimd", lambda g: g.dma_start(out=lnb[:, :], in_=lnb_i), slc)
            tconsts = k.dma("gpsimd",
                            lambda g: g.dma_start(out=identb[:, :], in_=ident_i), slc)

            t_x9 = [None] * NT
            t_n1 = [None] * NT
            xacc_sts = []
            slx = [k.slot() for _ in range(4)]
            slxa = [k.slot() for _ in range(4)]
            for i in range(NT):
                sl = slice((i % 4) * D, (i % 4 + 1) * D)
                c0 = (i % 4) * 8
                dep_slot = [t_x9[i - 4]] if i >= 4 else []
                t_ld = k.dma("sync", lambda s, i=i, sl=sl: s.dma_start(
                    out=xbuf[:, sl], in_=x_i[i * 128:(i + 1) * 128, :]),
                    slx[i % 4], after=dep_slot)

                a = [t_ld] + ([tconsts] if i == 0 else [])
                t1 = k.op("vector", lambda v, sl=sl, c0=c0: v.reduce_sum(
                    cols[:, c0:c0 + 1], xbuf[:, sl], axis=AX), a)
                t2 = k.op("vector", lambda v, c0=c0: v.tensor_scalar(
                    cols[:, c0 + 1:c0 + 2], cols[:, c0:c0 + 1], 1.0 / D, None,
                    OP.mult), [t1])
                t3 = k.op("vector", lambda v, sl=sl, c0=c0: v.tensor_scalar(
                    xcm[:, sl], xbuf[:, sl], cols[:, c0 + 1:c0 + 2], None,
                    OP.subtract), [t2])
                t4 = k.op("scalar", lambda s, sl=sl, c0=c0: s.activation(
                    sq[:, :], xcm[:, sl], AF.Square,
                    accum_out=cols[:, c0 + 2:c0 + 3]), [t3])
                t5 = k.op("vector", lambda v, c0=c0: v.tensor_scalar(
                    cols[:, c0 + 3:c0 + 4], cols[:, c0 + 2:c0 + 3], 1.0 / D, EPS,
                    OP.mult, OP.add), [t4])
                t6 = k.op("scalar", lambda s, c0=c0: s.sqrt(
                    cols[:, c0 + 4:c0 + 5], cols[:, c0 + 3:c0 + 4]), [t5])
                t7 = k.op("vector", lambda v, c0=c0: v.reciprocal(
                    cols[:, c0 + 5:c0 + 6], cols[:, c0 + 4:c0 + 5]), [t6])
                t8 = k.op("vector", lambda v, c0=c0: v.tensor_scalar(
                    cols[:, c0 + 6:c0 + 7], cols[:, c0 + 3:c0 + 4], 1.0, -EPS,
                    OP.mult, OP.add), [t7])
                t9 = k.op("vector", lambda v, c0=c0: v.tensor_mul(
                    cols[:, c0 + 6:c0 + 7], cols[:, c0 + 6:c0 + 7],
                    cols[:, c0 + 5:c0 + 6]), [t8])
                t11 = k.op("vector", lambda v, c0=c0: v.scalar_tensor_tensor(
                    cols[:, c0 + 6:c0 + 7], cols[:, c0 + 6:c0 + 7], 1.0,
                    cols[:, c0 + 5:c0 + 6], OP.mult, OP.mult), [t9])
                t12 = k.op("vector", lambda v, c0=c0: v.tensor_scalar(
                    cols[:, c0 + 6:c0 + 7], cols[:, c0 + 6:c0 + 7], 1.0, EPS,
                    OP.mult, OP.add), [t11])
                t13 = k.op("scalar", lambda s, c0=c0: s.sqrt(
                    cols[:, c0 + 7:c0 + 8], cols[:, c0 + 6:c0 + 7]), [t12])
                c2 = (i % 4) * 4
                t14 = k.op("vector", lambda v, c2=c2, c0=c0: v.reciprocal(
                    cols2[:, c2:c2 + 1], cols[:, c0 + 7:c0 + 8]), [t13])
                t15 = k.op("vector", lambda v, c2=c2, c0=c0: v.tensor_mul(
                    cols2[:, c2 + 1:c2 + 2], cols2[:, c2:c2 + 1],
                    cols[:, c0 + 5:c0 + 6]), [t14])
                dep_store = [xacc_sts[i - 4]] if i >= 4 else []
                tA = k.op("vector", lambda v, sl=sl, c0=c0: v.scalar_tensor_tensor(
                    xas[:, sl], xcm[:, sl], cols[:, c0 + 5:c0 + 6], xbuf[:, sl],
                    OP.mult, OP.add), [t7] + dep_store)
                t_x9[i] = tA
                t_st = k.dma("scalar", lambda s, i=i, sl=sl: s.dma_start(
                    out=xacc_o[i * 128:(i + 1) * 128, :], in_=xas[:, sl]),
                    slxa[i % 4], after=[tA])
                xacc_sts.append(t_st)
                store_toks.append(t_st)
                tB = k.op("vector", lambda v, sl=sl, c2=c2: v.tensor_scalar(
                    xcm[:, sl], xcm[:, sl], cols2[:, c2 + 1:c2 + 2], None,
                    OP.mult), [t15, tA, t4])
                tC = k.op("vector", lambda v, sl=sl: v.tensor_mul(
                    ln1[:, sl], xcm[:, sl], lng[:, :]), [tB])
                tD = k.op("vector", lambda v, i=i, sl=sl: v.tensor_add(
                    n1tm[:, i * D:(i + 1) * D], ln1[:, sl], lnb[:, :]), [tC])
                t_n1[i] = tD
            k.flush(block)

            t_trcopy = [None] * (NT * ND)
            for i in range(NT):
                for j in range(ND):
                    b = i * ND + j
                    pst = pstA if b % 2 == 0 else pstB
                    dep = [t_n1[i]]
                    if b >= 2:
                        dep.append(t_trcopy[b - 2])
                    if b == 0:
                        dep.append(tconsts)
                    t_tr = k.op("tensor", lambda t, i=i, j=j, pst=pst: t.transpose(
                        pst[:, :], n1tm[:, i * D + j * 128:i * D + (j + 1) * 128],
                        identb[:, :]), dep)
                    t_trcopy[b] = k.op("vector", lambda v, i=i, j=j, pst=pst:
                                       v.tensor_copy(
                                           n1fm[:, j * TOK + i * 128:j * TOK + (i + 1) * 128],
                                           pst[:, :]), [t_tr])
            k.flush(block)
            bar_AB = k.bar([tuple(s) for s in (slx[0], slx[1], slxa[0],
                                               slxa[1], slc)])

        with ExitStack() as sC:
            EC = sC.enter_context
            q_tm = EC(nc.sbuf_tensor("q_tm", [128, NT * J], BF16))
            k_tm = EC(nc.sbuf_tensor("k_tm", [128, NT * J], BF16))
            with ExitStack() as sCc:
                ECc = sCc.enter_context
                wqkb = ECc(nc.sbuf_tensor("wqkb", [128, 2 * 8 * 512], BF16))
                bqk = ECc(nc.sbuf_tensor("bqk", [1, J], BF16))
                ones1 = ECc(nc.sbuf_tensor("ones1", [1, 128], BF16))
                g0b = ECc(nc.sbuf_tensor("g0b", [128, J], BF16))
                g1b = ECc(nc.sbuf_tensor("g1b", [128, J], BF16))
                b0b = ECc(nc.sbuf_tensor("b0b", [128, J], BF16))
                b1b = ECc(nc.sbuf_tensor("b1b", [128, J], BF16))
                qsf = ECc(nc.sbuf_tensor("qsf", [128, 512], F32))
                qsb = ECc(nc.sbuf_tensor("qsb", [128, 512], BF16))
                qx = ECc(nc.sbuf_tensor("qx", [128, 512], BF16))

                slcc = k.slot()
                k.dma("gpsimd", lambda g: g.dma_start(out=bqk[:, :],
                                                      in_=bqk_i[0:1, :]), slcc,
                      after=bar_AB)
                t_o1 = k.op("vector", lambda v: v.memset(ones1[:, :], 1.0))
                k.dma("gpsimd", lambda g: g.dma_start(out=g0b[:, :],
                                                      in_=osg_i[0:128, :]), slcc)
                k.dma("gpsimd", lambda g: g.dma_start(out=g1b[:, :],
                                                      in_=osg_i[128:256, :]), slcc)
                k.dma("gpsimd", lambda g: g.dma_start(out=b0b[:, :],
                                                      in_=osb_i[0:128, :]), slcc)
                tcc = k.dma("gpsimd", lambda g: g.dma_start(out=b1b[:, :],
                                                            in_=osb_i[128:256, :]), slcc)

                slw = [k.slot(), k.slot()]
                t_wl = [None] * 4
                t_mm = {}
                t_ev1 = {}
                t_ev2 = {}
                for p in range(4):
                    js = p * 512
                    wbase = (p % 2) * 4096
                    dep_w = list(bar_AB) + [t_o1] if p < 2 else [t_mm[(p - 2, NT - 1)]]
                    t_wl[p] = k.dma("gpsimd", lambda g, js=js, wbase=wbase:
                                    g.dma_start(
                                        out=wqkb[:, wbase:wbase + 4096].rearrange(
                                            "p (k j) -> p k j", k=8),
                                        in_=wqk_i[:, js:js + 512].rearrange(
                                            "(k p) j -> p k j", p=128)),
                                    slw[p % 2], after=dep_w)
                    for i in range(NT):
                        b = p * NT + i
                        ps = psA if b % 2 == 0 else psB
                        dep = [t_wl[p], t_trcopy[i * ND + ND - 1], tcc]
                        if b >= 2:
                            dep.append(t_ev1[(b - 2) // NT, (b - 2) % NT])

                        def mm(t, i=i, js=js, wbase=wbase, ps=ps):
                            for kk in range(8):
                                t.matmul(
                                    ps[:, :],
                                    n1fm[:, kk * TOK + i * 128:kk * TOK + (i + 1) * 128],
                                    wqkb[:, wbase + kk * 512:wbase + (kk + 1) * 512],
                                    start=(kk == 0), stop=False)
                            return t.matmul(ps[:, :], ones1[:, :],
                                            bqk[:, js:js + 512],
                                            start=False, stop=True)
                        t_mm[(p, i)] = k.op("tensor", mm, dep)
                        dep1 = [t_mm[(p, i)], tcc]
                        if b == 0:
                            dep1 += bar_AB
                        if b >= 1:
                            dep1.append(t_ev2[(b - 1) // NT, (b - 1) % NT])
                        t2 = k.op("scalar", lambda s, ps=ps: s.activation(
                            qsb[:, :], ps[:, :], AF.Silu), dep1)
                        t_ev1[(p, i)] = t2
                        t_ev2[(p, i)] = t2
                        t3 = k.op("vector", lambda v, js=js: v.tensor_mul(
                            qx[:, :], qsb[:, :], g0b[:, js:js + 512]), [t2])
                        t4 = k.op("vector", lambda v, i=i, js=js: v.tensor_add(
                            q_tm[:, i * J + js:i * J + js + 512], qx[:, :],
                            b0b[:, js:js + 512]), [t3])
                        t5 = k.op("vector", lambda v, js=js: v.tensor_mul(
                            qx[:, :], qsb[:, :], g1b[:, js:js + 512]), [t4])
                        k.op("vector", lambda v, i=i, js=js: v.tensor_add(
                            k_tm[:, i * J + js:i * J + js + 512], qx[:, :],
                            b1b[:, js:js + 512]), [t5])
                k.flush(block)
                bar_C = k.bar()

            simstage = EC(nc.sbuf_tensor("simstage", [128, 2 * J], BF16))
            slst = [k.slot(), k.slot()]
            t_ev = {}
            for m in range(NJ):
                sbase = (m % 2) * J
                for p in range(4):
                    b = m * 4 + p
                    ps = psA if b % 2 == 0 else psB
                    dep = list(bar_C) if b < 2 else [t_ev[b - 2]]
                    if p == 0 and m >= 2:
                        dep.append((slst[m % 2][0], slst[m % 2][1]))

                    def mm(t, m=m, p=p, ps=ps):
                        for i in range(NT):
                            ins = t.matmul(
                                ps[:, :],
                                q_tm[:, i * J + m * 128:i * J + (m + 1) * 128],
                                k_tm[:, i * J + p * 512:i * J + (p + 1) * 512],
                                start=(i == 0), stop=(i == NT - 1))
                        return ins
                    t_m = k.op("tensor", mm, dep)
                    t_ev[b] = k.op("vector", lambda v, sbase=sbase, p=p, ps=ps:
                                   v.tensor_scalar(
                                       simstage[:, sbase + p * 512:sbase + (p + 1) * 512],
                                       ps[:, :], INVN, None, OP.mult), [t_m])
                t_st = k.dma("sync", lambda s, m=m, sbase=sbase: s.dma_start(
                    out=simp_o[m * 128:(m + 1) * 128, :],
                    in_=simstage[:, sbase:sbase + J]), slst[m % 2],
                    after=[t_ev[m * 4 + 3]])
                store_toks.append(t_st)
            k.flush(block)
            bar_D = k.bar()

        with ExitStack() as sE:
            EE = sE.enter_context
            whb = EE(nc.sbuf_tensor("whb", [128, 4 * 8 * 128], BF16))
            bhid = EE(nc.sbuf_tensor("bhid", [128, F4 // 128], F32))
            hv = EE(nc.sbuf_tensor("hv", [128, J], BF16))
            hg = EE(nc.sbuf_tensor("hg", [128, J], BF16))
            vgs = EE(nc.sbuf_tensor("vgs", [128, 2 * J], BF16))

            slb = k.slot()
            t_bh = k.dma("gpsimd", lambda g: g.dma_start(out=bhid[:, :], in_=bhid_i),
                         slb, after=bar_D)
            slwh = [k.slot() for _ in range(4)]
            slvg = [k.slot(), k.slot()]
            t_hev_flat = []
            t_mm_flat = []
            t_vg = [None] * NJ
            order = 0
            for mi in range(NJ):
                for half, m in enumerate((mi, mi + 16)):
                    tgt = hv if half == 0 else hg
                    wbase = (order % 4) * 1024
                    dep_w = list(bar_D) if order < 4 else [t_mm_flat[(order - 4) * 4 + 3]]
                    t_w = k.dma("gpsimd", lambda g, m=m, wbase=wbase: g.dma_start(
                        out=whb[:, wbase:wbase + 1024].rearrange(
                            "p (k f) -> p k f", k=8),
                        in_=whid_i[:, m * 128:(m + 1) * 128].rearrange(
                            "(k p) f -> p k f", p=128)),
                        slwh[order % 4], after=dep_w)
                    for p in range(4):
                        b = order * 4 + p
                        ps = psA if b % 2 == 0 else psB
                        dep = [t_w, t_bh] if b < 2 else [t_w, t_hev_flat[b - 2]]
                        if b == 0:
                            dep += bar_D

                        def mmh(t, p=p, ps=ps, wbase=wbase):
                            for kk in range(8):
                                ins = t.matmul(
                                    ps[:, :],
                                    whb[:, wbase + kk * 128:wbase + (kk + 1) * 128],
                                    n1fm[:, kk * TOK + p * 512:kk * TOK + (p + 1) * 512],
                                    start=(kk == 0), stop=(kk == 7))
                            return ins
                        t_m = k.op("tensor", mmh, dep)
                        t_mm_flat.append(t_m)
                        dep_e = [t_m]
                        if p == 0 and half == 0 and mi >= 1:
                            dep_e.append(t_vg[mi - 1])
                        t_e = k.op("scalar", lambda s, m=m, p=p, ps=ps, tgt=tgt:
                                   s.activation(tgt[:, p * 512:(p + 1) * 512],
                                                ps[:, :], AF.Silu,
                                                bias=bhid[:, m:m + 1]), dep_e)
                        t_hev_flat.append(t_e)
                    order += 1
                vbase = (mi % 2) * J
                dep_v = [t_hev_flat[-1], t_hev_flat[-5]]
                if mi >= 2:
                    dep_v.append((slvg[mi % 2][0], slvg[mi % 2][1]))
                t_vg[mi] = k.op("vector", lambda v, vbase=vbase: v.tensor_mul(
                    vgs[:, vbase:vbase + J], hv[:, :], hg[:, :]), dep_v)
                t_st = k.dma("sync", lambda s, mi=mi, vbase=vbase: s.dma_start(
                    out=vg_o[mi * 128:(mi + 1) * 128, :],
                    in_=vgs[:, vbase:vbase + J]), slvg[mi % 2], after=[t_vg[mi]])
                store_toks.append(t_st)
            k.flush(block)
            k.final_wait(block, "sync", store_toks)

    return nc


def l1_inputs(x_slice, inp):
    rep = lambda a: np.ascontiguousarray(
        np.broadcast_to(np.asarray(a, np.float32)[None, :], (128, a.shape[-1])))
    osg = np.asarray(inp["L0_attn_os_gamma"], np.float32)
    osb = np.asarray(inp["L0_attn_os_beta"], np.float32)
    osg_r = np.concatenate([np.broadcast_to(osg[0][None, :], (128, J)),
                            np.broadcast_to(osg[1][None, :], (128, J))], 0)
    osb_r = np.concatenate([np.broadcast_to(osb[0][None, :], (128, J)),
                            np.broadcast_to(osb[1][None, :], (128, J))], 0)
    return {
        "x_sl": np.ascontiguousarray(x_slice, dtype=np.float32),
        "lng_r": rep(inp["L0_attn_ln_g"]),
        "lnb_r": rep(inp["L0_attn_ln_b"]),
        "Wqk": np.ascontiguousarray(inp["L0_attn_W_qk"], np.float32),
        "bqk_r": rep(inp["L0_attn_b_qk"]),
        "osg_r": np.ascontiguousarray(osg_r, np.float32),
        "osb_r": np.ascontiguousarray(osb_r, np.float32),
        "Whid": np.ascontiguousarray(inp["L0_attn_W_hidden"], np.float32),
        "bhid_c": np.ascontiguousarray(
            np.asarray(inp["L0_attn_b_hidden"], np.float32).reshape(F4 // 128, 128).T),
        "ident": np.eye(128, dtype=np.float32),
    }


def build_l2():
    nc = _bass()
    simp4_i = nc.dram_tensor("simp4", [4 * J, J], BF16, kind="ExternalInput").ap()
    vg_i = nc.dram_tensor("vg_i", [J, TOK], BF16, kind="ExternalInput").ap()
    xacc_i = nc.dram_tensor("xacc_i", [TOK, D], F32, kind="ExternalInput").ap()
    wout_i = nc.dram_tensor("Wout", [J, D], F32, kind="ExternalInput").ap()
    bout_i = nc.dram_tensor("bout_r", [1, D], F32, kind="ExternalInput").ap()
    wg_i = nc.dram_tensor("Wg", [D, 8], F32, kind="ExternalInput").ap()
    bg_i = nc.dram_tensor("bg_r", [128, 8], F32, kind="ExternalInput").ap()
    ident_i = nc.dram_tensor("ident", [128, 128], F32, kind="ExternalInput").ap()
    tokb_i = nc.dram_tensor("tokbase_r", [128, 1], F32, kind="ExternalInput").ap()
    w1e_i = nc.dram_tensor("W1e", [D, F4], F32, kind="ExternalInput").ap()
    w2e_i = nc.dram_tensor("W2e", [F4, D], F32, kind="ExternalInput").ap()

    xmid_o = nc.dram_tensor("xmid_o", [TOK, D], F32, kind="ExternalOutput").ap()
    n2_o = nc.dram_tensor("n2_o", [TOK, D], BF16, kind="ExternalOutput").ap()
    route_o = nc.dram_tensor("route_o", [128, 96], F32, kind="ExternalOutput").ap()
    counts_o = nc.dram_tensor("counts_o", [8, 2], F32, kind="ExternalOutput").ap()
    w1c_o = nc.dram_tensor("w1c_o", [D, F4], BF16, kind="ExternalOutput").ap()
    w2c_o = nc.dram_tensor("w2c_o", [F4, D], BF16, kind="ExternalOutput").ap()
    T_o = nc.dram_tensor("T_o", [16, TOK], F32, kind="ExternalOutput").ap()
    T_flat = T_o.rearrange("a b -> (a b)")[:, None]

    with ExitStack() as ctx:
        E = ctx.enter_context
        k = K(nc, ctx)
        block = E(nc.Block())
        identb = E(nc.sbuf_tensor("identb", [128, 128], BF16))
        identf = E(nc.sbuf_tensor("identf", [128, 128], F32))
        scols = E(nc.sbuf_tensor("scols", [128, 16], F32))
        cols = E(nc.sbuf_tensor("cols", [128, 16], F32))
        n2tm = E(nc.sbuf_tensor("n2tm", [128, NT * D], BF16))
        psA = E(nc.psum_tensor("psA", [128, 512], F32))
        psB = E(nc.psum_tensor("psB", [128, 512], F32))
        store_toks = []

        tid = k.dma("gpsimd", lambda g: g.dma_start(out=identb[:, :], in_=ident_i),
                    k.slot())
        tidf = k.dma("sync", lambda s: s.dma_start(out=identf[:, :], in_=ident_i),
                     k.slot())

        with ExitStack() as s1:
            E1 = s1.enter_context
            sbuf4 = [E1(nc.sbuf_tensor(f"pp{g}", [128, 8 * J], BF16)) for g in range(4)]
            acc = E1(nc.sbuf_tensor("acc", [128, 2 * J], BF16))
            exps = E1(nc.sbuf_tensor("exps", [128, 2 * J], F32))
            sll = [k.slot() for _ in range(8)]
            t_lds = {}
            t_sdone = [None] * NJ
            t_exp = [None] * NJ
            for m in range(NJ):
                q4 = (m % 8) * J
                h2 = (m % 2) * J
                for g in range(4):
                    dep = [t_sdone[m - 8]] if m >= 8 else []
                    eng = "sync" if g % 2 == 0 else "gpsimd"
                    t_lds[(m, g)] = k.dma(eng, lambda s, m=m, g=g, q4=q4:
                        s.dma_start(out=sbuf4[g][:, q4:q4 + J],
                                    in_=simp4_i[g * J + m * 128:g * J + (m + 1) * 128, :]),
                        sll[g * 2 + m % 2], after=dep)
                dep_a = [t_lds[(m, 0)], t_lds[(m, 1)]]
                if m >= 2:
                    dep_a.append(t_exp[m - 2])
                ta = k.op("vector", lambda v, q4=q4, h2=h2: v.tensor_add(
                    acc[:, h2:h2 + J], sbuf4[0][:, q4:q4 + J],
                    sbuf4[1][:, q4:q4 + J]), dep_a)
                tb = k.op("vector", lambda v, q4=q4, h2=h2: v.tensor_add(
                    acc[:, h2:h2 + J], acc[:, h2:h2 + J], sbuf4[2][:, q4:q4 + J]),
                    [ta, t_lds[(m, 2)]])
                tc = k.op("vector", lambda v, q4=q4, h2=h2: v.tensor_add(
                    acc[:, h2:h2 + J], acc[:, h2:h2 + J], sbuf4[3][:, q4:q4 + J]),
                    [tb, t_lds[(m, 3)]])
                c0 = (m % 2) * 4
                dep_e = [tc]
                if m >= 2:
                    dep_e.append(t_sdone[m - 2])
                tex = k.op("scalar", lambda s, h2=h2, c0=c0: s.activation(
                    exps[:, h2:h2 + J], acc[:, h2:h2 + J], AF.Exp,
                    accum_out=cols[:, c0 + 1:c0 + 2]), dep_e)
                t_exp[m] = tex
                trc = k.op("vector", lambda v, c0=c0: v.reciprocal(
                    cols[:, c0 + 2:c0 + 3], cols[:, c0 + 1:c0 + 2]), [tex])
                t_sdone[m] = k.op("scalar", lambda s, m=m, h2=h2, c0=c0:
                    s.activation(
                        exps[:, h2:h2 + J], exps[:, h2:h2 + J], AF.Copy,
                        scale=cols[:, c0 + 2:c0 + 3],
                        accum_out=scols[:, m:m + 1]), [trc])
            k.flush(block)
            bar1 = k.bar([tuple(s) for s in sll])

        t_n2 = [None] * NT
        with ExitStack() as s2:
            E2 = s2.enter_context
            vgs = E2(nc.sbuf_tensor("vgs", [128, NJ * TOK], BF16))
            woutb = [E2(nc.sbuf_tensor(f"woutb{h}", [128, NJ * 512], BF16))
                     for h in range(2)]
            xaccb = E2(nc.sbuf_tensor("xaccb", [128, 2 * D], F32))
            xmid = E2(nc.sbuf_tensor("xmid", [128, 2 * D], F32))
            sqq = E2(nc.sbuf_tensor("sqq", [128, 2 * D], F32))
            sq2 = E2(nc.sbuf_tensor("sq2", [128, D], F32))

            slvg = [k.slot(), k.slot()]
            slwo = [k.slot(), k.slot()]
            slxa = [k.slot(), k.slot()]
            slxm = [k.slot(), k.slot()]
            sln2 = [k.slot(), k.slot()]
            t_wo = [None, None]
            for h in range(2):
                t_wo[h] = k.dma("gpsimd", lambda g, h=h: g.dma_start(
                    out=woutb[h][:, :].rearrange("p (k d) -> p k d", k=NJ),
                    in_=wout_i[:, h * 512:(h + 1) * 512].rearrange(
                        "(k p) d -> p k d", p=128)), slwo[h],
                    after=(bar1 if h == 0 else []))
            boutb = E2(nc.sbuf_tensor("boutb", [1, D], BF16))
            ones1b = E2(nc.sbuf_tensor("ones1b", [1, 128], BF16))
            t_bo = k.dma("gpsimd", lambda g: g.dma_start(out=boutb[:, :],
                                                         in_=bout_i), k.slot(),
                         after=bar1)
            t_o1b = k.op("vector", lambda v: v.memset(ones1b[:, :], 1.0))
            t_vgs = [None] * NJ
            for m in range(NJ):
                dep = list(bar1) if m < 2 else [t_vgs[m - 2]]
                t_l = k.dma("sync", lambda s, m=m: s.dma_start(
                    out=vgs[:, m * TOK:(m + 1) * TOK],
                    in_=vg_i[m * 128:(m + 1) * 128, :]), slvg[m % 2], after=dep)
                t_vgs[m] = k.op("vector", lambda v, m=m: v.tensor_scalar(
                    vgs[:, m * TOK:(m + 1) * TOK], vgs[:, m * TOK:(m + 1) * TOK],
                    scols[:, m:m + 1], None, OP.mult), [t_l])
            t_evx = [None] * (NT * 2)
            for t in range(NT):
                half = (t % 2) * D
                t_xl = k.dma("gpsimd", lambda s, t=t, half=half: s.dma_start(
                    out=xaccb[:, half:half + D],
                    in_=xacc_i[t * 128:(t + 1) * 128, :]), slxa[t % 2],
                    after=([t_n2[t - 2]] if t >= 2 else []))
                for dp in range(2):
                    b = t * 2 + dp
                    ps = psA if b % 2 == 0 else psB
                    dep = [t_wo[dp], t_vgs[7], t_bo, t_o1b]
                    if b >= 2:
                        dep.append(t_evx[b - 2])
                    if t >= 2 and dp == 0:
                        dep.append((slxm[t % 2][0], slxm[t % 2][1]))

                    def mmA(tt, t=t, ps=ps, dp=dp):
                        for kk in range(8):
                            ins = tt.matmul(
                                ps[:, :],
                                vgs[:, kk * TOK + t * 128:kk * TOK + (t + 1) * 128],
                                woutb[dp][:, kk * 512:(kk + 1) * 512],
                                start=(kk == 0), stop=False)
                        return ins

                    def mmB(tt, t=t, ps=ps, dp=dp):
                        for kk in range(8, NJ):
                            tt.matmul(
                                ps[:, :],
                                vgs[:, kk * TOK + t * 128:kk * TOK + (t + 1) * 128],
                                woutb[dp][:, kk * 512:(kk + 1) * 512],
                                start=False, stop=False)
                        return tt.matmul(
                            ps[:, :], ones1b[:, :],
                            boutb[0:1, dp * 512:(dp + 1) * 512],
                            start=False, stop=True)
                    k.op("tensor", mmA, dep)
                    t_m = k.op("tensor", mmB, [t_vgs[NJ - 1]])
                    t_evx[b] = k.op("vector", lambda v, half=half, dp=dp, ps=ps:
                        v.tensor_add(xmid[:, half + dp * 512:half + (dp + 1) * 512],
                                     ps[:, :],
                                     xaccb[:, half + dp * 512:half + (dp + 1) * 512]),
                        [t_m, t_xl])
                t_stx = k.dma("scalar", lambda s, t=t, half=half: s.dma_start(
                    out=xmid_o[t * 128:(t + 1) * 128, :], in_=xmid[:, half:half + D]),
                    slxm[t % 2], after=[t_evx[t * 2], t_evx[t * 2 + 1]])
                store_toks.append(t_stx)
                c4 = 4 + (t % 2) * 6
                t1 = k.op("vector", lambda v, half=half, c4=c4: v.reduce_sum(
                    cols[:, c4:c4 + 1], xmid[:, half:half + D], axis=AX),
                    [t_evx[t * 2], t_evx[t * 2 + 1]])
                t2 = k.op("vector", lambda v, c4=c4: v.tensor_scalar(
                    cols[:, c4 + 1:c4 + 2], cols[:, c4:c4 + 1], 1.0 / D, None,
                    OP.mult), [t1])
                t3 = k.op("vector", lambda v, half=half, c4=c4: v.tensor_scalar(
                    sqq[:, half:half + D], xmid[:, half:half + D],
                    cols[:, c4 + 1:c4 + 2], None, OP.subtract), [t2])
                t4 = k.op("scalar", lambda s, half=half, c4=c4: s.activation(
                    sq2[:, :], sqq[:, half:half + D], AF.Square,
                    accum_out=cols[:, c4 + 2:c4 + 3]), [t3])
                t5 = k.op("vector", lambda v, c4=c4: v.tensor_scalar(
                    cols[:, c4 + 3:c4 + 4], cols[:, c4 + 2:c4 + 3], 1.0 / D, EPS,
                    OP.mult, OP.add), [t4])
                t6 = k.op("scalar", lambda s, c4=c4: s.sqrt(
                    cols[:, c4 + 4:c4 + 5], cols[:, c4 + 3:c4 + 4]), [t5])
                t7 = k.op("vector", lambda v, c4=c4: v.reciprocal(
                    cols[:, c4 + 5:c4 + 6], cols[:, c4 + 4:c4 + 5]), [t6])
                t9 = k.op("vector", lambda v, t=t, half=half, c4=c4: v.tensor_scalar(
                    n2tm[:, t * D:(t + 1) * D], sqq[:, half:half + D],
                    cols[:, c4 + 5:c4 + 6], None, OP.mult), [t7])
                t_n2[t] = t9
                t_st2 = k.dma("scalar", lambda s, t=t: s.dma_start(
                    out=n2_o[t * 128:(t + 1) * 128, :],
                    in_=n2tm[:, t * D:(t + 1) * D]), sln2[t % 2], after=[t9])
                store_toks.append(t_st2)
            k.flush(block)
            bar2 = k.bar()

        with ExitStack() as s3:
            E3 = s3.enter_context
            n2fm = E3(nc.sbuf_tensor("n2fm", [128, ND * TOK], BF16))
            pstA = E3(nc.psum_tensor("pstA", [128, 128], BF16))
            pstB = E3(nc.psum_tensor("pstB", [128, 128], BF16))
            psTA = E3(nc.psum_tensor("psTA", [128, 128], F32))
            psTB = E3(nc.psum_tensor("psTB", [128, 128], F32))
            psC0 = E3(nc.psum_tensor("psC0", [8, 1], F32))
            psC1 = E3(nc.psum_tensor("psC1", [8, 1], F32))

            t_trc = [None] * (NT * ND)
            for i in range(NT):
                for jj in range(ND):
                    b = i * ND + jj
                    pst = pstA if b % 2 == 0 else pstB
                    dep = [t_n2[i]]
                    if b == 0:
                        dep.append(tid)
                    if b >= 2:
                        dep.append(t_trc[b - 2])
                    t_tr = k.op("tensor", lambda t, i=i, jj=jj, pst=pst: t.transpose(
                        pst[:, :], n2tm[:, i * D + jj * 128:i * D + (jj + 1) * 128],
                        identb[:, :]), dep)
                    t_trc[b] = k.op("vector", lambda v, i=i, jj=jj, pst=pst:
                        v.tensor_copy(
                            n2fm[:, jj * TOK + i * 128:jj * TOK + (i + 1) * 128],
                            pst[:, :]), [t_tr])
            k.flush(block)

            wgb = E3(nc.sbuf_tensor("wgb", [128, ND * 8], BF16))
            bgr = E3(nc.sbuf_tensor("bgr", [128, 8], F32))
            tokb = E3(nc.sbuf_tensor("tokb", [128, 1], F32))
            lg = E3(nc.sbuf_tensor("lg", [128, NT * 8], F32))
            iota8 = E3(nc.sbuf_tensor("iota8", [128, 8], F32))
            iotac = E3(nc.sbuf_tensor("iotac", [128, 1], F32))
            fld = E3(nc.sbuf_tensor("fld", [128, 6 * NT], F32))
            s8a = E3(nc.sbuf_tensor("s8a", [128, 8], F32))
            s8b = E3(nc.sbuf_tensor("s8b", [128, 8], F32))
            s8c = E3(nc.sbuf_tensor("s8c", [128, 8], F32))
            s8d = E3(nc.sbuf_tensor("s8d", [128, 8], F32))
            m0 = E3(nc.sbuf_tensor("m0", [128, NT * 8], F32))
            m1t = E3(nc.sbuf_tensor("m1t", [128, NT * 8], F32))
            m0f = E3(nc.sbuf_tensor("m0f", [8, J], F32))
            m1f = E3(nc.sbuf_tensor("m1f", [8, J], F32))
            ex0 = E3(nc.sbuf_tensor("ex0", [8, J], F32))
            ex1 = E3(nc.sbuf_tensor("ex1", [8, J], F32))
            ones8 = E3(nc.sbuf_tensor("ones8", [8, J], F32))
            onescol = E3(nc.sbuf_tensor("onescol", [128, 1], F32))
            ctile = E3(nc.sbuf_tensor("ctile", [8, 2], F32))
            zrow = E3(nc.sbuf_tensor("zrow", [1, J], F32))
            ivals = E3(nc.sbuf_tensor("ivals", [128, 2], F32))
            itgtf = E3(nc.sbuf_tensor("itgtf", [128, 4], F32))
            itgt = E3(nc.sbuf_tensor("itgt", [128, 4], I32))

            t_wg = k.dma("gpsimd", lambda g: g.dma_start(
                out=wgb[:, :].rearrange("p (k e) -> p k e", k=ND),
                in_=wg_i.rearrange("(k p) e -> p k e", p=128)), k.slot(), after=bar2)
            t_bg = k.dma("gpsimd", lambda g: g.dma_start(out=bgr[:, :], in_=bg_i),
                         k.slot(), after=bar2)
            t_tb = k.dma("gpsimd", lambda g: g.dma_start(out=tokb[:, :], in_=tokb_i),
                         k.slot(), after=bar2)
            slwc = k.slot()
            for kk in range(8):
                k.dma("gpsimd", lambda g, kk=kk: g.dma_start(
                    out=w1c_o[kk * 128:(kk + 1) * 128, :],
                    in_=w1e_i[kk * 128:(kk + 1) * 128, :]), slwc)
                k.dma("gpsimd", lambda g, kk=kk: g.dma_start(
                    out=w2c_o[kk * 512:(kk + 1) * 512, :],
                    in_=w2e_i[kk * 512:(kk + 1) * 512, :]), slwc)
            store_toks.append((slwc[0], 16 * 16))
            t_io = k.op("gpsimd", lambda g: g.iota(
                iota8[:, :], pattern=[[1, 8]], base=0, channel_multiplier=0,
                allow_small_or_imprecise_dtypes=True))
            t_ic = k.op("gpsimd", lambda g: g.iota(
                iotac[:, :], pattern=[[0, 1]], base=1, channel_multiplier=1,
                allow_small_or_imprecise_dtypes=True))
            t_on = k.op("vector", lambda v: v.memset(onescol[:, :], 1.0))
            t_o8 = k.op("vector", lambda v: v.memset(ones8[:, :], 1.0))
            t_zr = k.op("vector", lambda v: v.memset(zrow[:, :], 0.0))
            slz = k.slot()
            for r in range(16):
                k.dma("sync", lambda s, r=r: s.dma_start(
                    out=T_o[r:r + 1, :], in_=zrow[:, :]), slz,
                    after=[t_zr] if r == 0 else [])
            t_zall = (slz[0], slz[1])

            t_lg = [None] * NT
            for t in range(NT):
                ps = psA if t % 2 == 0 else psB
                dep = [t_wg, t_trc[NT * ND - 1]]
                if t >= 2:
                    dep.append(t_lg[t - 2])

                def mmg(tt, t=t, ps=ps):
                    for kk in range(ND):
                        ins = tt.matmul(
                            ps[:, 0:8],
                            n2fm[:, kk * TOK + t * 128:kk * TOK + (t + 1) * 128],
                            wgb[:, kk * 8:(kk + 1) * 8],
                            start=(kk == 0), stop=(kk == ND - 1))
                    return ins
                t_m = k.op("tensor", mmg, dep)
                t_lg[t] = k.op("vector", lambda v, t=t, ps=ps: v.tensor_add(
                    lg[:, t * 8:(t + 1) * 8], ps[:, 0:8], bgr[:, :]),
                    [t_m, t_bg])

            t_m0w = [None] * NT
            t_m1w = [None] * NT
            t_gts = [None] * NT
            for t in range(NT):
                lgt = lg[:, t * 8:(t + 1) * 8]
                a1 = k.op("vector", lambda v, lgt=lgt: v.reduce_max(
                    cols[:, 0:1], lgt, axis=AX), [t_lg[t], t_io, t_on])
                a2 = k.op("vector", lambda v, lgt=lgt: v.tensor_scalar(
                    s8a[:, :], lgt, cols[:, 0:1], None, OP.is_equal), [a1])
                a3 = k.op("vector", lambda v: v.tensor_mul(s8b[:, :], s8a[:, :],
                                                           iota8[:, :]), [a2])
                a4 = k.op("vector", lambda v: v.tensor_scalar(
                    s8c[:, :], s8a[:, :], 9.0, None, OP.mult), [a3])
                a5 = k.op("vector", lambda v: v.tensor_sub(s8b[:, :], s8b[:, :],
                                                           s8c[:, :]), [a4])
                a6 = k.op("vector", lambda v: v.tensor_scalar(
                    s8b[:, :], s8b[:, :], 9.0, None, OP.add), [a5])
                a7 = k.op("vector", lambda v, t=t: v.tensor_reduce(
                    fld[:, 0 * NT + t:0 * NT + t + 1], s8b[:, :], AX, OP.min), [a6])
                a8 = k.op("vector", lambda v, t=t: v.tensor_scalar(
                    s8c[:, :], iota8[:, :], fld[:, 0 * NT + t:0 * NT + t + 1],
                    None, OP.is_equal), [a7])
                a9 = k.op("vector", lambda v: v.tensor_scalar(
                    s8c[:, :], s8c[:, :], 1e9, None, OP.mult), [a8])
                a10 = k.op("vector", lambda v, lgt=lgt: v.tensor_sub(
                    s8d[:, :], lgt, s8c[:, :]), [a9])
                a11 = k.op("vector", lambda v: v.reduce_max(
                    cols[:, 1:2], s8d[:, :], axis=AX), [a10])
                a12 = k.op("vector", lambda v: v.tensor_scalar(
                    s8c[:, :], s8d[:, :], cols[:, 1:2], None, OP.is_equal), [a11])
                a13 = k.op("vector", lambda v: v.tensor_mul(s8d[:, :], s8c[:, :],
                                                            iota8[:, :]), [a12])
                a14 = k.op("vector", lambda v: v.tensor_scalar(
                    s8c[:, :], s8c[:, :], 9.0, None, OP.mult), [a13])
                a15 = k.op("vector", lambda v: v.tensor_sub(s8d[:, :], s8d[:, :],
                                                            s8c[:, :]), [a14])
                a16 = k.op("vector", lambda v: v.tensor_scalar(
                    s8d[:, :], s8d[:, :], 9.0, None, OP.add), [a15])
                a17 = k.op("vector", lambda v, t=t: v.tensor_reduce(
                    fld[:, 1 * NT + t:1 * NT + t + 1], s8d[:, :], AX, OP.min), [a16])
                a18 = k.op("vector", lambda v: v.tensor_sub(
                    cols[:, 2:3], cols[:, 0:1], cols[:, 1:2]), [a17, a11])
                a19 = k.op("scalar", lambda s, t=t: s.activation(
                    fld[:, 4 * NT + t:4 * NT + t + 1], cols[:, 2:3], AF.Sigmoid),
                    [a18])
                a20 = k.op("vector", lambda v, t=t: v.tensor_scalar(
                    fld[:, 5 * NT + t:5 * NT + t + 1],
                    fld[:, 4 * NT + t:4 * NT + t + 1], -1.0, 1.0,
                    OP.mult, OP.add), [a19])
                t_gts[t] = a20
                t_m0w[t] = k.op("vector", lambda v, t=t: v.tensor_scalar(
                    m0[:, t * 8:(t + 1) * 8], iota8[:, :],
                    fld[:, 0 * NT + t:0 * NT + t + 1], None, OP.is_equal), [a7])
                t_m1w[t] = k.op("vector", lambda v, t=t: v.tensor_scalar(
                    m1t[:, t * 8:(t + 1) * 8], iota8[:, :],
                    fld[:, 1 * NT + t:1 * NT + t + 1], None, OP.is_equal), [a17])
            k.flush(block)

            t_cnt0 = None
            t_cnt1 = None

            def mmc0(tt):
                for t in range(NT):
                    ins = tt.matmul(psC0[:, :], m0[:, t * 8:(t + 1) * 8],
                                    onescol[:, :], start=(t == 0),
                                    stop=(t == NT - 1))
                return ins

            def mmc1(tt):
                for t in range(NT):
                    ins = tt.matmul(psC1[:, :], m1t[:, t * 8:(t + 1) * 8],
                                    onescol[:, :], start=(t == 0),
                                    stop=(t == NT - 1))
                return ins
            t_cnt0 = k.op("tensor", mmc0, t_m0w + [t_on])
            t_cnt1 = k.op("tensor", mmc1, t_m1w)
            tc0 = k.op("vector", lambda v: v.tensor_copy(ctile[:, 0:1], psC0[:, :]),
                       [t_cnt0])
            tc1 = k.op("vector", lambda v: v.tensor_copy(ctile[:, 1:2], psC1[:, :]),
                       [t_cnt1])
            t_stc = k.dma("sync", lambda s: s.dma_start(out=counts_o, in_=ctile[:, :]),
                          k.slot(), after=[tc0, tc1])
            store_toks.append(t_stc)

            t_mf = [None] * (2 * NT)
            for t in range(NT):
                for kk, (src, dst) in enumerate(((m0, m0f), (m1t, m1f))):
                    b = 2 * t + kk
                    pst = psTA if b % 2 == 0 else psTB
                    dep = [t_m0w[t] if kk == 0 else t_m1w[t], tidf]
                    if b >= 2:
                        dep.append(t_mf[b - 2])
                    t_tr = k.op("tensor", lambda tt, t=t, src=src, pst=pst:
                        tt.transpose(pst[0:8, 0:128], src[:, t * 8:(t + 1) * 8],
                                     identf[:, :]), dep)
                    t_mf[b] = k.op("vector", lambda v, t=t, dst=dst, pst=pst:
                        v.tensor_copy(dst[:, t * 128:(t + 1) * 128],
                                      pst[0:8, 0:128]), [t_tr])
            t_sc0 = k.op("vector", lambda v: v.tensor_tensor_scan(
                ex0[:, :], ones8[:, :], m0f[:, :], 0.0, OP.mult, OP.add),
                [t_mf[2 * NT - 2], t_mf[2 * NT - 1], t_o8])
            t_sc0b = k.op("vector", lambda v: v.tensor_sub(ex0[:, :], ex0[:, :],
                                                           m0f[:, :]), [t_sc0])
            t_sc1 = k.op("vector", lambda v: v.tensor_tensor_scan(
                ex1[:, :], ones8[:, :], m1f[:, :], 0.0, OP.mult, OP.add), [t_sc0b])
            t_sc1b = k.op("vector", lambda v: v.tensor_sub(ex1[:, :], ex1[:, :],
                                                           m1f[:, :]), [t_sc1])
            k.flush(block)

            slsc = [k.slot(), k.slot()]
            t_lpd = [None] * (2 * NT)
            for t in range(NT):
                for kk in range(2):
                    b = 2 * t + kk
                    exf = ex0 if kk == 0 else ex1
                    msk = m0 if kk == 0 else m1t
                    pst = psTA if b % 2 == 0 else psTB
                    dep = [t_sc0b if kk == 0 else t_sc1b]
                    if b >= 2:
                        dep.append(t_lpd[b - 2])
                    t_tr = k.op("tensor", lambda tt, t=t, exf=exf, pst=pst:
                        tt.transpose(pst[0:128, 0:8], exf[:, t * 128:(t + 1) * 128],
                                     identf[0:8, 0:8]), dep)
                    t_cp = k.op("vector", lambda v, pst=pst: v.tensor_copy(
                        s8a[:, :], pst[0:128, 0:8]), [t_tr])
                    t_mu = k.op("vector", lambda v, t=t, msk=msk: v.tensor_mul(
                        s8b[:, :], msk[:, t * 8:(t + 1) * 8], s8a[:, :]), [t_cp])
                    t_lp = k.op("vector", lambda v, t=t, kk=kk: v.reduce_sum(
                        fld[:, (2 + kk) * NT + t:(2 + kk) * NT + t + 1],
                        s8b[:, :], axis=AX), [t_mu])
                    t_lpd[b] = t_lp
                    c0 = 2 * (t % 2) + kk if False else (t % 2)
                    tv = k.op("vector", lambda v, t=t, c0=c0: v.scalar_tensor_tensor(
                        ivals[:, c0:c0 + 1], iotac[:, :], float(t * 128),
                        tokb[:, :], OP.add, OP.add),
                        [t_lp, t_ic, t_tb,
                         (slsc[t % 2][0], slsc[t % 2][1])])
                    cc = 2 * (t % 2) + kk
                    tg1 = k.op("vector", lambda v, t=t, kk=kk, cc=cc:
                        v.scalar_tensor_tensor(
                            itgtf[:, cc:cc + 1], fld[:, kk * NT + t:kk * NT + t + 1],
                            2048.0, fld[:, (2 + kk) * NT + t:(2 + kk) * NT + t + 1],
                            OP.mult, OP.add), [t_lp])
                    tg2 = k.op("vector", lambda v, kk=kk, cc=cc: v.tensor_scalar(
                        itgtf[:, cc:cc + 1], itgtf[:, cc:cc + 1],
                        float(kk * 16384), None, OP.add), [tg1])
                    tg3 = k.op("vector", lambda v, cc=cc: v.tensor_copy(
                        itgt[:, cc:cc + 1], itgtf[:, cc:cc + 1]), [tg2])
                    t_sct = k.dma("gpsimd", lambda g, c0=c0, cc=cc:
                        g.indirect_dma_start(
                            T_flat, IndirectOffsetOnAxis(itgt[:, cc:cc + 1], 0),
                            ivals[:, c0:c0 + 1], None),
                        slsc[t % 2], after=[tv, tg3, t_zall])
                    store_toks.append(t_sct)
            slr = k.slot()
            for f in range(6):
                t_r = k.dma("sync", lambda s, f=f: s.dma_start(
                    out=route_o[:, f * NT:(f + 1) * NT],
                    in_=fld[:, f * NT:(f + 1) * NT]), slr,
                    after=[t_lpd[2 * NT - 1], t_gts[NT - 1]])
                store_toks.append(t_r)
            k.flush(block)
        k.final_wait(block, "sync", store_toks)
    return nc


def l2_inputs(simp4, vg, xacc, inp, core):
    rep = lambda a: np.ascontiguousarray(
        np.broadcast_to(np.asarray(a, np.float32)[None, :], (128, a.shape[-1])))
    return {
        "simp4": np.ascontiguousarray(simp4.reshape(4 * J, J), BF),
        "vg_i": np.ascontiguousarray(vg, BF),
        "xacc_i": np.ascontiguousarray(xacc, np.float32),
        "Wout": np.ascontiguousarray(inp["L0_attn_W_out"], np.float32),
        "bout_r": np.ascontiguousarray(
            np.asarray(inp["L0_attn_b_out"], np.float32)[None, :]),
        "Wg": np.ascontiguousarray(inp["L0_ffn_Wg"], np.float32),
        "bg_r": rep(inp["L0_ffn_bg"]),
        "ident": np.eye(128, dtype=np.float32),
        "tokbase_r": np.full((128, 1), core * TOK, np.float32),
        "W1e": np.ascontiguousarray(np.asarray(inp["L0_ffn_W1"], np.float32)[core]),
        "W2e": np.ascontiguousarray(np.asarray(inp["L0_ffn_W2"], np.float32)[core]),
    }


NPAN = 10


def build_l3(debug=False):
    nc = _bass()
    n2g_i = nc.dram_tensor("n2glob", [NCORES * TOK + 128, D], BF16,
                           kind="ExternalInput").ap()
    tmy_i = nc.dram_tensor("Tmy", [18, TOK], F32, kind="ExternalInput").ap()
    tmy_flat = tmy_i.rearrange("a b -> (a b)")[:, None]
    cnt_i = nc.dram_tensor("cntrow", [1, 16], F32, kind="ExternalInput").ap()
    w1_i = nc.dram_tensor("W1", [D, F4], BF16, kind="ExternalInput").ap()
    b1_i = nc.dram_tensor("b1_c", [128, F4 // 128], F32, kind="ExternalInput").ap()
    w2_i = nc.dram_tensor("W2", [F4, D], BF16, kind="ExternalInput").ap()
    b2_i = nc.dram_tensor("b2_r", [128, D], F32, kind="ExternalInput").ap()

    y_o = nc.dram_tensor("y_o", [CAPP, D], BF16, kind="ExternalOutput").ap()
    ident_i3 = nc.dram_tensor("ident", [128, 128], F32, kind="ExternalInput").ap()

    with ExitStack() as ctx:
        E = ctx.enter_context
        k = K(nc, ctx)
        block = E(nc.Block())
        w1b = E(nc.sbuf_tensor("w1b", [128, ND * F4], BF16))
        w2b = E(nc.sbuf_tensor("w2b", [128, 32 * D], BF16))
        hpan = E(nc.sbuf_tensor("hpan", [128, 32 * 512], BF16))
        buffm = E(nc.sbuf_tensor("buffm", [128, 2 * ND * 512], BF16))
        b1c = E(nc.sbuf_tensor("b1c", [128, F4 // 128], F32))
        b2r = E(nc.sbuf_tensor("b2r", [128, D], F32))
        ystage = E(nc.sbuf_tensor("ystage", [128, 2 * D], BF16))
        crow = E(nc.sbuf_tensor("crow", [1, 16], F32))
        ones16 = E(nc.sbuf_tensor("ones16", [1, 16], F32))
        ones1r = E(nc.sbuf_tensor("ones1r", [1, 128], F32))
        bases = E(nc.sbuf_tensor("bases", [1, 32], F32))
        basesrep = E(nc.sbuf_tensor("basesrep", [128, 32], F32))
        iotaS = E(nc.sbuf_tensor("iotaS", [128, 39], F32))
        acc39 = E(nc.sbuf_tensor("acc39", [128, 39], F32))
        m39 = E(nc.sbuf_tensor("m39", [128, 39], F32))
        buftm = E(nc.sbuf_tensor("buftm", [128, 4 * D], BF16))
        identb3 = E(nc.sbuf_tensor("identb3", [128, 128], BF16))
        tokf = E(nc.sbuf_tensor("tokf", [128, 39], F32))
        idx2i = E(nc.sbuf_tensor("idx2i", [128, 39], I32))
        idxi = E(nc.sbuf_tensor("idxi", [128, 39], I32))
        psmm = [E(nc.psum_tensor(f"ps{i}", [128, 512], F32)) for i in range(4)]
        psR = E(nc.psum_tensor("psR", [128, 32], F32))
        store_toks = []

        t_cl = k.dma("sync", lambda s: s.dma_start(out=crow[:, :], in_=cnt_i),
                     k.slot())
        t_b1 = k.dma("sync", lambda s: s.dma_start(out=b1c[:, :], in_=b1_i), k.slot())
        t_b2 = k.dma("sync", lambda s: s.dma_start(out=b2r[:, :], in_=b2_i), k.slot())
        t_id3 = k.dma("gpsimd", lambda g: g.dma_start(out=identb3[:, :],
                                                      in_=ident_i3), k.slot())
        slw1 = k.slot()
        slw2 = k.slot()
        for kk in range(ND):
            k.dma("sync", lambda s, kk=kk: s.dma_start(
                out=w1b[:, kk * F4:(kk + 1) * F4],
                in_=w1_i[kk * 128:(kk + 1) * 128, :]), slw1)
        for k2 in range(8):
            k.dma("scalar", lambda s, k2=k2: s.dma_start(
                out=w2b[:, :].rearrange("p (k dd) -> p k dd", k=32)[:, k2 * 4:(k2 + 1) * 4, :],
                in_=w2_i[k2 * 512:(k2 + 1) * 512, :].rearrange(
                    "(k p) dd -> p k dd", p=128)), slw2)
        t_w1_all = (slw1[0], 8 * 16)
        t_w2_all = (slw2[0], 8 * 16)

        t_o16 = k.op("vector", lambda v: v.memset(ones16[:, :], 1.0))
        t_o1r = k.op("vector", lambda v: v.memset(ones1r[:, :], 1.0))
        t_io = k.op("gpsimd", lambda g: g.iota(
            iotaS[:, :], pattern=[[128, 39]], base=0, channel_multiplier=1,
            allow_small_or_imprecise_dtypes=True))
        t_sc = k.op("vector", lambda v: v.tensor_tensor_scan(
            bases[:, 0:16], ones16[:, :], crow[:, :], 0.0, OP.mult, OP.add),
            [t_cl, t_o16])
        t_ex = k.op("vector", lambda v: v.tensor_sub(bases[:, 0:16], bases[:, 0:16],
                                                     crow[:, :]), [t_sc])
        t_jm0 = k.op("vector", lambda v: v.memset(bases[:, 16:17], 0.0))
        t_jm = k.op("vector", lambda v: v.tensor_scalar(
            bases[:, 17:32], crow[:, 0:15], -1.0, 2048.0, OP.mult, OP.add),
            [t_ex, t_jm0])
        t_rep = k.op("tensor", lambda t: t.matmul(
            psR[:, :], ones1r[:, :], bases[:, :], start=True, stop=True),
            [t_jm, t_o1r])
        t_repc = k.op("vector", lambda v: v.tensor_copy(basesrep[:, :], psR[:, :]),
                      [t_rep])
        t_acc = k.op("vector", lambda v: v.tensor_copy(acc39[:, :], iotaS[:, :]),
                     [t_io])
        prev = t_acc
        for r in range(1, 16):
            tm_ = k.op("vector", lambda v, r=r: v.tensor_scalar(
                m39[:, :], iotaS[:, :], basesrep[:, r:r + 1], None, OP.is_ge),
                [prev, t_repc])
            prev = k.op("vector", lambda v, r=r: v.scalar_tensor_tensor(
                acc39[:, :], m39[:, :], basesrep[:, 16 + r:17 + r], acc39[:, :],
                OP.mult, OP.add), [tm_])
        t_i2 = k.op("vector", lambda v: v.tensor_copy(idx2i[:, :], acc39[:, :]),
                    [prev])
        k.flush(block)

        sltk = [k.slot() for _ in range(4)]
        t_tk = [None] * 39
        t_fix = [None] * 39

        def emit_tokgrp(gidx):
            g0 = 4 * gidx
            g1 = min(g0 + 4, 39)
            for a in range(g0, g1):
                dep = [t_i2] if a < 4 else [t_tk[a - 4]]
                t_tk[a] = k.dma("gpsimd", lambda g, a=a: g.indirect_dma_start(
                    tokf[:, a:a + 1], None, tmy_flat,
                    IndirectOffsetOnAxis(idx2i[:, a:a + 1], 0)), sltk[a % 4],
                    after=dep)
            depf = [t_tk[a] for a in range(g0, g1)]
            f1 = k.op("vector", lambda v, g0=g0, g1=g1: v.tensor_scalar(
                m39[:, g0:g1], tokf[:, g0:g1], 0.0, None, OP.is_equal), depf)
            f2 = k.op("vector", lambda v, g0=g0, g1=g1: v.scalar_tensor_tensor(
                m39[:, g0:g1], m39[:, g0:g1], float(NCORES * TOK + 1),
                tokf[:, g0:g1], OP.mult, OP.add), [f1])
            f3 = k.op("vector", lambda v, g0=g0, g1=g1: v.tensor_scalar(
                m39[:, g0:g1], m39[:, g0:g1], -1.0, None, OP.add), [f2])
            f4 = k.op("vector", lambda v, g0=g0, g1=g1: v.tensor_copy(
                idxi[:, g0:g1], m39[:, g0:g1]), [f3])
            for a in range(g0, g1):
                t_fix[a] = f4
        emit_tokgrp(0)
        slga = [k.slot() for _ in range(4)]
        t_gablk = [None] * 39
        t_trcp = [None] * 39

        def emit_gather(a):
            par = a % 4
            dep_g = [t_fix[a]]
            if a >= 4:
                dep_g.append(t_trcp[a - 4])
            t_gablk[a] = k.dma("gpsimd", lambda g, a=a, par=par:
                g.indirect_dma_start(
                    buftm[:, par * D:(par + 1) * D], None,
                    n2g_i, IndirectOffsetOnAxis(idxi[:, a:a + 1], 0)),
                slga[par], after=dep_g)
        for a in range(4):
            emit_gather(a)
        k.flush(block)

        pstT = [E(nc.psum_tensor("pstT0", [128, 128], BF16)),
                E(nc.psum_tensor("pstT1", [128, 128], BF16))]
        trn_cnt = [0]
        t_trcv = [None, None]
        slys = [k.slot(), k.slot()]
        t_h = {}
        t_y = {}
        ev1_flat = []
        ev2_flat = []
        ev2_key = {}
        t_mm2_last = None
        b1cnt = 0
        b2cnt = 0
        for p in range(NPAN):
            nsl = 512 if p < NPAN - 1 else CAPP - 512 * (NPAN - 1)
            nsm = nsl // 128
            pb = (p % 2) * ND * 512
            blks = range(4 * p, min(4 * p + 4, 39))
            tr_toks = []
            for a in blks:
                par = a % 4
                for kk in range(ND):
                    bb = trn_cnt[0]
                    trn_cnt[0] += 1
                    pst = pstT[bb % 2]
                    dep_t = [t_gablk[a], t_id3]
                    if bb >= 2:
                        dep_t.append(t_trcv[bb % 2])
                    t_tr = k.op("tensor", lambda t, par=par, kk=kk, pst=pst:
                        t.transpose(pst[:, :],
                                    buftm[:, par * D + kk * 128:par * D + (kk + 1) * 128],
                                    identb3[:, :]), dep_t)
                    t_cv = k.op("vector", lambda v, p=p, a=a, kk=kk, pb=pb, pst=pst:
                        v.tensor_copy(
                            buffm[:, pb + kk * 512 + (a - 4 * p) * 128:
                                  pb + kk * 512 + (a - 4 * p) * 128 + 128],
                            pst[:, :]), [t_tr])
                    t_trcv[bb % 2] = t_cv
                    tr_toks.append(t_cv)
                t_trcp[a] = tr_toks[-1]
            t_trall = tr_toks[-1]
            if p + 1 < NPAN:
                emit_tokgrp(p + 1)
            for a2 in range(4 * (p + 1), min(4 * (p + 2), 39)):
                emit_gather(a2)
            for m in range(32):
                b = b1cnt
                b1cnt += 1
                ps = psmm[b % 2]
                dep = [t_trall, t_w1_all]
                if b >= 2:
                    dep.append(ev1_flat[b - 2])

                def mm1(t, m=m, ps=ps, pb=pb, nsl=nsl):
                    for kk in range(ND):
                        ins = t.matmul(
                            ps[:, 0:nsl],
                            w1b[:, kk * F4 + m * 128:kk * F4 + (m + 1) * 128],
                            buffm[:, pb + kk * 512:pb + kk * 512 + nsl],
                            start=(kk == 0), stop=(kk == ND - 1))
                    return ins
                t_m = k.op("tensor", mm1, dep)
                dep_gelu = [t_m, t_b1]
                if p > 0 and m == 0:
                    dep_gelu.append(t_mm2_last)
                t_e = k.op("scalar", lambda s, m=m, ps=ps, nsl=nsl: s.activation(
                    hpan[:, m * 512:m * 512 + nsl], ps[:, 0:nsl],
                    AF.Gelu_apprx_tanh, bias=b1c[:, m:m + 1]),
                    dep_gelu)
                t_h[(p, m)] = t_e
                ev1_flat.append(t_e)
            for sm in range(nsm):
                for dp in range(2):
                    b = b2cnt
                    b2cnt += 1
                    ps = psmm[2 + b % 2]
                    dep = [t_h[(p, 15)], t_w2_all]
                    if b >= 2:
                        dep.append(ev2_flat[b - 2])

                    def mm2a(t, sm=sm, dp=dp, ps=ps):
                        for kk in range(16):
                            ins = t.matmul(
                                ps[:, :],
                                hpan[:, kk * 512 + sm * 128:kk * 512 + (sm + 1) * 128],
                                w2b[:, kk * D + dp * 512:kk * D + (dp + 1) * 512],
                                start=(kk == 0), stop=False)
                        return ins

                    def mm2b(t, sm=sm, dp=dp, ps=ps):
                        for kk in range(16, 32):
                            ins = t.matmul(
                                ps[:, :],
                                hpan[:, kk * 512 + sm * 128:kk * 512 + (sm + 1) * 128],
                                w2b[:, kk * D + dp * 512:kk * D + (dp + 1) * 512],
                                start=False, stop=(kk == 31))
                        return ins
                    k.op("tensor", mm2a, dep)
                    t_m2 = k.op("tensor", mm2b, [t_h[(p, 31)]])
                    t_mm2_last = t_m2
                    yh = (sm % 2) * D
                    dep_e = [t_m2, t_b2]
                    if b >= 4:
                        dep_e.append(t_y[ev2_key[b - 4]])
                    t_e2 = k.op("vector", lambda v, yh=yh, dp=dp, ps=ps: v.tensor_add(
                        ystage[:, yh + dp * 512:yh + (dp + 1) * 512], ps[:, :],
                        b2r[:, dp * 512:(dp + 1) * 512]), dep_e)
                    ev2_flat.append(t_e2)
                    ev2_key[b] = (p, sm, dp)
                t_st = k.dma("scalar", lambda s, p=p, sm=sm, yh=yh:
                    s.dma_start(out=y_o[p * 512 + sm * 128:p * 512 + (sm + 1) * 128, :],
                                in_=ystage[:, yh:yh + D]), slys[sm % 2],
                    after=[ev2_flat[-1], ev2_flat[-2]])
                t_y[ev2_key[b2cnt - 1]] = t_st
                t_y[ev2_key[b2cnt - 2]] = t_st
                store_toks.append(t_st)
        k.flush(block)
        k.final_wait(block, "sync", store_toks)
    return nc


def l3_inputs(n2glob, T_all, counts_all, inp, core):
    e = core
    tmy = np.zeros((18, TOK), np.float32)
    crow = np.zeros((1, 16), np.float32)
    for kk in range(2):
        for cp in range(8):
            tmy[kk * 8 + cp] = T_all[cp][kk * 8 + e]
            crow[0, kk * 8 + cp] = counts_all[cp][e, kk]
    b1 = np.asarray(inp["L0_ffn_b1"], np.float32)[e]
    b2 = np.asarray(inp["L0_ffn_b2"], np.float32)[e]
    W1 = np.asarray(inp["w1b_pre"])
    W2 = np.asarray(inp["w2b_pre"])
    return {
        "n2glob": np.ascontiguousarray(
            np.concatenate([np.asarray(n2glob, BF),
                            np.zeros((128, D), BF)], 0)),
        "Tmy": tmy,
        "cntrow": crow,
        "W1": np.ascontiguousarray(W1),
        "b1_c": np.ascontiguousarray(b1.reshape(F4 // 128, 128).T),
        "W2": np.ascontiguousarray(W2),
        "b2_r": np.ascontiguousarray(np.broadcast_to(b2[None, :], (128, D))),
        "ident": np.eye(128, dtype=np.float32),
    }


def build_l4():
    nc = _bass()
    yg_i = nc.dram_tensor("yglob", [NCORES * CAPP + 128, D], BF16,
                          kind="ExternalInput").ap()
    xmid_i = nc.dram_tensor("xmid_i", [TOK, D], F32, kind="ExternalInput").ap()
    route_i = nc.dram_tensor("route_i", [128, 96], F32, kind="ExternalInput").ap()
    cnt0_i = nc.dram_tensor("cnt0", [8, 8], F32, kind="ExternalInput").ap()
    cnt1_i = nc.dram_tensor("cnt1", [8, 8], F32, kind="ExternalInput").ap()
    mlt_i = nc.dram_tensor("masklt", [8, 2], F32, kind="ExternalInput").ap()
    ident_i = nc.dram_tensor("ident", [128, 128], F32, kind="ExternalInput").ap()

    out_o = nc.dram_tensor("out_o", [TOK, D], F32, kind="ExternalOutput").ap()

    with ExitStack() as ctx:
        E = ctx.enter_context
        k = K(nc, ctx)
        block = E(nc.Block())
        identf = E(nc.sbuf_tensor("identf", [128, 128], F32))
        cnt0b = E(nc.sbuf_tensor("cnt0b", [8, 8], F32))
        cnt1b = E(nc.sbuf_tensor("cnt1b", [8, 8], F32))
        mltb = E(nc.sbuf_tensor("mltb", [8, 2], F32))
        fld = E(nc.sbuf_tensor("fld", [128, 96], F32))
        iota8 = E(nc.sbuf_tensor("iota8", [128, 8], F32))
        onesr = E(nc.sbuf_tensor("onesr", [1, 128], F32))
        pre8 = E(nc.sbuf_tensor("pre8", [8, 2], F32))
        pre8b = E(nc.sbuf_tensor("pre8b", [8, 2], F32))
        rows28 = E(nc.sbuf_tensor("rows28", [1, 16], F32))
        rows28b = E(nc.sbuf_tensor("rows28b", [1, 16], F32))
        b1row = E(nc.sbuf_tensor("b1row", [1, 8], F32))
        brep = E(nc.sbuf_tensor("brep", [128, 16], F32))
        m8 = E(nc.sbuf_tensor("m8", [128, 8], F32))
        s8 = E(nc.sbuf_tensor("s8", [128, 8], F32))
        colsq = E(nc.sbuf_tensor("colsq", [128, 16], F32))
        idxg = E(nc.sbuf_tensor("idxg", [128, 8], I32))
        idxgf = E(nc.sbuf_tensor("idxgf", [128, 8], F32))
        At = E(nc.sbuf_tensor("At", [128, 4 * D], BF16))
        Bt = E(nc.sbuf_tensor("Bt", [128, 4 * D], BF16))
        xmb = E(nc.sbuf_tensor("xmb", [128, 4 * D], F32))
        outs = E(nc.sbuf_tensor("outs", [128, 4 * D], F32))
        psA = E(nc.psum_tensor("psA", [128, 512], F32))
        psB = E(nc.psum_tensor("psB", [128, 512], F32))
        store_toks = []

        slc = k.slot()
        k.dma("sync", lambda s: s.dma_start(out=cnt0b[:, :], in_=cnt0_i), slc)
        k.dma("sync", lambda s: s.dma_start(out=cnt1b[:, :], in_=cnt1_i), slc)
        k.dma("sync", lambda s: s.dma_start(out=mltb[:, :], in_=mlt_i), slc)
        k.dma("sync", lambda s: s.dma_start(out=identf[:, :], in_=ident_i), slc)
        t_call = k.dma("sync", lambda s: s.dma_start(out=fld[:, :], in_=route_i), slc)
        t_c0 = t_c1 = t_ml = t_idf = t_fl = t_call
        t_io = k.op("gpsimd", lambda g: g.iota(
            iota8[:, :], pattern=[[1, 8]], base=0, channel_multiplier=0,
            allow_small_or_imprecise_dtypes=True))
        t_on = k.op("vector", lambda v: v.memset(onesr[:, :], 1.0))
        t_mA = k.op("vector", lambda v: v.memset(At[:, :], 0))
        t_mB = k.op("vector", lambda v: v.memset(Bt[:, :], 0))

        def mmb0(t):
            return t.matmul(psA[0:8, 0:2], cnt0b[:, :], mltb[:, :],
                            start=True, stop=True)

        def mmb1(t):
            return t.matmul(psB[0:8, 0:2], cnt1b[:, :], mltb[:, :],
                            start=True, stop=True)
        t_b0 = k.op("tensor", mmb0, [t_c0, t_ml])
        t_b1m = k.op("tensor", mmb1, [t_c1, t_ml])
        t_p0 = k.op("vector", lambda v: v.tensor_copy(pre8[:, :], psA[0:8, 0:2]),
                    [t_b0])
        t_p1 = k.op("vector", lambda v: v.tensor_copy(pre8b[:, :], psB[0:8, 0:2]),
                    [t_b1m])
        t_t0 = k.op("tensor", lambda t: t.transpose(
            psA[0:1, 0:8], pre8[:, 0:1], identf[0:8, 0:8]), [t_p0, t_idf])
        t_t0c = k.op("vector", lambda v: v.tensor_copy(rows28[0:1, 0:8],
                                                       psA[0:1, 0:8]), [t_t0])
        t_t0b = k.op("tensor", lambda t: t.transpose(
            psB[0:1, 0:8], pre8[:, 1:2], identf[0:8, 0:8]), [t_p0, t_idf, t_p1])
        t_t0bc = k.op("vector", lambda v: v.tensor_copy(rows28[0:1, 8:16],
                                                        psB[0:1, 0:8]), [t_t0b])
        t_t1 = k.op("tensor", lambda t: t.transpose(
            psA[0:1, 8:16], pre8b[:, 0:1], identf[0:8, 0:8]),
            [t_p1, t_t0c, t_t0])
        t_t1c = k.op("vector", lambda v: v.tensor_copy(rows28b[0:1, 0:8],
                                                       psA[0:1, 8:16]), [t_t1])
        t_br = k.op("vector", lambda v: v.tensor_add(
            b1row[:, :], rows28[0:1, 8:16], rows28b[0:1, 0:8]),
            [t_t1c, t_t0bc])

        def mmr0(t):
            return t.matmul(psA[0:128, 0:8], onesr[:, :], rows28[0:1, 0:8],
                            start=True, stop=True)

        def mmr1(t):
            return t.matmul(psB[0:128, 0:8], onesr[:, :], b1row[:, :],
                            start=True, stop=True)
        t_r0 = k.op("tensor", mmr0, [t_br, t_on])
        t_r1 = k.op("tensor", mmr1, [t_r0])
        t_r0c = k.op("vector", lambda v: v.tensor_copy(brep[:, 0:8], psA[0:128, 0:8]),
                     [t_r0])
        t_r1c = k.op("vector", lambda v: v.tensor_copy(brep[:, 8:16], psB[0:128, 0:8]),
                     [t_r1])
        k.flush(block)

        slg = [k.slot() for _ in range(8)]
        slxm = [k.slot() for _ in range(4)]
        slout = [k.slot() for _ in range(4)]
        t_done = [None] * NT
        t_gath = {}
        for grp in range(0, NT, 4):
            for t in range(grp, grp + 4):
                half = (t % 4) * D
                t_xl = k.dma("sync", lambda s, t=t, half=half: s.dma_start(
                    out=xmb[:, half:half + D], in_=xmid_i[t * 128:(t + 1) * 128, :]),
                    slxm[t % 4], after=[t_done[t - 4]] if t >= 4 else [t_fl])
                t_gath[(t, "x")] = t_xl
                for kk in range(2):
                    cc = (t % 4) * 2 + kk
                    e_col = fld[:, kk * NT + t:kk * NT + t + 1]
                    lp_col = fld[:, (2 + kk) * NT + t:(2 + kk) * NT + t + 1]
                    g_col = fld[:, (4 + kk) * NT + t:(4 + kk) * NT + t + 1]
                    dep0 = [t_r0c, t_r1c] if t < 4 else [t_done[t - 4]]
                    a1 = k.op("vector", lambda v, e_col=e_col: v.tensor_scalar(
                        m8[:, :], iota8[:, :], e_col, None, OP.is_equal),
                        dep0 + [t_io])
                    a2 = k.op("vector", lambda v, kk=kk: v.tensor_mul(
                        s8[:, :], m8[:, :], brep[:, kk * 8:(kk + 1) * 8]), [a1])
                    a3 = k.op("vector", lambda v, cc=cc: v.reduce_sum(
                        colsq[:, cc:cc + 1], s8[:, :], axis=AX), [a2])
                    a4 = k.op("vector", lambda v, cc=cc, lp_col=lp_col: v.tensor_add(
                        colsq[:, cc:cc + 1], colsq[:, cc:cc + 1], lp_col), [a3])
                    a5 = k.op("vector", lambda v, cc=cc: v.tensor_scalar(
                        colsq[:, 8 + cc:9 + cc], colsq[:, cc:cc + 1], float(CAP),
                        None, OP.is_lt), [a4])
                    a6 = k.op("vector", lambda v, cc=cc, g_col=g_col: v.tensor_mul(
                        colsq[:, 8 + cc:9 + cc], colsq[:, 8 + cc:9 + cc], g_col), [a5])
                    a5b = k.op("vector", lambda v, cc=cc: v.tensor_scalar(
                        s8[:, 0:1], colsq[:, cc:cc + 1], float(CAP), None,
                        OP.is_lt), [a6])
                    a7 = k.op("vector", lambda v, cc=cc, e_col=e_col:
                        v.scalar_tensor_tensor(
                            idxgf[:, cc:cc + 1], e_col, float(CAPP),
                            colsq[:, cc:cc + 1], OP.mult, OP.add), [a5b])
                    SENT = float(NCORES * CAPP)
                    a8 = k.op("vector", lambda v: v.tensor_scalar(
                        s8[:, 1:2], s8[:, 0:1], -SENT, SENT, OP.mult, OP.add), [a7])
                    a8b = k.op("vector", lambda v, cc=cc: v.tensor_mul(
                        idxgf[:, cc:cc + 1], idxgf[:, cc:cc + 1], s8[:, 0:1]), [a8])
                    a9 = k.op("vector", lambda v, cc=cc: v.tensor_add(
                        idxgf[:, cc:cc + 1], idxgf[:, cc:cc + 1], s8[:, 1:2]), [a8b])
                    a10 = k.op("vector", lambda v, cc=cc: v.tensor_copy(
                        idxg[:, cc:cc + 1], idxgf[:, cc:cc + 1]), [a9])
                    tgt = At if kk == 0 else Bt
                    t_g = k.dma("gpsimd", lambda g, cc=cc, tgt=tgt, half=half:
                        g.indirect_dma_start(
                            tgt[:, half:half + D], None,
                            yg_i, IndirectOffsetOnAxis(idxg[:, cc:cc + 1], 0)),
                        slg[cc], after=[a10, t_mA, t_mB])
                    t_gath[(t, kk)] = t_g
            for t in range(grp, grp + 4):
                half = (t % 4) * D
                dep_c1 = [t_gath[(t, "x")], t_gath[(t, 0)]]
                if t >= 4:
                    dep_c1.append((slout[t % 4][0], slout[t % 4][1]))
                c1 = k.op("vector", lambda v, half=half, t=t: v.scalar_tensor_tensor(
                    outs[:, half:half + D], At[:, half:half + D],
                    colsq[:, 8 + (t % 4) * 2:9 + (t % 4) * 2],
                    xmb[:, half:half + D], OP.mult, OP.add),
                    dep_c1)
                c2 = k.op("vector", lambda v, half=half, t=t: v.scalar_tensor_tensor(
                    outs[:, half:half + D], Bt[:, half:half + D],
                    colsq[:, 8 + (t % 4) * 2 + 1:9 + (t % 4) * 2 + 1],
                    outs[:, half:half + D], OP.mult, OP.add),
                    [c1, t_gath[(t, 1)]])
                t_st = k.dma("scalar", lambda s, t=t, half=half: s.dma_start(
                    out=out_o[t * 128:(t + 1) * 128, :], in_=outs[:, half:half + D]),
                    slout[t % 4], after=[c2])
                t_done[t] = c2
                store_toks.append(t_st)
        k.flush(block)
        k.final_wait(block, "sync", store_toks)
    return nc


def l4_inputs(yglob, xmid, route, counts_all, core):
    cnt0 = np.stack([counts_all[c][:, 0] for c in range(8)])
    cnt1 = np.stack([counts_all[c][:, 1] for c in range(8)])
    mlt = np.zeros((8, 2), np.float32)
    mlt[:core, 0] = 1.0
    mlt[:, 1] = 1.0
    return {
        "yglob": np.ascontiguousarray(np.concatenate(
            [np.asarray(yglob, BF).reshape(NCORES * CAPP, D),
             np.zeros((128, D), BF)], 0)),
        "xmid_i": np.ascontiguousarray(xmid, np.float32),
        "route_i": np.ascontiguousarray(route, np.float32),
        "cnt0": np.ascontiguousarray(cnt0, np.float32),
        "cnt1": np.ascontiguousarray(cnt1, np.float32),
        "masklt": mlt,
        "ident": np.eye(128, dtype=np.float32),
    }


_NC_CACHE = {}


def _get_nc(name):
    if name not in _NC_CACHE:
        _NC_CACHE[name] = {"l1": build_l1, "l2": build_l2,
                           "l3": build_l3, "l4": build_l4}[name]()
    return _NC_CACHE[name]


def kernel(**inputs):
    inp = inputs
    x = np.asarray(inp["L0_attn_x"], np.float32).reshape(NCORES * TOK, D)
    cores = list(range(NCORES))

    nc1 = _get_nc("l1")
    maps1 = [l1_inputs(x[c * TOK:(c + 1) * TOK], inp) for c in cores]
    r1 = run_bass_kernel_spmd(nc1, maps1, core_ids=cores).results

    nc2 = _get_nc("l2")
    maps2 = []
    for c in cores:
        g = (c // 4) * 4
        simp4 = np.stack([r1[i]["simp_o"] for i in range(g, g + 4)])
        maps2.append(l2_inputs(simp4, r1[c]["vg_o"], r1[c]["xacc_o"], inp, c))
    r2 = run_bass_kernel_spmd(nc2, maps2, core_ids=cores).results

    nc3 = _get_nc("l3")
    n2glob = np.concatenate([r2[c]["n2_o"] for c in cores], 0)
    T_all = [r2[c]["T_o"] for c in cores]
    counts_all = [r2[c]["counts_o"] for c in cores]
    maps3 = []
    for c in cores:
        inp3 = dict(inp)
        inp3["w1b_pre"] = r2[c]["w1c_o"]
        inp3["w2b_pre"] = r2[c]["w2c_o"]
        maps3.append(l3_inputs(n2glob, T_all, counts_all, inp3, c))
    r3 = run_bass_kernel_spmd(nc3, maps3, core_ids=cores).results

    nc4 = _get_nc("l4")
    yglob = np.concatenate([r3[c]["y_o"] for c in cores], 0)
    maps4 = [l4_inputs(yglob, r2[c]["xmid_o"], r2[c]["route_o"],
                       counts_all, c) for c in cores]
    r4 = run_bass_kernel_spmd(nc4, maps4, core_ids=cores).results

    out = np.concatenate([r4[c]["out_o"] for c in cores], 0)
    return out.reshape(BB, NN, D)


# version 14
# speedup vs baseline: 1.1907x; 1.0067x over previous
import numpy as np
import ml_dtypes
from contextlib import ExitStack

import concourse.bass as bass
import concourse.mybir as mybir
from concourse.bass import IndirectOffsetOnAxis
from concourse.bass_utils import run_bass_kernel_spmd

F32 = mybir.dt.float32
BF16 = mybir.dt.bfloat16
I32 = mybir.dt.int32
AF = mybir.ActivationFunctionType
OP = mybir.AluOpType
AX = mybir.AxisListType.X
BF = ml_dtypes.bfloat16

NCORES = 8
BB, NN, D = 2, 8192, 1024
TOK = 2048
J = 2048
F4 = 4096
NT = TOK // 128
ND = D // 128
NJ = J // 128
EPS = 1e-5
INVN = 1.0 / NN
CAP = 4916
CAPP = 4992
TBL = 36864

ENGS = ("vector", "scalar", "tensor", "gpsimd", "sync")


class K:

    def __init__(self, nc, ctx):
        self.nc, self.ctx = nc, ctx
        self.esem, self.ecnt = {}, {}
        for e in ENGS:
            self.esem[e] = ctx.enter_context(nc.semaphore("es_" + e))
            self.ecnt[e] = 0
        self.hw = {e: {} for e in ENGS}
        self.pend = {e: [] for e in ENGS}
        self._slot_n = 0

    def slot(self, name=None):
        self._slot_n += 1
        h = self.ctx.enter_context(
            self.nc.semaphore(name or f"sl{self._slot_n}"))
        return [h, 0]

    def op(self, eng, fn, after=()):
        self.ecnt[eng] += 1
        v = self.ecnt[eng]
        self.pend[eng].append((fn, tuple(after), self.esem[eng], 1, v))
        return (self.esem[eng], v)

    def dma(self, eng, fn, slot, after=()):
        slot[1] += 16
        v = slot[1]
        self.pend[eng].append((fn, tuple(after), slot[0], 16, v))
        return (slot[0], v)

    def bar(self, extra=()):
        toks = [(self.esem[e], self.ecnt[e]) for e in ENGS if self.ecnt[e] > 0]
        toks += [t for t in extra if t[1] > 0]
        return toks

    def flush(self, block):
        for eng in ENGS:
            items = self.pend[eng]
            if not items:
                continue
            hw = self.hw[eng]

            def body(E, items=tuple(items), hw=hw):
                for fn, after, sem, incby, v in items:
                    for (sh, sv) in after:
                        key = id(sh)
                        if hw.get(key, -1) >= sv:
                            continue
                        E.wait_ge(sh, sv)
                        hw[key] = sv
                    fn(E).then_inc(sem, incby)
            getattr(block, eng)(body)
            self.pend[eng] = []

    def final_wait(self, block, eng, tokens):
        def body(E, tokens=tuple(tokens)):
            for (sh, sv) in tokens:
                E.wait_ge(sh, sv)
        getattr(block, eng)(body)


def _bass(scratch=16384):
    return bass.Bass("TRN2", target_bir_lowering=False,
                     dynamic_dma_scratch_size=scratch)


def build_l1():
    nc = _bass()
    x_i = nc.dram_tensor("x_sl", [TOK, D], F32, kind="ExternalInput").ap()
    lng_i = nc.dram_tensor("lng_r", [128, D], F32, kind="ExternalInput").ap()
    lnb_i = nc.dram_tensor("lnb_r", [128, D], F32, kind="ExternalInput").ap()
    wqk_i = nc.dram_tensor("Wqk", [D, J], F32, kind="ExternalInput").ap()
    bqk_i = nc.dram_tensor("bqk_r", [128, J], F32, kind="ExternalInput").ap()
    osg_i = nc.dram_tensor("osg_r", [256, J], F32, kind="ExternalInput").ap()
    osb_i = nc.dram_tensor("osb_r", [256, J], F32, kind="ExternalInput").ap()
    whid_i = nc.dram_tensor("Whid", [D, F4], F32, kind="ExternalInput").ap()
    bhid_i = nc.dram_tensor("bhid_c", [128, F4 // 128], F32,
                            kind="ExternalInput").ap()
    ident_i = nc.dram_tensor("ident", [128, 128], F32, kind="ExternalInput").ap()

    xacc_o = nc.dram_tensor("xacc_o", [TOK, D], F32, kind="ExternalOutput").ap()
    simp_o = nc.dram_tensor("simp_o", [J, J], BF16, kind="ExternalOutput").ap()
    vg_o = nc.dram_tensor("vg_o", [J, TOK], BF16, kind="ExternalOutput").ap()

    with ExitStack() as ctx:
        E = ctx.enter_context
        k = K(nc, ctx)
        block = E(nc.Block())
        n1fm = E(nc.sbuf_tensor("n1fm", [128, ND * TOK], BF16))
        identb = E(nc.sbuf_tensor("identb", [128, 128], BF16))
        cols = E(nc.sbuf_tensor("cols", [128, 32], F32))
        cols2 = E(nc.sbuf_tensor("cols2", [128, 16], F32))
        cols_pad = E(nc.sbuf_tensor("colspad", [128, 16], F32))
        psA = E(nc.psum_tensor("psA", [128, 512], F32))
        psB = E(nc.psum_tensor("psB", [128, 512], F32))
        pstA = E(nc.psum_tensor("pstA", [128, 128], BF16))
        pstB = E(nc.psum_tensor("pstB", [128, 128], BF16))
        store_toks = []

        with ExitStack() as sA:
            EA = sA.enter_context
            n1tm = EA(nc.sbuf_tensor("n1tm", [128, NT * D], BF16))
            lng = EA(nc.sbuf_tensor("lng", [128, D], F32))
            lnb = EA(nc.sbuf_tensor("lnb", [128, D], F32))
            xbuf = EA(nc.sbuf_tensor("xbuf", [128, 4 * D], F32))
            xcm = EA(nc.sbuf_tensor("xcm", [128, 4 * D], F32))
            ln1 = EA(nc.sbuf_tensor("ln1", [128, 4 * D], F32))
            sq = EA(nc.sbuf_tensor("sq", [128, D], F32))
            xas = EA(nc.sbuf_tensor("xas", [128, 4 * D], F32))

            slx = [k.slot(), k.slot()]
            slxa = [k.slot(), k.slot()]
            slc = k.slot()

            k.dma("gpsimd", lambda g: g.dma_start(out=lng[:, :], in_=lng_i), slc)
            k.dma("gpsimd", lambda g: g.dma_start(out=lnb[:, :], in_=lnb_i), slc)
            tconsts = k.dma("gpsimd",
                            lambda g: g.dma_start(out=identb[:, :], in_=ident_i), slc)

            t_x9 = [None] * NT
            t_n1 = [None] * NT
            xacc_sts = []
            slx = [k.slot() for _ in range(4)]
            slxa = [k.slot() for _ in range(4)]
            for i in range(NT):
                sl = slice((i % 4) * D, (i % 4 + 1) * D)
                c0 = (i % 4) * 8
                dep_slot = [t_x9[i - 4]] if i >= 4 else []
                t_ld = k.dma("sync", lambda s, i=i, sl=sl: s.dma_start(
                    out=xbuf[:, sl], in_=x_i[i * 128:(i + 1) * 128, :]),
                    slx[i % 4], after=dep_slot)

                a = [t_ld] + ([tconsts] if i == 0 else [])
                t1 = k.op("vector", lambda v, sl=sl, c0=c0: v.reduce_sum(
                    cols[:, c0:c0 + 1], xbuf[:, sl], axis=AX), a)
                t2 = k.op("vector", lambda v, c0=c0: v.tensor_scalar(
                    cols[:, c0 + 1:c0 + 2], cols[:, c0:c0 + 1], 1.0 / D, None,
                    OP.mult), [t1])
                t3 = k.op("vector", lambda v, sl=sl, c0=c0: v.tensor_scalar(
                    xcm[:, sl], xbuf[:, sl], cols[:, c0 + 1:c0 + 2], None,
                    OP.subtract), [t2])
                t4 = k.op("scalar", lambda s, sl=sl, c0=c0: s.activation(
                    sq[:, :], xcm[:, sl], AF.Square,
                    accum_out=cols[:, c0 + 2:c0 + 3]), [t3])
                t5 = k.op("vector", lambda v, c0=c0: v.tensor_scalar(
                    cols[:, c0 + 3:c0 + 4], cols[:, c0 + 2:c0 + 3], 1.0 / D, EPS,
                    OP.mult, OP.add), [t4])
                t6 = k.op("scalar", lambda s, c0=c0: s.sqrt(
                    cols[:, c0 + 4:c0 + 5], cols[:, c0 + 3:c0 + 4]), [t5])
                t7 = k.op("vector", lambda v, c0=c0: v.reciprocal(
                    cols[:, c0 + 5:c0 + 6], cols[:, c0 + 4:c0 + 5]), [t6])
                t8 = k.op("vector", lambda v, c0=c0: v.tensor_scalar(
                    cols[:, c0 + 6:c0 + 7], cols[:, c0 + 3:c0 + 4], 1.0, -EPS,
                    OP.mult, OP.add), [t7])
                t9 = k.op("vector", lambda v, c0=c0: v.tensor_mul(
                    cols[:, c0 + 6:c0 + 7], cols[:, c0 + 6:c0 + 7],
                    cols[:, c0 + 5:c0 + 6]), [t8])
                t11 = k.op("vector", lambda v, c0=c0: v.scalar_tensor_tensor(
                    cols[:, c0 + 6:c0 + 7], cols[:, c0 + 6:c0 + 7], 1.0,
                    cols[:, c0 + 5:c0 + 6], OP.mult, OP.mult), [t9])
                t12 = k.op("vector", lambda v, c0=c0: v.tensor_scalar(
                    cols[:, c0 + 6:c0 + 7], cols[:, c0 + 6:c0 + 7], 1.0, EPS,
                    OP.mult, OP.add), [t11])
                t13 = k.op("scalar", lambda s, c0=c0: s.sqrt(
                    cols[:, c0 + 7:c0 + 8], cols[:, c0 + 6:c0 + 7]), [t12])
                c2 = (i % 4) * 4
                t14 = k.op("vector", lambda v, c2=c2, c0=c0: v.reciprocal(
                    cols2[:, c2:c2 + 1], cols[:, c0 + 7:c0 + 8]), [t13])
                t15 = k.op("vector", lambda v, c2=c2, c0=c0: v.tensor_mul(
                    cols2[:, c2 + 1:c2 + 2], cols2[:, c2:c2 + 1],
                    cols[:, c0 + 5:c0 + 6]), [t14])
                dep_store = [xacc_sts[i - 4]] if i >= 4 else []
                tA = k.op("vector", lambda v, sl=sl, c0=c0: v.scalar_tensor_tensor(
                    xas[:, sl], xcm[:, sl], cols[:, c0 + 5:c0 + 6], xbuf[:, sl],
                    OP.mult, OP.add), [t7] + dep_store)
                t_x9[i] = tA
                t_st = k.dma("scalar", lambda s, i=i, sl=sl: s.dma_start(
                    out=xacc_o[i * 128:(i + 1) * 128, :], in_=xas[:, sl]),
                    slxa[i % 4], after=[tA])
                xacc_sts.append(t_st)
                store_toks.append(t_st)
                tB = k.op("vector", lambda v, sl=sl, c2=c2: v.tensor_scalar(
                    xcm[:, sl], xcm[:, sl], cols2[:, c2 + 1:c2 + 2], None,
                    OP.mult), [t15, tA, t4])
                tC = k.op("vector", lambda v, sl=sl: v.tensor_mul(
                    ln1[:, sl], xcm[:, sl], lng[:, :]), [tB])
                tD = k.op("vector", lambda v, i=i, sl=sl: v.tensor_add(
                    n1tm[:, i * D:(i + 1) * D], ln1[:, sl], lnb[:, :]), [tC])
                t_n1[i] = tD
            k.flush(block)

            t_trcopy = [None] * (NT * ND)
            for i in range(NT):
                for j in range(ND):
                    b = i * ND + j
                    pst = pstA if b % 2 == 0 else pstB
                    dep = [t_n1[i]]
                    if b >= 2:
                        dep.append(t_trcopy[b - 2])
                    if b == 0:
                        dep.append(tconsts)
                    t_tr = k.op("tensor", lambda t, i=i, j=j, pst=pst: t.transpose(
                        pst[:, :], n1tm[:, i * D + j * 128:i * D + (j + 1) * 128],
                        identb[:, :]), dep)
                    t_trcopy[b] = k.op("vector", lambda v, i=i, j=j, pst=pst:
                                       v.tensor_copy(
                                           n1fm[:, j * TOK + i * 128:j * TOK + (i + 1) * 128],
                                           pst[:, :]), [t_tr])
            k.flush(block)
            bar_AB = k.bar([tuple(s) for s in (slx[0], slx[1], slxa[0],
                                               slxa[1], slc)])

        with ExitStack() as sC:
            EC = sC.enter_context
            q_tm = EC(nc.sbuf_tensor("q_tm", [128, NT * J], BF16))
            k_tm = EC(nc.sbuf_tensor("k_tm", [128, NT * J], BF16))
            with ExitStack() as sCc:
                ECc = sCc.enter_context
                wqkb = ECc(nc.sbuf_tensor("wqkb", [128, 2 * 8 * 512], BF16))
                bqk = ECc(nc.sbuf_tensor("bqk", [1, J], BF16))
                ones1 = ECc(nc.sbuf_tensor("ones1", [1, 128], BF16))
                g0b = ECc(nc.sbuf_tensor("g0b", [128, J], BF16))
                g1b = ECc(nc.sbuf_tensor("g1b", [128, J], BF16))
                b0b = ECc(nc.sbuf_tensor("b0b", [128, J], BF16))
                b1b = ECc(nc.sbuf_tensor("b1b", [128, J], BF16))
                qsf = ECc(nc.sbuf_tensor("qsf", [128, 512], F32))
                qsb = ECc(nc.sbuf_tensor("qsb", [128, 2 * 512], BF16))
                qx = ECc(nc.sbuf_tensor("qx", [128, 2 * 512], BF16))

                slcc = k.slot()
                k.dma("gpsimd", lambda g: g.dma_start(out=bqk[:, :],
                                                      in_=bqk_i[0:1, :]), slcc,
                      after=bar_AB)
                t_o1 = k.op("vector", lambda v: v.memset(ones1[:, :], 1.0))
                k.dma("gpsimd", lambda g: g.dma_start(out=g0b[:, :],
                                                      in_=osg_i[0:128, :]), slcc)
                k.dma("gpsimd", lambda g: g.dma_start(out=g1b[:, :],
                                                      in_=osg_i[128:256, :]), slcc)
                k.dma("gpsimd", lambda g: g.dma_start(out=b0b[:, :],
                                                      in_=osb_i[0:128, :]), slcc)
                tcc = k.dma("gpsimd", lambda g: g.dma_start(out=b1b[:, :],
                                                            in_=osb_i[128:256, :]), slcc)

                slw = [k.slot(), k.slot()]
                t_wl = [None] * 4
                t_mm = {}
                t_ev1 = {}
                t_ev2 = {}
                t_v6 = {}
                for p in range(4):
                    js = p * 512
                    wbase = (p % 2) * 4096
                    dep_w = list(bar_AB) + [t_o1] if p < 2 else [t_mm[(p - 2, NT - 1)]]
                    t_wl[p] = k.dma("gpsimd", lambda g, js=js, wbase=wbase:
                                    g.dma_start(
                                        out=wqkb[:, wbase:wbase + 4096].rearrange(
                                            "p (k j) -> p k j", k=8),
                                        in_=wqk_i[:, js:js + 512].rearrange(
                                            "(k p) j -> p k j", p=128)),
                                    slw[p % 2], after=dep_w)
                    for i in range(NT):
                        b = p * NT + i
                        ps = psA if b % 2 == 0 else psB
                        dep = [t_wl[p], t_trcopy[i * ND + ND - 1], tcc]
                        if b >= 2:
                            dep.append(t_ev1[(b - 2) // NT, (b - 2) % NT])

                        def mm(t, i=i, js=js, wbase=wbase, ps=ps):
                            for kk in range(8):
                                t.matmul(
                                    ps[:, :],
                                    n1fm[:, kk * TOK + i * 128:kk * TOK + (i + 1) * 128],
                                    wqkb[:, wbase + kk * 512:wbase + (kk + 1) * 512],
                                    start=(kk == 0), stop=False)
                            return t.matmul(ps[:, :], ones1[:, :],
                                            bqk[:, js:js + 512],
                                            start=False, stop=True)
                        t_mm[(p, i)] = k.op("tensor", mm, dep)
                        dep1 = [t_mm[(p, i)], tcc]
                        if b == 0:
                            dep1 += bar_AB
                        if b >= 1:
                            dep1.append(t_ev2[(b - 1) // NT, (b - 1) % NT])
                        qb = (b % 2) * 512
                        if b >= 2:
                            dep1.append(t_v6[b - 2])
                        t2 = k.op("scalar", lambda s, ps=ps, qb=qb: s.activation(
                            qsb[:, qb:qb + 512], ps[:, :], AF.Silu), dep1)
                        t_ev1[(p, i)] = t2
                        t_ev2[(p, i)] = t2
                        t3 = k.op("vector", lambda v, js=js, qb=qb: v.tensor_mul(
                            qx[:, qb:qb + 512], qsb[:, qb:qb + 512],
                            g0b[:, js:js + 512]), [t2])
                        t4 = k.op("vector", lambda v, i=i, js=js, qb=qb:
                            v.tensor_add(
                                q_tm[:, i * J + js:i * J + js + 512],
                                qx[:, qb:qb + 512], b0b[:, js:js + 512]), [t3])
                        t5 = k.op("vector", lambda v, js=js, qb=qb: v.tensor_mul(
                            qx[:, qb:qb + 512], qsb[:, qb:qb + 512],
                            g1b[:, js:js + 512]), [t4])
                        t_v6[b] = k.op("vector", lambda v, i=i, js=js, qb=qb:
                            v.tensor_add(
                                k_tm[:, i * J + js:i * J + js + 512],
                                qx[:, qb:qb + 512], b1b[:, js:js + 512]), [t5])
                k.flush(block)
                bar_C = k.bar()

            simstage = EC(nc.sbuf_tensor("simstage", [128, 2 * J], BF16))
            slst = [k.slot(), k.slot()]
            t_ev = {}
            for m in range(NJ):
                sbase = (m % 2) * J
                for p in range(4):
                    b = m * 4 + p
                    ps = psA if b % 2 == 0 else psB
                    dep = list(bar_C) if b < 2 else [t_ev[b - 2]]
                    if p == 0 and m >= 2:
                        dep.append((slst[m % 2][0], slst[m % 2][1]))

                    def mm(t, m=m, p=p, ps=ps):
                        for i in range(NT):
                            ins = t.matmul(
                                ps[:, :],
                                q_tm[:, i * J + m * 128:i * J + (m + 1) * 128],
                                k_tm[:, i * J + p * 512:i * J + (p + 1) * 512],
                                start=(i == 0), stop=(i == NT - 1))
                        return ins
                    t_m = k.op("tensor", mm, dep)
                    t_ev[b] = k.op("vector", lambda v, sbase=sbase, p=p, ps=ps:
                                   v.tensor_scalar(
                                       simstage[:, sbase + p * 512:sbase + (p + 1) * 512],
                                       ps[:, :], INVN, None, OP.mult), [t_m])
                t_st = k.dma("sync", lambda s, m=m, sbase=sbase: s.dma_start(
                    out=simp_o[m * 128:(m + 1) * 128, :],
                    in_=simstage[:, sbase:sbase + J]), slst[m % 2],
                    after=[t_ev[m * 4 + 3]])
                store_toks.append(t_st)
            k.flush(block)
            bar_D = k.bar()

        with ExitStack() as sE:
            EE = sE.enter_context
            whb = EE(nc.sbuf_tensor("whb", [128, 4 * 8 * 128], BF16))
            bhid = EE(nc.sbuf_tensor("bhid", [128, F4 // 128], F32))
            hv = EE(nc.sbuf_tensor("hv", [128, J], BF16))
            hg = EE(nc.sbuf_tensor("hg", [128, J], BF16))
            vgs = EE(nc.sbuf_tensor("vgs", [128, 2 * J], BF16))

            slb = k.slot()
            t_bh = k.dma("gpsimd", lambda g: g.dma_start(out=bhid[:, :], in_=bhid_i),
                         slb, after=bar_D)
            slwh = [k.slot() for _ in range(4)]
            slvg = [k.slot(), k.slot()]
            t_hev_flat = []
            t_mm_flat = []
            t_vg = [None] * NJ
            order = 0
            for mi in range(NJ):
                for half, m in enumerate((mi, mi + 16)):
                    tgt = hv if half == 0 else hg
                    wbase = (order % 4) * 1024
                    dep_w = list(bar_D) if order < 4 else [t_mm_flat[(order - 4) * 4 + 3]]
                    t_w = k.dma("gpsimd", lambda g, m=m, wbase=wbase: g.dma_start(
                        out=whb[:, wbase:wbase + 1024].rearrange(
                            "p (k f) -> p k f", k=8),
                        in_=whid_i[:, m * 128:(m + 1) * 128].rearrange(
                            "(k p) f -> p k f", p=128)),
                        slwh[order % 4], after=dep_w)
                    for p in range(4):
                        b = order * 4 + p
                        ps = psA if b % 2 == 0 else psB
                        dep = [t_w, t_bh] if b < 2 else [t_w, t_hev_flat[b - 2]]
                        if b == 0:
                            dep += bar_D

                        def mmh(t, p=p, ps=ps, wbase=wbase):
                            for kk in range(8):
                                ins = t.matmul(
                                    ps[:, :],
                                    whb[:, wbase + kk * 128:wbase + (kk + 1) * 128],
                                    n1fm[:, kk * TOK + p * 512:kk * TOK + (p + 1) * 512],
                                    start=(kk == 0), stop=(kk == 7))
                            return ins
                        t_m = k.op("tensor", mmh, dep)
                        t_mm_flat.append(t_m)
                        dep_e = [t_m]
                        if p == 0 and half == 0 and mi >= 1:
                            dep_e.append(t_vg[mi - 1])
                        t_e = k.op("scalar", lambda s, m=m, p=p, ps=ps, tgt=tgt:
                                   s.activation(tgt[:, p * 512:(p + 1) * 512],
                                                ps[:, :], AF.Silu,
                                                bias=bhid[:, m:m + 1]), dep_e)
                        t_hev_flat.append(t_e)
                    order += 1
                vbase = (mi % 2) * J
                dep_v = [t_hev_flat[-1], t_hev_flat[-5]]
                if mi >= 2:
                    dep_v.append((slvg[mi % 2][0], slvg[mi % 2][1]))
                t_vg[mi] = k.op("vector", lambda v, vbase=vbase: v.tensor_mul(
                    vgs[:, vbase:vbase + J], hv[:, :], hg[:, :]), dep_v)
                t_st = k.dma("sync", lambda s, mi=mi, vbase=vbase: s.dma_start(
                    out=vg_o[mi * 128:(mi + 1) * 128, :],
                    in_=vgs[:, vbase:vbase + J]), slvg[mi % 2], after=[t_vg[mi]])
                store_toks.append(t_st)
            k.flush(block)
            k.final_wait(block, "sync", store_toks)

    return nc


def l1_inputs(x_slice, inp):
    rep = lambda a: np.ascontiguousarray(
        np.broadcast_to(np.asarray(a, np.float32)[None, :], (128, a.shape[-1])))
    osg = np.asarray(inp["L0_attn_os_gamma"], np.float32)
    osb = np.asarray(inp["L0_attn_os_beta"], np.float32)
    osg_r = np.concatenate([np.broadcast_to(osg[0][None, :], (128, J)),
                            np.broadcast_to(osg[1][None, :], (128, J))], 0)
    osb_r = np.concatenate([np.broadcast_to(osb[0][None, :], (128, J)),
                            np.broadcast_to(osb[1][None, :], (128, J))], 0)
    return {
        "x_sl": np.ascontiguousarray(x_slice, dtype=np.float32),
        "lng_r": rep(inp["L0_attn_ln_g"]),
        "lnb_r": rep(inp["L0_attn_ln_b"]),
        "Wqk": np.ascontiguousarray(inp["L0_attn_W_qk"], np.float32),
        "bqk_r": rep(inp["L0_attn_b_qk"]),
        "osg_r": np.ascontiguousarray(osg_r, np.float32),
        "osb_r": np.ascontiguousarray(osb_r, np.float32),
        "Whid": np.ascontiguousarray(inp["L0_attn_W_hidden"], np.float32),
        "bhid_c": np.ascontiguousarray(
            np.asarray(inp["L0_attn_b_hidden"], np.float32).reshape(F4 // 128, 128).T),
        "ident": np.eye(128, dtype=np.float32),
    }


def build_l2():
    nc = _bass()
    simp4_i = nc.dram_tensor("simp4", [4 * J, J], BF16, kind="ExternalInput").ap()
    vg_i = nc.dram_tensor("vg_i", [J, TOK], BF16, kind="ExternalInput").ap()
    xacc_i = nc.dram_tensor("xacc_i", [TOK, D], F32, kind="ExternalInput").ap()
    wout_i = nc.dram_tensor("Wout", [J, D], F32, kind="ExternalInput").ap()
    bout_i = nc.dram_tensor("bout_r", [1, D], F32, kind="ExternalInput").ap()
    wg_i = nc.dram_tensor("Wg", [D, 8], F32, kind="ExternalInput").ap()
    bg_i = nc.dram_tensor("bg_r", [128, 8], F32, kind="ExternalInput").ap()
    ident_i = nc.dram_tensor("ident", [128, 128], F32, kind="ExternalInput").ap()
    tokb_i = nc.dram_tensor("tokbase_r", [128, 1], F32, kind="ExternalInput").ap()
    w1e_i = nc.dram_tensor("W1e", [D, F4], F32, kind="ExternalInput").ap()
    w2e_i = nc.dram_tensor("W2e", [F4, D], F32, kind="ExternalInput").ap()

    xmid_o = nc.dram_tensor("xmid_o", [TOK, D], F32, kind="ExternalOutput").ap()
    n2_o = nc.dram_tensor("n2_o", [TOK, D], BF16, kind="ExternalOutput").ap()
    route_o = nc.dram_tensor("route_o", [128, 96], F32, kind="ExternalOutput").ap()
    counts_o = nc.dram_tensor("counts_o", [8, 2], F32, kind="ExternalOutput").ap()
    w1c_o = nc.dram_tensor("w1c_o", [D, F4], BF16, kind="ExternalOutput").ap()
    w2c_o = nc.dram_tensor("w2c_o", [F4, D], BF16, kind="ExternalOutput").ap()
    T_o = nc.dram_tensor("T_o", [16, TOK], F32, kind="ExternalOutput").ap()
    T_flat = T_o.rearrange("a b -> (a b)")[:, None]

    with ExitStack() as ctx:
        E = ctx.enter_context
        k = K(nc, ctx)
        block = E(nc.Block())
        identb = E(nc.sbuf_tensor("identb", [128, 128], BF16))
        identf = E(nc.sbuf_tensor("identf", [128, 128], F32))
        scols = E(nc.sbuf_tensor("scols", [128, 16], F32))
        cols = E(nc.sbuf_tensor("cols", [128, 16], F32))
        n2tm = E(nc.sbuf_tensor("n2tm", [128, NT * D], BF16))
        psA = E(nc.psum_tensor("psA", [128, 512], F32))
        psB = E(nc.psum_tensor("psB", [128, 512], F32))
        store_toks = []

        tid = k.dma("gpsimd", lambda g: g.dma_start(out=identb[:, :], in_=ident_i),
                    k.slot())
        tidf = k.dma("sync", lambda s: s.dma_start(out=identf[:, :], in_=ident_i),
                     k.slot())

        with ExitStack() as s1:
            E1 = s1.enter_context
            sbuf4 = [E1(nc.sbuf_tensor(f"pp{g}", [128, 8 * J], BF16)) for g in range(4)]
            acc = E1(nc.sbuf_tensor("acc", [128, 2 * J], BF16))
            exps = E1(nc.sbuf_tensor("exps", [128, 2 * J], F32))
            sll = [k.slot() for _ in range(8)]
            t_lds = {}
            t_sdone = [None] * NJ
            t_exp = [None] * NJ
            for m in range(NJ):
                q4 = (m % 8) * J
                h2 = (m % 2) * J
                for g in range(4):
                    dep = [t_sdone[m - 8]] if m >= 8 else []
                    eng = "sync" if g % 2 == 0 else "gpsimd"
                    t_lds[(m, g)] = k.dma(eng, lambda s, m=m, g=g, q4=q4:
                        s.dma_start(out=sbuf4[g][:, q4:q4 + J],
                                    in_=simp4_i[g * J + m * 128:g * J + (m + 1) * 128, :]),
                        sll[g * 2 + m % 2], after=dep)
                dep_a = [t_lds[(m, 0)], t_lds[(m, 1)]]
                if m >= 2:
                    dep_a.append(t_exp[m - 2])
                ta = k.op("vector", lambda v, q4=q4, h2=h2: v.tensor_add(
                    acc[:, h2:h2 + J], sbuf4[0][:, q4:q4 + J],
                    sbuf4[1][:, q4:q4 + J]), dep_a)
                tb = k.op("vector", lambda v, q4=q4, h2=h2: v.tensor_add(
                    acc[:, h2:h2 + J], acc[:, h2:h2 + J], sbuf4[2][:, q4:q4 + J]),
                    [ta, t_lds[(m, 2)]])
                tc = k.op("vector", lambda v, q4=q4, h2=h2: v.tensor_add(
                    acc[:, h2:h2 + J], acc[:, h2:h2 + J], sbuf4[3][:, q4:q4 + J]),
                    [tb, t_lds[(m, 3)]])
                c0 = (m % 2) * 4
                dep_e = [tc]
                if m >= 2:
                    dep_e.append(t_sdone[m - 2])
                tex = k.op("scalar", lambda s, h2=h2, c0=c0: s.activation(
                    exps[:, h2:h2 + J], acc[:, h2:h2 + J], AF.Exp,
                    accum_out=cols[:, c0 + 1:c0 + 2]), dep_e)
                t_exp[m] = tex
                trc = k.op("vector", lambda v, c0=c0: v.reciprocal(
                    cols[:, c0 + 2:c0 + 3], cols[:, c0 + 1:c0 + 2]), [tex])
                t_sdone[m] = k.op("scalar", lambda s, m=m, h2=h2, c0=c0:
                    s.activation(
                        exps[:, h2:h2 + J], exps[:, h2:h2 + J], AF.Copy,
                        scale=cols[:, c0 + 2:c0 + 3],
                        accum_out=scols[:, m:m + 1]), [trc])
            k.flush(block)
            bar1 = k.bar([tuple(s) for s in sll])

        t_n2 = [None] * NT
        with ExitStack() as s2:
            E2 = s2.enter_context
            vgs = E2(nc.sbuf_tensor("vgs", [128, NJ * TOK], BF16))
            woutb = [E2(nc.sbuf_tensor(f"woutb{h}", [128, NJ * 512], BF16))
                     for h in range(2)]
            xaccb = E2(nc.sbuf_tensor("xaccb", [128, 2 * D], F32))
            xmid = E2(nc.sbuf_tensor("xmid", [128, 2 * D], F32))
            sqq = E2(nc.sbuf_tensor("sqq", [128, 2 * D], F32))
            sq2 = E2(nc.sbuf_tensor("sq2", [128, D], F32))

            slvg = [k.slot(), k.slot()]
            slwo = [k.slot(), k.slot()]
            slxa = [k.slot(), k.slot()]
            slxm = [k.slot(), k.slot()]
            sln2 = [k.slot(), k.slot()]
            t_wo = [None, None]
            for h in range(2):
                t_wo[h] = k.dma("gpsimd", lambda g, h=h: g.dma_start(
                    out=woutb[h][:, :].rearrange("p (k d) -> p k d", k=NJ),
                    in_=wout_i[:, h * 512:(h + 1) * 512].rearrange(
                        "(k p) d -> p k d", p=128)), slwo[h],
                    after=(bar1 if h == 0 else []))
            boutb = E2(nc.sbuf_tensor("boutb", [1, D], BF16))
            ones1b = E2(nc.sbuf_tensor("ones1b", [1, 128], BF16))
            t_bo = k.dma("gpsimd", lambda g: g.dma_start(out=boutb[:, :],
                                                         in_=bout_i), k.slot(),
                         after=bar1)
            t_o1b = k.op("vector", lambda v: v.memset(ones1b[:, :], 1.0))
            t_vgs = [None] * NJ
            for m in range(NJ):
                dep = list(bar1) if m < 2 else [t_vgs[m - 2]]
                t_l = k.dma("sync", lambda s, m=m: s.dma_start(
                    out=vgs[:, m * TOK:(m + 1) * TOK],
                    in_=vg_i[m * 128:(m + 1) * 128, :]), slvg[m % 2], after=dep)
                t_vgs[m] = k.op("vector", lambda v, m=m: v.tensor_scalar(
                    vgs[:, m * TOK:(m + 1) * TOK], vgs[:, m * TOK:(m + 1) * TOK],
                    scols[:, m:m + 1], None, OP.mult), [t_l])
            t_evx = [None] * (NT * 2)
            for t in range(NT):
                half = (t % 2) * D
                t_xl = k.dma("gpsimd", lambda s, t=t, half=half: s.dma_start(
                    out=xaccb[:, half:half + D],
                    in_=xacc_i[t * 128:(t + 1) * 128, :]), slxa[t % 2],
                    after=([t_n2[t - 2]] if t >= 2 else []))
                for dp in range(2):
                    b = t * 2 + dp
                    ps = psA if b % 2 == 0 else psB
                    dep = [t_wo[dp], t_vgs[7], t_bo, t_o1b]
                    if b >= 2:
                        dep.append(t_evx[b - 2])
                    if t >= 2 and dp == 0:
                        dep.append((slxm[t % 2][0], slxm[t % 2][1]))

                    def mmA(tt, t=t, ps=ps, dp=dp):
                        for kk in range(8):
                            ins = tt.matmul(
                                ps[:, :],
                                vgs[:, kk * TOK + t * 128:kk * TOK + (t + 1) * 128],
                                woutb[dp][:, kk * 512:(kk + 1) * 512],
                                start=(kk == 0), stop=False)
                        return ins

                    def mmB(tt, t=t, ps=ps, dp=dp):
                        for kk in range(8, NJ):
                            tt.matmul(
                                ps[:, :],
                                vgs[:, kk * TOK + t * 128:kk * TOK + (t + 1) * 128],
                                woutb[dp][:, kk * 512:(kk + 1) * 512],
                                start=False, stop=False)
                        return tt.matmul(
                            ps[:, :], ones1b[:, :],
                            boutb[0:1, dp * 512:(dp + 1) * 512],
                            start=False, stop=True)
                    k.op("tensor", mmA, dep)
                    t_m = k.op("tensor", mmB, [t_vgs[NJ - 1]])
                    t_evx[b] = k.op("vector", lambda v, half=half, dp=dp, ps=ps:
                        v.tensor_add(xmid[:, half + dp * 512:half + (dp + 1) * 512],
                                     ps[:, :],
                                     xaccb[:, half + dp * 512:half + (dp + 1) * 512]),
                        [t_m, t_xl])
                t_stx = k.dma("scalar", lambda s, t=t, half=half: s.dma_start(
                    out=xmid_o[t * 128:(t + 1) * 128, :], in_=xmid[:, half:half + D]),
                    slxm[t % 2], after=[t_evx[t * 2], t_evx[t * 2 + 1]])
                store_toks.append(t_stx)
                c4 = 4 + (t % 2) * 6
                t1 = k.op("vector", lambda v, half=half, c4=c4: v.reduce_sum(
                    cols[:, c4:c4 + 1], xmid[:, half:half + D], axis=AX),
                    [t_evx[t * 2], t_evx[t * 2 + 1]])
                t2 = k.op("vector", lambda v, c4=c4: v.tensor_scalar(
                    cols[:, c4 + 1:c4 + 2], cols[:, c4:c4 + 1], 1.0 / D, None,
                    OP.mult), [t1])
                t3 = k.op("vector", lambda v, half=half, c4=c4: v.tensor_scalar(
                    sqq[:, half:half + D], xmid[:, half:half + D],
                    cols[:, c4 + 1:c4 + 2], None, OP.subtract), [t2])
                t4 = k.op("scalar", lambda s, half=half, c4=c4: s.activation(
                    sq2[:, :], sqq[:, half:half + D], AF.Square,
                    accum_out=cols[:, c4 + 2:c4 + 3]), [t3])
                t5 = k.op("vector", lambda v, c4=c4: v.tensor_scalar(
                    cols[:, c4 + 3:c4 + 4], cols[:, c4 + 2:c4 + 3], 1.0 / D, EPS,
                    OP.mult, OP.add), [t4])
                t6 = k.op("scalar", lambda s, c4=c4: s.sqrt(
                    cols[:, c4 + 4:c4 + 5], cols[:, c4 + 3:c4 + 4]), [t5])
                t7 = k.op("vector", lambda v, c4=c4: v.reciprocal(
                    cols[:, c4 + 5:c4 + 6], cols[:, c4 + 4:c4 + 5]), [t6])
                t9 = k.op("vector", lambda v, t=t, half=half, c4=c4: v.tensor_scalar(
                    n2tm[:, t * D:(t + 1) * D], sqq[:, half:half + D],
                    cols[:, c4 + 5:c4 + 6], None, OP.mult), [t7])
                t_n2[t] = t9
                t_st2 = k.dma("scalar", lambda s, t=t: s.dma_start(
                    out=n2_o[t * 128:(t + 1) * 128, :],
                    in_=n2tm[:, t * D:(t + 1) * D]), sln2[t % 2], after=[t9])
                store_toks.append(t_st2)
            k.flush(block)
            bar2 = k.bar()

        with ExitStack() as s3:
            E3 = s3.enter_context
            n2fm = E3(nc.sbuf_tensor("n2fm", [128, ND * TOK], BF16))
            pstA = E3(nc.psum_tensor("pstA", [128, 128], BF16))
            pstB = E3(nc.psum_tensor("pstB", [128, 128], BF16))
            psTA = E3(nc.psum_tensor("psTA", [128, 128], F32))
            psTB = E3(nc.psum_tensor("psTB", [128, 128], F32))
            psC0 = E3(nc.psum_tensor("psC0", [8, 1], F32))
            psC1 = E3(nc.psum_tensor("psC1", [8, 1], F32))

            t_trc = [None] * (NT * ND)
            for i in range(NT):
                for jj in range(ND):
                    b = i * ND + jj
                    pst = pstA if b % 2 == 0 else pstB
                    dep = [t_n2[i]]
                    if b == 0:
                        dep.append(tid)
                    if b >= 2:
                        dep.append(t_trc[b - 2])
                    t_tr = k.op("tensor", lambda t, i=i, jj=jj, pst=pst: t.transpose(
                        pst[:, :], n2tm[:, i * D + jj * 128:i * D + (jj + 1) * 128],
                        identb[:, :]), dep)
                    t_trc[b] = k.op("vector", lambda v, i=i, jj=jj, pst=pst:
                        v.tensor_copy(
                            n2fm[:, jj * TOK + i * 128:jj * TOK + (i + 1) * 128],
                            pst[:, :]), [t_tr])
            k.flush(block)

            wgb = E3(nc.sbuf_tensor("wgb", [128, ND * 8], BF16))
            bgr = E3(nc.sbuf_tensor("bgr", [128, 8], F32))
            tokb = E3(nc.sbuf_tensor("tokb", [128, 1], F32))
            lg = E3(nc.sbuf_tensor("lg", [128, NT * 8], F32))
            iota8 = E3(nc.sbuf_tensor("iota8", [128, 8], F32))
            iotac = E3(nc.sbuf_tensor("iotac", [128, 1], F32))
            fld = E3(nc.sbuf_tensor("fld", [128, 6 * NT], F32))
            s8a = E3(nc.sbuf_tensor("s8a", [128, 8], F32))
            s8b = E3(nc.sbuf_tensor("s8b", [128, 8], F32))
            s8c = E3(nc.sbuf_tensor("s8c", [128, 8], F32))
            s8d = E3(nc.sbuf_tensor("s8d", [128, 8], F32))
            m0 = E3(nc.sbuf_tensor("m0", [128, NT * 8], F32))
            m1t = E3(nc.sbuf_tensor("m1t", [128, NT * 8], F32))
            m0f = E3(nc.sbuf_tensor("m0f", [8, J], F32))
            m1f = E3(nc.sbuf_tensor("m1f", [8, J], F32))
            ex0 = E3(nc.sbuf_tensor("ex0", [8, J], F32))
            ex1 = E3(nc.sbuf_tensor("ex1", [8, J], F32))
            ones8 = E3(nc.sbuf_tensor("ones8", [8, J], F32))
            onescol = E3(nc.sbuf_tensor("onescol", [128, 1], F32))
            ctile = E3(nc.sbuf_tensor("ctile", [8, 2], F32))
            zrow = E3(nc.sbuf_tensor("zrow", [1, J], F32))
            ivals = E3(nc.sbuf_tensor("ivals", [128, 2], F32))
            itgtf = E3(nc.sbuf_tensor("itgtf", [128, 4], F32))
            itgt = E3(nc.sbuf_tensor("itgt", [128, 4], I32))

            t_wg = k.dma("gpsimd", lambda g: g.dma_start(
                out=wgb[:, :].rearrange("p (k e) -> p k e", k=ND),
                in_=wg_i.rearrange("(k p) e -> p k e", p=128)), k.slot(), after=bar2)
            t_bg = k.dma("gpsimd", lambda g: g.dma_start(out=bgr[:, :], in_=bg_i),
                         k.slot(), after=bar2)
            t_tb = k.dma("gpsimd", lambda g: g.dma_start(out=tokb[:, :], in_=tokb_i),
                         k.slot(), after=bar2)
            slwc = k.slot()
            for kk in range(8):
                k.dma("gpsimd", lambda g, kk=kk: g.dma_start(
                    out=w1c_o[kk * 128:(kk + 1) * 128, :],
                    in_=w1e_i[kk * 128:(kk + 1) * 128, :]), slwc)
                k.dma("gpsimd", lambda g, kk=kk: g.dma_start(
                    out=w2c_o[kk * 512:(kk + 1) * 512, :],
                    in_=w2e_i[kk * 512:(kk + 1) * 512, :]), slwc)
            store_toks.append((slwc[0], 16 * 16))
            t_io = k.op("gpsimd", lambda g: g.iota(
                iota8[:, :], pattern=[[1, 8]], base=0, channel_multiplier=0,
                allow_small_or_imprecise_dtypes=True))
            t_ic = k.op("gpsimd", lambda g: g.iota(
                iotac[:, :], pattern=[[0, 1]], base=1, channel_multiplier=1,
                allow_small_or_imprecise_dtypes=True))
            t_on = k.op("vector", lambda v: v.memset(onescol[:, :], 1.0))
            t_o8 = k.op("vector", lambda v: v.memset(ones8[:, :], 1.0))
            t_zr = k.op("vector", lambda v: v.memset(zrow[:, :], 0.0))
            slz = k.slot()
            for r in range(16):
                k.dma("sync", lambda s, r=r: s.dma_start(
                    out=T_o[r:r + 1, :], in_=zrow[:, :]), slz,
                    after=[t_zr] if r == 0 else [])
            t_zall = (slz[0], slz[1])

            t_lg = [None] * NT
            for t in range(NT):
                ps = psA if t % 2 == 0 else psB
                dep = [t_wg, t_trc[NT * ND - 1]]
                if t >= 2:
                    dep.append(t_lg[t - 2])

                def mmg(tt, t=t, ps=ps):
                    for kk in range(ND):
                        ins = tt.matmul(
                            ps[:, 0:8],
                            n2fm[:, kk * TOK + t * 128:kk * TOK + (t + 1) * 128],
                            wgb[:, kk * 8:(kk + 1) * 8],
                            start=(kk == 0), stop=(kk == ND - 1))
                    return ins
                t_m = k.op("tensor", mmg, dep)
                t_lg[t] = k.op("vector", lambda v, t=t, ps=ps: v.tensor_add(
                    lg[:, t * 8:(t + 1) * 8], ps[:, 0:8], bgr[:, :]),
                    [t_m, t_bg])

            t_m0w = [None] * NT
            t_m1w = [None] * NT
            t_gts = [None] * NT
            for t in range(NT):
                lgt = lg[:, t * 8:(t + 1) * 8]
                a1 = k.op("vector", lambda v, lgt=lgt: v.reduce_max(
                    cols[:, 0:1], lgt, axis=AX), [t_lg[t], t_io, t_on])
                a2 = k.op("vector", lambda v, lgt=lgt: v.tensor_scalar(
                    s8a[:, :], lgt, cols[:, 0:1], None, OP.is_equal), [a1])
                a3 = k.op("vector", lambda v: v.tensor_mul(s8b[:, :], s8a[:, :],
                                                           iota8[:, :]), [a2])
                a4 = k.op("vector", lambda v: v.tensor_scalar(
                    s8c[:, :], s8a[:, :], 9.0, None, OP.mult), [a3])
                a5 = k.op("vector", lambda v: v.tensor_sub(s8b[:, :], s8b[:, :],
                                                           s8c[:, :]), [a4])
                a6 = k.op("vector", lambda v: v.tensor_scalar(
                    s8b[:, :], s8b[:, :], 9.0, None, OP.add), [a5])
                a7 = k.op("vector", lambda v, t=t: v.tensor_reduce(
                    fld[:, 0 * NT + t:0 * NT + t + 1], s8b[:, :], AX, OP.min), [a6])
                a8 = k.op("vector", lambda v, t=t: v.tensor_scalar(
                    s8c[:, :], iota8[:, :], fld[:, 0 * NT + t:0 * NT + t + 1],
                    None, OP.is_equal), [a7])
                a9 = k.op("vector", lambda v: v.tensor_scalar(
                    s8c[:, :], s8c[:, :], 1e9, None, OP.mult), [a8])
                a10 = k.op("vector", lambda v, lgt=lgt: v.tensor_sub(
                    s8d[:, :], lgt, s8c[:, :]), [a9])
                a11 = k.op("vector", lambda v: v.reduce_max(
                    cols[:, 1:2], s8d[:, :], axis=AX), [a10])
                a12 = k.op("vector", lambda v: v.tensor_scalar(
                    s8c[:, :], s8d[:, :], cols[:, 1:2], None, OP.is_equal), [a11])
                a13 = k.op("vector", lambda v: v.tensor_mul(s8d[:, :], s8c[:, :],
                                                            iota8[:, :]), [a12])
                a14 = k.op("vector", lambda v: v.tensor_scalar(
                    s8c[:, :], s8c[:, :], 9.0, None, OP.mult), [a13])
                a15 = k.op("vector", lambda v: v.tensor_sub(s8d[:, :], s8d[:, :],
                                                            s8c[:, :]), [a14])
                a16 = k.op("vector", lambda v: v.tensor_scalar(
                    s8d[:, :], s8d[:, :], 9.0, None, OP.add), [a15])
                a17 = k.op("vector", lambda v, t=t: v.tensor_reduce(
                    fld[:, 1 * NT + t:1 * NT + t + 1], s8d[:, :], AX, OP.min), [a16])
                a18 = k.op("vector", lambda v: v.tensor_sub(
                    cols[:, 2:3], cols[:, 0:1], cols[:, 1:2]), [a17, a11])
                a19 = k.op("scalar", lambda s, t=t: s.activation(
                    fld[:, 4 * NT + t:4 * NT + t + 1], cols[:, 2:3], AF.Sigmoid),
                    [a18])
                a20 = k.op("vector", lambda v, t=t: v.tensor_scalar(
                    fld[:, 5 * NT + t:5 * NT + t + 1],
                    fld[:, 4 * NT + t:4 * NT + t + 1], -1.0, 1.0,
                    OP.mult, OP.add), [a19])
                t_gts[t] = a20
                t_m0w[t] = k.op("vector", lambda v, t=t: v.tensor_scalar(
                    m0[:, t * 8:(t + 1) * 8], iota8[:, :],
                    fld[:, 0 * NT + t:0 * NT + t + 1], None, OP.is_equal), [a7])
                t_m1w[t] = k.op("vector", lambda v, t=t: v.tensor_scalar(
                    m1t[:, t * 8:(t + 1) * 8], iota8[:, :],
                    fld[:, 1 * NT + t:1 * NT + t + 1], None, OP.is_equal), [a17])
            k.flush(block)

            t_cnt0 = None
            t_cnt1 = None

            def mmc0(tt):
                for t in range(NT):
                    ins = tt.matmul(psC0[:, :], m0[:, t * 8:(t + 1) * 8],
                                    onescol[:, :], start=(t == 0),
                                    stop=(t == NT - 1))
                return ins

            def mmc1(tt):
                for t in range(NT):
                    ins = tt.matmul(psC1[:, :], m1t[:, t * 8:(t + 1) * 8],
                                    onescol[:, :], start=(t == 0),
                                    stop=(t == NT - 1))
                return ins
            t_cnt0 = k.op("tensor", mmc0, t_m0w + [t_on])
            t_cnt1 = k.op("tensor", mmc1, t_m1w)
            tc0 = k.op("vector", lambda v: v.tensor_copy(ctile[:, 0:1], psC0[:, :]),
                       [t_cnt0])
            tc1 = k.op("vector", lambda v: v.tensor_copy(ctile[:, 1:2], psC1[:, :]),
                       [t_cnt1])
            t_stc = k.dma("sync", lambda s: s.dma_start(out=counts_o, in_=ctile[:, :]),
                          k.slot(), after=[tc0, tc1])
            store_toks.append(t_stc)

            t_mf = [None] * (2 * NT)
            for t in range(NT):
                for kk, (src, dst) in enumerate(((m0, m0f), (m1t, m1f))):
                    b = 2 * t + kk
                    pst = psTA if b % 2 == 0 else psTB
                    dep = [t_m0w[t] if kk == 0 else t_m1w[t], tidf]
                    if b >= 2:
                        dep.append(t_mf[b - 2])
                    t_tr = k.op("tensor", lambda tt, t=t, src=src, pst=pst:
                        tt.transpose(pst[0:8, 0:128], src[:, t * 8:(t + 1) * 8],
                                     identf[:, :]), dep)
                    t_mf[b] = k.op("vector", lambda v, t=t, dst=dst, pst=pst:
                        v.tensor_copy(dst[:, t * 128:(t + 1) * 128],
                                      pst[0:8, 0:128]), [t_tr])
            t_sc0 = k.op("vector", lambda v: v.tensor_tensor_scan(
                ex0[:, :], ones8[:, :], m0f[:, :], 0.0, OP.mult, OP.add),
                [t_mf[2 * NT - 2], t_mf[2 * NT - 1], t_o8])
            t_sc0b = k.op("vector", lambda v: v.tensor_sub(ex0[:, :], ex0[:, :],
                                                           m0f[:, :]), [t_sc0])
            t_sc1 = k.op("vector", lambda v: v.tensor_tensor_scan(
                ex1[:, :], ones8[:, :], m1f[:, :], 0.0, OP.mult, OP.add), [t_sc0b])
            t_sc1b = k.op("vector", lambda v: v.tensor_sub(ex1[:, :], ex1[:, :],
                                                           m1f[:, :]), [t_sc1])
            k.flush(block)

            slsc = [k.slot(), k.slot()]
            t_lpd = [None] * (2 * NT)
            for t in range(NT):
                for kk in range(2):
                    b = 2 * t + kk
                    exf = ex0 if kk == 0 else ex1
                    msk = m0 if kk == 0 else m1t
                    pst = psTA if b % 2 == 0 else psTB
                    dep = [t_sc0b if kk == 0 else t_sc1b]
                    if b >= 2:
                        dep.append(t_lpd[b - 2])
                    t_tr = k.op("tensor", lambda tt, t=t, exf=exf, pst=pst:
                        tt.transpose(pst[0:128, 0:8], exf[:, t * 128:(t + 1) * 128],
                                     identf[0:8, 0:8]), dep)
                    t_cp = k.op("vector", lambda v, pst=pst: v.tensor_copy(
                        s8a[:, :], pst[0:128, 0:8]), [t_tr])
                    t_mu = k.op("vector", lambda v, t=t, msk=msk: v.tensor_mul(
                        s8b[:, :], msk[:, t * 8:(t + 1) * 8], s8a[:, :]), [t_cp])
                    t_lp = k.op("vector", lambda v, t=t, kk=kk: v.reduce_sum(
                        fld[:, (2 + kk) * NT + t:(2 + kk) * NT + t + 1],
                        s8b[:, :], axis=AX), [t_mu])
                    t_lpd[b] = t_lp
                    c0 = 2 * (t % 2) + kk if False else (t % 2)
                    tv = k.op("vector", lambda v, t=t, c0=c0: v.scalar_tensor_tensor(
                        ivals[:, c0:c0 + 1], iotac[:, :], float(t * 128),
                        tokb[:, :], OP.add, OP.add),
                        [t_lp, t_ic, t_tb,
                         (slsc[t % 2][0], slsc[t % 2][1])])
                    cc = 2 * (t % 2) + kk
                    tg1 = k.op("vector", lambda v, t=t, kk=kk, cc=cc:
                        v.scalar_tensor_tensor(
                            itgtf[:, cc:cc + 1], fld[:, kk * NT + t:kk * NT + t + 1],
                            2048.0, fld[:, (2 + kk) * NT + t:(2 + kk) * NT + t + 1],
                            OP.mult, OP.add), [t_lp])
                    tg2 = k.op("vector", lambda v, kk=kk, cc=cc: v.tensor_scalar(
                        itgtf[:, cc:cc + 1], itgtf[:, cc:cc + 1],
                        float(kk * 16384), None, OP.add), [tg1])
                    tg3 = k.op("vector", lambda v, cc=cc: v.tensor_copy(
                        itgt[:, cc:cc + 1], itgtf[:, cc:cc + 1]), [tg2])
                    t_sct = k.dma("gpsimd", lambda g, c0=c0, cc=cc:
                        g.indirect_dma_start(
                            T_flat, IndirectOffsetOnAxis(itgt[:, cc:cc + 1], 0),
                            ivals[:, c0:c0 + 1], None),
                        slsc[t % 2], after=[tv, tg3, t_zall])
                    store_toks.append(t_sct)
            slr = k.slot()
            for f in range(6):
                t_r = k.dma("sync", lambda s, f=f: s.dma_start(
                    out=route_o[:, f * NT:(f + 1) * NT],
                    in_=fld[:, f * NT:(f + 1) * NT]), slr,
                    after=[t_lpd[2 * NT - 1], t_gts[NT - 1]])
                store_toks.append(t_r)
            k.flush(block)
        k.final_wait(block, "sync", store_toks)
    return nc


def l2_inputs(simp4, vg, xacc, inp, core):
    rep = lambda a: np.ascontiguousarray(
        np.broadcast_to(np.asarray(a, np.float32)[None, :], (128, a.shape[-1])))
    return {
        "simp4": np.ascontiguousarray(simp4.reshape(4 * J, J), BF),
        "vg_i": np.ascontiguousarray(vg, BF),
        "xacc_i": np.ascontiguousarray(xacc, np.float32),
        "Wout": np.ascontiguousarray(inp["L0_attn_W_out"], np.float32),
        "bout_r": np.ascontiguousarray(
            np.asarray(inp["L0_attn_b_out"], np.float32)[None, :]),
        "Wg": np.ascontiguousarray(inp["L0_ffn_Wg"], np.float32),
        "bg_r": rep(inp["L0_ffn_bg"]),
        "ident": np.eye(128, dtype=np.float32),
        "tokbase_r": np.full((128, 1), core * TOK, np.float32),
        "W1e": np.ascontiguousarray(np.asarray(inp["L0_ffn_W1"], np.float32)[core]),
        "W2e": np.ascontiguousarray(np.asarray(inp["L0_ffn_W2"], np.float32)[core]),
    }


NPAN = 10


def build_l3(debug=False):
    nc = _bass()
    n2g_i = nc.dram_tensor("n2glob", [NCORES * TOK + 128, D], BF16,
                           kind="ExternalInput").ap()
    tmy_i = nc.dram_tensor("Tmy", [18, TOK], F32, kind="ExternalInput").ap()
    tmy_flat = tmy_i.rearrange("a b -> (a b)")[:, None]
    cnt_i = nc.dram_tensor("cntrow", [1, 16], F32, kind="ExternalInput").ap()
    w1_i = nc.dram_tensor("W1", [D, F4], BF16, kind="ExternalInput").ap()
    b1_i = nc.dram_tensor("b1_c", [128, F4 // 128], F32, kind="ExternalInput").ap()
    w2_i = nc.dram_tensor("W2", [F4, D], BF16, kind="ExternalInput").ap()
    b2_i = nc.dram_tensor("b2_r", [128, D], F32, kind="ExternalInput").ap()

    y_o = nc.dram_tensor("y_o", [CAPP, D], BF16, kind="ExternalOutput").ap()
    ident_i3 = nc.dram_tensor("ident", [128, 128], F32, kind="ExternalInput").ap()

    with ExitStack() as ctx:
        E = ctx.enter_context
        k = K(nc, ctx)
        block = E(nc.Block())
        w1b = E(nc.sbuf_tensor("w1b", [128, ND * F4], BF16))
        w2b = E(nc.sbuf_tensor("w2b", [128, 32 * D], BF16))
        hpan = E(nc.sbuf_tensor("hpan", [128, 32 * 512], BF16))
        buffm = E(nc.sbuf_tensor("buffm", [128, 2 * ND * 512], BF16))
        b1c = E(nc.sbuf_tensor("b1c", [128, F4 // 128], F32))
        b2r = E(nc.sbuf_tensor("b2r", [128, D], F32))
        ystage = E(nc.sbuf_tensor("ystage", [128, 2 * D], BF16))
        crow = E(nc.sbuf_tensor("crow", [1, 16], F32))
        ones16 = E(nc.sbuf_tensor("ones16", [1, 16], F32))
        ones1r = E(nc.sbuf_tensor("ones1r", [1, 128], F32))
        bases = E(nc.sbuf_tensor("bases", [1, 32], F32))
        basesrep = E(nc.sbuf_tensor("basesrep", [128, 32], F32))
        iotaS = E(nc.sbuf_tensor("iotaS", [128, 39], F32))
        acc39 = E(nc.sbuf_tensor("acc39", [128, 39], F32))
        m39 = E(nc.sbuf_tensor("m39", [128, 39], F32))
        buftm = E(nc.sbuf_tensor("buftm", [128, 4 * D], BF16))
        identb3 = E(nc.sbuf_tensor("identb3", [128, 128], BF16))
        tokf = E(nc.sbuf_tensor("tokf", [128, 39], F32))
        idx2i = E(nc.sbuf_tensor("idx2i", [128, 39], I32))
        idxi = E(nc.sbuf_tensor("idxi", [128, 39], I32))
        psmm = [E(nc.psum_tensor(f"ps{i}", [128, 512], F32)) for i in range(4)]
        psR = E(nc.psum_tensor("psR", [128, 32], F32))
        store_toks = []

        t_cl = k.dma("sync", lambda s: s.dma_start(out=crow[:, :], in_=cnt_i),
                     k.slot())
        t_b1 = k.dma("sync", lambda s: s.dma_start(out=b1c[:, :], in_=b1_i), k.slot())
        t_b2 = k.dma("sync", lambda s: s.dma_start(out=b2r[:, :], in_=b2_i), k.slot())
        t_id3 = k.dma("gpsimd", lambda g: g.dma_start(out=identb3[:, :],
                                                      in_=ident_i3), k.slot())
        slw1 = k.slot()
        slw2 = k.slot()
        for kk in range(ND):
            k.dma("sync", lambda s, kk=kk: s.dma_start(
                out=w1b[:, kk * F4:(kk + 1) * F4],
                in_=w1_i[kk * 128:(kk + 1) * 128, :]), slw1)
        for k2 in range(8):
            k.dma("scalar", lambda s, k2=k2: s.dma_start(
                out=w2b[:, :].rearrange("p (k dd) -> p k dd", k=32)[:, k2 * 4:(k2 + 1) * 4, :],
                in_=w2_i[k2 * 512:(k2 + 1) * 512, :].rearrange(
                    "(k p) dd -> p k dd", p=128)), slw2)
        t_w1_all = (slw1[0], 8 * 16)
        t_w2_all = (slw2[0], 8 * 16)

        t_o16 = k.op("vector", lambda v: v.memset(ones16[:, :], 1.0))
        t_o1r = k.op("vector", lambda v: v.memset(ones1r[:, :], 1.0))
        t_io = k.op("gpsimd", lambda g: g.iota(
            iotaS[:, :], pattern=[[128, 39]], base=0, channel_multiplier=1,
            allow_small_or_imprecise_dtypes=True))
        t_sc = k.op("vector", lambda v: v.tensor_tensor_scan(
            bases[:, 0:16], ones16[:, :], crow[:, :], 0.0, OP.mult, OP.add),
            [t_cl, t_o16])
        t_ex = k.op("vector", lambda v: v.tensor_sub(bases[:, 0:16], bases[:, 0:16],
                                                     crow[:, :]), [t_sc])
        t_jm0 = k.op("vector", lambda v: v.memset(bases[:, 16:17], 0.0))
        t_jm = k.op("vector", lambda v: v.tensor_scalar(
            bases[:, 17:32], crow[:, 0:15], -1.0, 2048.0, OP.mult, OP.add),
            [t_ex, t_jm0])
        t_rep = k.op("tensor", lambda t: t.matmul(
            psR[:, :], ones1r[:, :], bases[:, :], start=True, stop=True),
            [t_jm, t_o1r])
        t_repc = k.op("vector", lambda v: v.tensor_copy(basesrep[:, :], psR[:, :]),
                      [t_rep])
        t_acc = k.op("vector", lambda v: v.tensor_copy(acc39[:, :], iotaS[:, :]),
                     [t_io])
        prev = t_acc
        for r in range(1, 16):
            tm_ = k.op("vector", lambda v, r=r: v.tensor_scalar(
                m39[:, :], iotaS[:, :], basesrep[:, r:r + 1], None, OP.is_ge),
                [prev, t_repc])
            prev = k.op("vector", lambda v, r=r: v.scalar_tensor_tensor(
                acc39[:, :], m39[:, :], basesrep[:, 16 + r:17 + r], acc39[:, :],
                OP.mult, OP.add), [tm_])
        t_i2 = k.op("vector", lambda v: v.tensor_copy(idx2i[:, :], acc39[:, :]),
                    [prev])
        k.flush(block)

        sltk = [k.slot() for _ in range(4)]
        t_tk = [None] * 39
        t_fix = [None] * 39

        def emit_tokgrp(gidx):
            g0 = 4 * gidx
            g1 = min(g0 + 4, 39)
            for a in range(g0, g1):
                dep = [t_i2] if a < 4 else [t_tk[a - 4]]
                t_tk[a] = k.dma("gpsimd", lambda g, a=a: g.indirect_dma_start(
                    tokf[:, a:a + 1], None, tmy_flat,
                    IndirectOffsetOnAxis(idx2i[:, a:a + 1], 0)), sltk[a % 4],
                    after=dep)
            depf = [t_tk[a] for a in range(g0, g1)]
            f1 = k.op("vector", lambda v, g0=g0, g1=g1: v.tensor_scalar(
                m39[:, g0:g1], tokf[:, g0:g1], 0.0, None, OP.is_equal), depf)
            f2 = k.op("vector", lambda v, g0=g0, g1=g1: v.scalar_tensor_tensor(
                m39[:, g0:g1], m39[:, g0:g1], float(NCORES * TOK + 1),
                tokf[:, g0:g1], OP.mult, OP.add), [f1])
            f3 = k.op("vector", lambda v, g0=g0, g1=g1: v.tensor_scalar(
                m39[:, g0:g1], m39[:, g0:g1], -1.0, None, OP.add), [f2])
            f4 = k.op("vector", lambda v, g0=g0, g1=g1: v.tensor_copy(
                idxi[:, g0:g1], m39[:, g0:g1]), [f3])
            for a in range(g0, g1):
                t_fix[a] = f4
        emit_tokgrp(0)
        slga = [k.slot() for _ in range(4)]
        t_gablk = [None] * 39
        t_trcp = [None] * 39

        def emit_gather(a):
            par = a % 4
            dep_g = [t_fix[a]]
            if a >= 4:
                dep_g.append(t_trcp[a - 4])
            t_gablk[a] = k.dma("gpsimd", lambda g, a=a, par=par:
                g.indirect_dma_start(
                    buftm[:, par * D:(par + 1) * D], None,
                    n2g_i, IndirectOffsetOnAxis(idxi[:, a:a + 1], 0)),
                slga[par], after=dep_g)
        for a in range(4):
            emit_gather(a)
        k.flush(block)

        pstT = [E(nc.psum_tensor("pstT0", [128, 128], BF16)),
                E(nc.psum_tensor("pstT1", [128, 128], BF16))]
        trn_cnt = [0]
        t_trcv = [None, None]
        slys = [k.slot(), k.slot()]
        t_h = {}
        t_y = {}
        ev1_flat = []
        ev2_flat = []
        ev2_key = {}
        t_mm2_last = None
        b1cnt = 0
        b2cnt = 0
        for p in range(NPAN):
            nsl = 512 if p < NPAN - 1 else CAPP - 512 * (NPAN - 1)
            nsm = nsl // 128
            pb = (p % 2) * ND * 512
            blks = range(4 * p, min(4 * p + 4, 39))
            tr_toks = []
            for a in blks:
                par = a % 4
                for kk in range(ND):
                    bb = trn_cnt[0]
                    trn_cnt[0] += 1
                    pst = pstT[bb % 2]
                    dep_t = [t_gablk[a], t_id3]
                    if bb >= 2:
                        dep_t.append(t_trcv[bb % 2])
                    t_tr = k.op("tensor", lambda t, par=par, kk=kk, pst=pst:
                        t.transpose(pst[:, :],
                                    buftm[:, par * D + kk * 128:par * D + (kk + 1) * 128],
                                    identb3[:, :]), dep_t)
                    t_cv = k.op("vector", lambda v, p=p, a=a, kk=kk, pb=pb, pst=pst:
                        v.tensor_copy(
                            buffm[:, pb + kk * 512 + (a - 4 * p) * 128:
                                  pb + kk * 512 + (a - 4 * p) * 128 + 128],
                            pst[:, :]), [t_tr])
                    t_trcv[bb % 2] = t_cv
                    tr_toks.append(t_cv)
                t_trcp[a] = tr_toks[-1]
            t_trall = tr_toks[-1]
            if p + 1 < NPAN:
                emit_tokgrp(p + 1)
            for a2 in range(4 * (p + 1), min(4 * (p + 2), 39)):
                emit_gather(a2)
            for m in range(32):
                b = b1cnt
                b1cnt += 1
                ps = psmm[b % 2]
                dep = [t_trall, t_w1_all]
                if b >= 2:
                    dep.append(ev1_flat[b - 2])

                def mm1(t, m=m, ps=ps, pb=pb, nsl=nsl):
                    for kk in range(ND):
                        ins = t.matmul(
                            ps[:, 0:nsl],
                            w1b[:, kk * F4 + m * 128:kk * F4 + (m + 1) * 128],
                            buffm[:, pb + kk * 512:pb + kk * 512 + nsl],
                            start=(kk == 0), stop=(kk == ND - 1))
                    return ins
                t_m = k.op("tensor", mm1, dep)
                dep_gelu = [t_m, t_b1]
                if p > 0 and m == 0:
                    dep_gelu.append(t_mm2_last)
                t_e = k.op("scalar", lambda s, m=m, ps=ps, nsl=nsl: s.activation(
                    hpan[:, m * 512:m * 512 + nsl], ps[:, 0:nsl],
                    AF.Gelu_apprx_tanh, bias=b1c[:, m:m + 1]),
                    dep_gelu)
                t_h[(p, m)] = t_e
                ev1_flat.append(t_e)
            for sm in range(nsm):
                for dp in range(2):
                    b = b2cnt
                    b2cnt += 1
                    ps = psmm[2 + b % 2]
                    dep = [t_h[(p, 15)], t_w2_all]
                    if b >= 2:
                        dep.append(ev2_flat[b - 2])

                    def mm2a(t, sm=sm, dp=dp, ps=ps):
                        for kk in range(16):
                            ins = t.matmul(
                                ps[:, :],
                                hpan[:, kk * 512 + sm * 128:kk * 512 + (sm + 1) * 128],
                                w2b[:, kk * D + dp * 512:kk * D + (dp + 1) * 512],
                                start=(kk == 0), stop=False)
                        return ins

                    def mm2b(t, sm=sm, dp=dp, ps=ps):
                        for kk in range(16, 32):
                            ins = t.matmul(
                                ps[:, :],
                                hpan[:, kk * 512 + sm * 128:kk * 512 + (sm + 1) * 128],
                                w2b[:, kk * D + dp * 512:kk * D + (dp + 1) * 512],
                                start=False, stop=(kk == 31))
                        return ins
                    k.op("tensor", mm2a, dep)
                    t_m2 = k.op("tensor", mm2b, [t_h[(p, 31)]])
                    t_mm2_last = t_m2
                    yh = (sm % 2) * D
                    dep_e = [t_m2, t_b2]
                    if b >= 4:
                        dep_e.append(t_y[ev2_key[b - 4]])
                    t_e2 = k.op("vector", lambda v, yh=yh, dp=dp, ps=ps: v.tensor_add(
                        ystage[:, yh + dp * 512:yh + (dp + 1) * 512], ps[:, :],
                        b2r[:, dp * 512:(dp + 1) * 512]), dep_e)
                    ev2_flat.append(t_e2)
                    ev2_key[b] = (p, sm, dp)
                t_st = k.dma("scalar", lambda s, p=p, sm=sm, yh=yh:
                    s.dma_start(out=y_o[p * 512 + sm * 128:p * 512 + (sm + 1) * 128, :],
                                in_=ystage[:, yh:yh + D]), slys[sm % 2],
                    after=[ev2_flat[-1], ev2_flat[-2]])
                t_y[ev2_key[b2cnt - 1]] = t_st
                t_y[ev2_key[b2cnt - 2]] = t_st
                store_toks.append(t_st)
        k.flush(block)
        k.final_wait(block, "sync", store_toks)
    return nc


def l3_inputs(n2glob, T_all, counts_all, inp, core):
    e = core
    tmy = np.zeros((18, TOK), np.float32)
    crow = np.zeros((1, 16), np.float32)
    for kk in range(2):
        for cp in range(8):
            tmy[kk * 8 + cp] = T_all[cp][kk * 8 + e]
            crow[0, kk * 8 + cp] = counts_all[cp][e, kk]
    b1 = np.asarray(inp["L0_ffn_b1"], np.float32)[e]
    b2 = np.asarray(inp["L0_ffn_b2"], np.float32)[e]
    W1 = np.asarray(inp["w1b_pre"])
    W2 = np.asarray(inp["w2b_pre"])
    return {
        "n2glob": np.ascontiguousarray(
            np.concatenate([np.asarray(n2glob, BF),
                            np.zeros((128, D), BF)], 0)),
        "Tmy": tmy,
        "cntrow": crow,
        "W1": np.ascontiguousarray(W1),
        "b1_c": np.ascontiguousarray(b1.reshape(F4 // 128, 128).T),
        "W2": np.ascontiguousarray(W2),
        "b2_r": np.ascontiguousarray(np.broadcast_to(b2[None, :], (128, D))),
        "ident": np.eye(128, dtype=np.float32),
    }


def build_l4():
    nc = _bass()
    yg_i = nc.dram_tensor("yglob", [NCORES * CAPP + 128, D], BF16,
                          kind="ExternalInput").ap()
    xmid_i = nc.dram_tensor("xmid_i", [TOK, D], F32, kind="ExternalInput").ap()
    route_i = nc.dram_tensor("route_i", [128, 96], F32, kind="ExternalInput").ap()
    cnt0_i = nc.dram_tensor("cnt0", [8, 8], F32, kind="ExternalInput").ap()
    cnt1_i = nc.dram_tensor("cnt1", [8, 8], F32, kind="ExternalInput").ap()
    mlt_i = nc.dram_tensor("masklt", [8, 2], F32, kind="ExternalInput").ap()
    ident_i = nc.dram_tensor("ident", [128, 128], F32, kind="ExternalInput").ap()

    out_o = nc.dram_tensor("out_o", [TOK, D], F32, kind="ExternalOutput").ap()

    with ExitStack() as ctx:
        E = ctx.enter_context
        k = K(nc, ctx)
        block = E(nc.Block())
        identf = E(nc.sbuf_tensor("identf", [128, 128], F32))
        cnt0b = E(nc.sbuf_tensor("cnt0b", [8, 8], F32))
        cnt1b = E(nc.sbuf_tensor("cnt1b", [8, 8], F32))
        mltb = E(nc.sbuf_tensor("mltb", [8, 2], F32))
        fld = E(nc.sbuf_tensor("fld", [128, 96], F32))
        iota8 = E(nc.sbuf_tensor("iota8", [128, 8], F32))
        onesr = E(nc.sbuf_tensor("onesr", [1, 128], F32))
        pre8 = E(nc.sbuf_tensor("pre8", [8, 2], F32))
        pre8b = E(nc.sbuf_tensor("pre8b", [8, 2], F32))
        rows28 = E(nc.sbuf_tensor("rows28", [1, 16], F32))
        rows28b = E(nc.sbuf_tensor("rows28b", [1, 16], F32))
        b1row = E(nc.sbuf_tensor("b1row", [1, 8], F32))
        brep = E(nc.sbuf_tensor("brep", [128, 16], F32))
        m8 = E(nc.sbuf_tensor("m8", [128, 8], F32))
        s8 = E(nc.sbuf_tensor("s8", [128, 8], F32))
        colsq = E(nc.sbuf_tensor("colsq", [128, 16], F32))
        idxg = E(nc.sbuf_tensor("idxg", [128, 8], I32))
        idxgf = E(nc.sbuf_tensor("idxgf", [128, 8], F32))
        At = E(nc.sbuf_tensor("At", [128, 4 * D], BF16))
        Bt = E(nc.sbuf_tensor("Bt", [128, 4 * D], BF16))
        xmb = E(nc.sbuf_tensor("xmb", [128, 4 * D], F32))
        outs = E(nc.sbuf_tensor("outs", [128, 4 * D], F32))
        psA = E(nc.psum_tensor("psA", [128, 512], F32))
        psB = E(nc.psum_tensor("psB", [128, 512], F32))
        store_toks = []

        slc = k.slot()
        k.dma("sync", lambda s: s.dma_start(out=cnt0b[:, :], in_=cnt0_i), slc)
        k.dma("sync", lambda s: s.dma_start(out=cnt1b[:, :], in_=cnt1_i), slc)
        k.dma("sync", lambda s: s.dma_start(out=mltb[:, :], in_=mlt_i), slc)
        k.dma("sync", lambda s: s.dma_start(out=identf[:, :], in_=ident_i), slc)
        t_call = k.dma("sync", lambda s: s.dma_start(out=fld[:, :], in_=route_i), slc)
        t_c0 = t_c1 = t_ml = t_idf = t_fl = t_call
        t_io = k.op("gpsimd", lambda g: g.iota(
            iota8[:, :], pattern=[[1, 8]], base=0, channel_multiplier=0,
            allow_small_or_imprecise_dtypes=True))
        t_on = k.op("vector", lambda v: v.memset(onesr[:, :], 1.0))
        t_mA = k.op("vector", lambda v: v.memset(At[:, :], 0))
        t_mB = k.op("vector", lambda v: v.memset(Bt[:, :], 0))

        def mmb0(t):
            return t.matmul(psA[0:8, 0:2], cnt0b[:, :], mltb[:, :],
                            start=True, stop=True)

        def mmb1(t):
            return t.matmul(psB[0:8, 0:2], cnt1b[:, :], mltb[:, :],
                            start=True, stop=True)
        t_b0 = k.op("tensor", mmb0, [t_c0, t_ml])
        t_b1m = k.op("tensor", mmb1, [t_c1, t_ml])
        t_p0 = k.op("vector", lambda v: v.tensor_copy(pre8[:, :], psA[0:8, 0:2]),
                    [t_b0])
        t_p1 = k.op("vector", lambda v: v.tensor_copy(pre8b[:, :], psB[0:8, 0:2]),
                    [t_b1m])
        t_t0 = k.op("tensor", lambda t: t.transpose(
            psA[0:1, 0:8], pre8[:, 0:1], identf[0:8, 0:8]), [t_p0, t_idf])
        t_t0c = k.op("vector", lambda v: v.tensor_copy(rows28[0:1, 0:8],
                                                       psA[0:1, 0:8]), [t_t0])
        t_t0b = k.op("tensor", lambda t: t.transpose(
            psB[0:1, 0:8], pre8[:, 1:2], identf[0:8, 0:8]), [t_p0, t_idf, t_p1])
        t_t0bc = k.op("vector", lambda v: v.tensor_copy(rows28[0:1, 8:16],
                                                        psB[0:1, 0:8]), [t_t0b])
        t_t1 = k.op("tensor", lambda t: t.transpose(
            psA[0:1, 8:16], pre8b[:, 0:1], identf[0:8, 0:8]),
            [t_p1, t_t0c, t_t0])
        t_t1c = k.op("vector", lambda v: v.tensor_copy(rows28b[0:1, 0:8],
                                                       psA[0:1, 8:16]), [t_t1])
        t_br = k.op("vector", lambda v: v.tensor_add(
            b1row[:, :], rows28[0:1, 8:16], rows28b[0:1, 0:8]),
            [t_t1c, t_t0bc])

        def mmr0(t):
            return t.matmul(psA[0:128, 0:8], onesr[:, :], rows28[0:1, 0:8],
                            start=True, stop=True)

        def mmr1(t):
            return t.matmul(psB[0:128, 0:8], onesr[:, :], b1row[:, :],
                            start=True, stop=True)
        t_r0 = k.op("tensor", mmr0, [t_br, t_on])
        t_r1 = k.op("tensor", mmr1, [t_r0])
        t_r0c = k.op("vector", lambda v: v.tensor_copy(brep[:, 0:8], psA[0:128, 0:8]),
                     [t_r0])
        t_r1c = k.op("vector", lambda v: v.tensor_copy(brep[:, 8:16], psB[0:128, 0:8]),
                     [t_r1])
        k.flush(block)

        slg = [k.slot() for _ in range(8)]
        slxm = [k.slot() for _ in range(4)]
        slout = [k.slot() for _ in range(4)]
        t_done = [None] * NT
        t_gath = {}
        for grp in range(0, NT, 4):
            for t in range(grp, grp + 4):
                half = (t % 4) * D
                t_xl = k.dma("sync", lambda s, t=t, half=half: s.dma_start(
                    out=xmb[:, half:half + D], in_=xmid_i[t * 128:(t + 1) * 128, :]),
                    slxm[t % 4], after=[t_done[t - 4]] if t >= 4 else [t_fl])
                t_gath[(t, "x")] = t_xl
                for kk in range(2):
                    cc = (t % 4) * 2 + kk
                    e_col = fld[:, kk * NT + t:kk * NT + t + 1]
                    lp_col = fld[:, (2 + kk) * NT + t:(2 + kk) * NT + t + 1]
                    g_col = fld[:, (4 + kk) * NT + t:(4 + kk) * NT + t + 1]
                    dep0 = [t_r0c, t_r1c] if t < 4 else [t_done[t - 4]]
                    a1 = k.op("vector", lambda v, e_col=e_col: v.tensor_scalar(
                        m8[:, :], iota8[:, :], e_col, None, OP.is_equal),
                        dep0 + [t_io])
                    a2 = k.op("vector", lambda v, kk=kk: v.tensor_mul(
                        s8[:, :], m8[:, :], brep[:, kk * 8:(kk + 1) * 8]), [a1])
                    a3 = k.op("vector", lambda v, cc=cc: v.reduce_sum(
                        colsq[:, cc:cc + 1], s8[:, :], axis=AX), [a2])
                    a4 = k.op("vector", lambda v, cc=cc, lp_col=lp_col: v.tensor_add(
                        colsq[:, cc:cc + 1], colsq[:, cc:cc + 1], lp_col), [a3])
                    a5 = k.op("vector", lambda v, cc=cc: v.tensor_scalar(
                        colsq[:, 8 + cc:9 + cc], colsq[:, cc:cc + 1], float(CAP),
                        None, OP.is_lt), [a4])
                    a6 = k.op("vector", lambda v, cc=cc, g_col=g_col: v.tensor_mul(
                        colsq[:, 8 + cc:9 + cc], colsq[:, 8 + cc:9 + cc], g_col), [a5])
                    a5b = k.op("vector", lambda v, cc=cc: v.tensor_scalar(
                        s8[:, 0:1], colsq[:, cc:cc + 1], float(CAP), None,
                        OP.is_lt), [a6])
                    a7 = k.op("vector", lambda v, cc=cc, e_col=e_col:
                        v.scalar_tensor_tensor(
                            idxgf[:, cc:cc + 1], e_col, float(CAPP),
                            colsq[:, cc:cc + 1], OP.mult, OP.add), [a5b])
                    SENT = float(NCORES * CAPP)
                    a8 = k.op("vector", lambda v: v.tensor_scalar(
                        s8[:, 1:2], s8[:, 0:1], -SENT, SENT, OP.mult, OP.add), [a7])
                    a8b = k.op("vector", lambda v, cc=cc: v.tensor_mul(
                        idxgf[:, cc:cc + 1], idxgf[:, cc:cc + 1], s8[:, 0:1]), [a8])
                    a9 = k.op("vector", lambda v, cc=cc: v.tensor_add(
                        idxgf[:, cc:cc + 1], idxgf[:, cc:cc + 1], s8[:, 1:2]), [a8b])
                    a10 = k.op("vector", lambda v, cc=cc: v.tensor_copy(
                        idxg[:, cc:cc + 1], idxgf[:, cc:cc + 1]), [a9])
                    tgt = At if kk == 0 else Bt
                    t_g = k.dma("gpsimd", lambda g, cc=cc, tgt=tgt, half=half:
                        g.indirect_dma_start(
                            tgt[:, half:half + D], None,
                            yg_i, IndirectOffsetOnAxis(idxg[:, cc:cc + 1], 0)),
                        slg[cc], after=[a10, t_mA, t_mB])
                    t_gath[(t, kk)] = t_g
            for t in range(grp, grp + 4):
                half = (t % 4) * D
                dep_c1 = [t_gath[(t, "x")], t_gath[(t, 0)]]
                if t >= 4:
                    dep_c1.append((slout[t % 4][0], slout[t % 4][1]))
                c1 = k.op("vector", lambda v, half=half, t=t: v.scalar_tensor_tensor(
                    outs[:, half:half + D], At[:, half:half + D],
                    colsq[:, 8 + (t % 4) * 2:9 + (t % 4) * 2],
                    xmb[:, half:half + D], OP.mult, OP.add),
                    dep_c1)
                c2 = k.op("vector", lambda v, half=half, t=t: v.scalar_tensor_tensor(
                    outs[:, half:half + D], Bt[:, half:half + D],
                    colsq[:, 8 + (t % 4) * 2 + 1:9 + (t % 4) * 2 + 1],
                    outs[:, half:half + D], OP.mult, OP.add),
                    [c1, t_gath[(t, 1)]])
                t_st = k.dma("scalar", lambda s, t=t, half=half: s.dma_start(
                    out=out_o[t * 128:(t + 1) * 128, :], in_=outs[:, half:half + D]),
                    slout[t % 4], after=[c2])
                t_done[t] = c2
                store_toks.append(t_st)
        k.flush(block)
        k.final_wait(block, "sync", store_toks)
    return nc


def l4_inputs(yglob, xmid, route, counts_all, core):
    cnt0 = np.stack([counts_all[c][:, 0] for c in range(8)])
    cnt1 = np.stack([counts_all[c][:, 1] for c in range(8)])
    mlt = np.zeros((8, 2), np.float32)
    mlt[:core, 0] = 1.0
    mlt[:, 1] = 1.0
    return {
        "yglob": np.ascontiguousarray(np.concatenate(
            [np.asarray(yglob, BF).reshape(NCORES * CAPP, D),
             np.zeros((128, D), BF)], 0)),
        "xmid_i": np.ascontiguousarray(xmid, np.float32),
        "route_i": np.ascontiguousarray(route, np.float32),
        "cnt0": np.ascontiguousarray(cnt0, np.float32),
        "cnt1": np.ascontiguousarray(cnt1, np.float32),
        "masklt": mlt,
        "ident": np.eye(128, dtype=np.float32),
    }


_NC_CACHE = {}


def _get_nc(name):
    if name not in _NC_CACHE:
        _NC_CACHE[name] = {"l1": build_l1, "l2": build_l2,
                           "l3": build_l3, "l4": build_l4}[name]()
    return _NC_CACHE[name]


def kernel(**inputs):
    inp = inputs
    x = np.asarray(inp["L0_attn_x"], np.float32).reshape(NCORES * TOK, D)
    cores = list(range(NCORES))

    nc1 = _get_nc("l1")
    maps1 = [l1_inputs(x[c * TOK:(c + 1) * TOK], inp) for c in cores]
    r1 = run_bass_kernel_spmd(nc1, maps1, core_ids=cores).results

    nc2 = _get_nc("l2")
    maps2 = []
    for c in cores:
        g = (c // 4) * 4
        simp4 = np.stack([r1[i]["simp_o"] for i in range(g, g + 4)])
        maps2.append(l2_inputs(simp4, r1[c]["vg_o"], r1[c]["xacc_o"], inp, c))
    r2 = run_bass_kernel_spmd(nc2, maps2, core_ids=cores).results

    nc3 = _get_nc("l3")
    n2glob = np.concatenate([r2[c]["n2_o"] for c in cores], 0)
    T_all = [r2[c]["T_o"] for c in cores]
    counts_all = [r2[c]["counts_o"] for c in cores]
    maps3 = []
    for c in cores:
        inp3 = dict(inp)
        inp3["w1b_pre"] = r2[c]["w1c_o"]
        inp3["w2b_pre"] = r2[c]["w2c_o"]
        maps3.append(l3_inputs(n2glob, T_all, counts_all, inp3, c))
    r3 = run_bass_kernel_spmd(nc3, maps3, core_ids=cores).results

    nc4 = _get_nc("l4")
    yglob = np.concatenate([r3[c]["y_o"] for c in cores], 0)
    maps4 = [l4_inputs(yglob, r2[c]["xmid_o"], r2[c]["route_o"],
                       counts_all, c) for c in cores]
    r4 = run_bass_kernel_spmd(nc4, maps4, core_ids=cores).results

    out = np.concatenate([r4[c]["out_o"] for c in cores], 0)
    return out.reshape(BB, NN, D)
